# version 02
# baseline (speedup 1.0000x reference)
import numpy as np
from contextlib import ExitStack
import concourse.bass as bass
import concourse.mybir as mybir

F32 = mybir.dt.float32
BF16 = mybir.dt.bfloat16
I32 = mybir.dt.int32
AF = mybir.ActivationFunctionType
ALU = mybir.AluOpType

def scan_consts_np():
    C = 64
    j = np.arange(2*C); t = np.arange(C)
    cst = {}
    cst["ident"] = np.eye(128, dtype=np.float32)
    cst["mask_ut"] = (j[:, None] < j[None, :]).astype(np.float32)
    cst["mask_gt"] = (t[:, None] < (j[None, :]//2)).astype(np.float32)
    md = (j[:, None]//64 == j[None, :]//64) & (j[:, None] < j[None, :])
    cst["mask_diag64"] = md.astype(np.float32)
    cst["mask_qu"] = ((j[:, None]//2) <= t[None, :]).astype(np.float32)
    cst["mask_qn"] = (t[:, None] <= t[None, :]).astype(np.float32)
    hs = np.zeros((2, 128), np.float32); hs[0, :64] = 1; hs[1, 64:] = 1
    cst["headsel2"] = hs
    cst["headsel2T"] = hs.T.copy()
    hsg = np.zeros((128, 256), np.float32)
    for p_ in range(2):
        hsg[64 + 2*p_, 128*p_:128*p_+64] = 1
        hsg[65 + 2*p_, 128*p_+64:128*p_+128] = 1
    cst["hsg"] = hsg
    cst["ones_col"] = np.ones((128, 1), np.float32)
    cst["zeros_col"] = np.zeros((128, 1), np.float32)
    cst["ones_row"] = np.ones((1, 512), np.float32)
    return cst


class Rec:
    def __init__(self):
        self.rec = []
        self.n = {k: 0 for k in ('t', 'v', 's', 'q', 'q2')}

    def run(self, eng, fn, after=()):
        self.rec.append((eng, fn, tuple(after)))
        self.n[eng] += 1
        return self.n[eng]

    def q2(self, fn, after=()):
        self.rec.append(('q2', fn, tuple(after)))
        self.n['q2'] += 1
        return self.n['q2']

    def mark(self, eng):
        return self.n[eng]

    def barrier(self, with_q=False):
        pt, ps = self.n['t'], self.n['s']
        extra = ((('q', self.n['q']),) if with_q else ())
        self.rec.append(('v', 'BARRIER', (('t', pt), ('s', ps)) + extra))
        self.n['v'] += 1
        nv = self.n['v']
        self.rec.append(('t', 'BARRIER', (('v', nv),)))
        self.n['t'] += 1
        self.rec.append(('s', 'BARRIER', (('v', nv),)))
        self.n['s'] += 1
        self.rec.append(('q', 'QWAIT', (('v', nv),)))

    def dma_reg(self, out_ap, dten, k, dims, after=()):
        self.rec.append(('q', ('DMAREG', out_ap, dten, k, dims), tuple(after)))
        self.n['q'] += 1
        return self.n['q']

    def dma(self, out, in_, after=(), slow=False):
        self.rec.append(('q', lambda h, out=out, in_=in_, slow=slow: h.dma_start(
            out, in_, allow_slow_non_contiguous=slow), tuple(after)))
        self.n['q'] += 1
        return self.n['q']

    def dma2(self, out, in_, eng='q2', after=()):
        self.rec.append((eng, lambda h, out=out, in_=in_: h.dma_start(out, in_), tuple(after)))
        self.n.setdefault(eng, 0)
        self.n[eng] += 1
        return self.n[eng]

    def replay(self, nc, block, sems, regref=None):
        handles = {}
        waitedq = {k: set() for k in self.n}
        for (_, fn, after) in self.rec:
            for (e2, c2) in after:
                if e2 == 'q' and c2 > 0:
                    waitedq[e2].add(c2)
        def mk(eng_key, sect):
            def emit(h):
                cnt = 0
                for (eng, fn, after) in self.rec:
                    if eng != eng_key:
                        continue
                    if isinstance(fn, tuple) and fn[0] == 'DMAREG':
                        _, out_ap, dten, k, dims = fn
                        for (e2, c2) in after:
                            if c2 > 0:
                                h.wait_ge(sems[e2], c2*16 if e2 == 'q' else c2)
                        if k not in regref:
                            regref[k] = h.alloc_register(f"goff{k}")
                        h.reg_load(regref[k], regref['t'][:1, k:k+1])
                        ins = h.dma_start(out_ap, bass.AP(dten, regref[k], dims))
                        ins.then_inc(sems[eng_key], 16)
                        cnt += 1
                        if cnt in waitedq[eng_key]:
                            h.wait_ge(sems[eng_key], cnt*16)
                        continue
                    if fn == 'QWAIT':
                        for (e2, c2) in after:
                            if c2 > 0:
                                h.wait_ge(sems[e2], c2*16 if e2 == 'q' else c2)
                        continue
                    if cnt > 0 and (eng_key in ('v', 's') or fn == 'BARRIER'):
                        h.wait_ge(sems[eng_key], cnt)
                    for (e2, c2) in after:
                        if c2 > 0 and e2 != eng_key:
                            h.wait_ge(sems[e2], c2*16 if e2 == 'q' else c2)
                    if fn == 'BARRIER':
                        h.sem_inc(sems[eng_key], 1)
                        cnt += 1
                        continue
                    ins = fn(h)
                    cnt += 1
                    if eng_key == 'q':
                        ins.then_inc(sems[eng_key], 16)
                        if cnt in waitedq[eng_key]:
                            h.wait_ge(sems[eng_key], cnt*16)
                    else:
                        ins.then_inc(sems[eng_key], 1)
                if eng_key == 'q' and cnt:
                    h.wait_ge(sems[eng_key], cnt*16)
                elif eng_key == 'q2' and cnt:
                    h.wait_ge(sems[eng_key], cnt)
            return emit
        block.sync(mk('q', None))
        block.tensor(mk('t', None))
        block.vector(mk('v', None))
        block.scalar(mk('s', None))
        if self.n.get('q2'):
            block.gpsimd(mk('q2', None))


class ScanABufs:
    def __init__(self, nc, stack, tag=""):
        sb = lambda n, sh: stack.enter_context(nc.sbuf_tensor(f"sa{tag}_{n}", sh, F32))
        ps = lambda n, sh: stack.enter_context(nc.psum_tensor(f"sa{tag}_{n}", sh, F32))
        self.csp = sb("csp", [128, 64]); self.cspp = sb("cspp", [128, 64])
        self.e1 = sb("e1", [128, 64]); self.e1p = sb("e1p", [128, 64])
        self.e2 = sb("e2", [128, 64]); self.e2p = sb("e2p", [128, 64])
        self.gk = sb("gk", [128, 64]); self.qt = sb("qt", [128, 64])
        self.kb = sb("kb", [128, 128]); self.ku = sb("ku", [128, 128])
        self.kn = sb("kn", [128, 64])
        self.atm = [sb("atma", [128, 128]), sb("atmb", [128, 128])]
        self.gtm = [sb("gtma", [64, 128]), sb("gtmb", [64, 128])]
        self.vst = sb("vst", [64, 128])
        self.x = sb("x", [128, 128])
        self.p = sb("p", [128, 128])
        self.xw = sb("xw", [128, 64])
        self.ltp = sb("ltp", [128, 128]); self.lp = sb("lp", [128, 128])
        self.yz = [sb(f"yz{i}", [128, 128]) for i in range(9)]
        self.tt = sb("tt", [128, 128]); self.tt2 = sb("tt2", [128, 128])
        self.kut = sb("kut", [128, 128]); self.knt = sb("knt", [64, 128])
        self.qum = [sb("quma", [128, 64]), sb("qumb", [128, 64])]
        self.qnm = [sb("qnma", [64, 64]), sb("qnmb", [64, 64])]
        self.ps1 = ps("ps1", [128, 128])
        self.ps2 = ps("ps2", [128, 128])
        self.ps3 = ps("ps3", [128, 128])
        self.ps4 = ps("ps4", [128, 64])
        self.ps5 = ps("ps5", [128, 64])
        self.pso = self.ps5
        self.psz = self.ps4


def emit_scan_chunk(ctx, cst, B, Kfm, SP, GB, Vsl, p, Z, Qfm=None, Ofm=None, ofm_after=()):
    r = ctx.run
    vc0 = r('v', lambda v: v.tensor_tensor_scan(B.csp[:], SP, SP, 0.0, ALU.add, ALU.bypass))
    vc1 = r('v', lambda v: v.tensor_sub(B.cspp[:], B.csp[:], SP))
    s1 = r('s', lambda s: s.activation(B.e1[:], B.csp[:], AF.Exp, scale=-1.0), after=[('v', vc0)])
    s2 = r('s', lambda s: s.activation(B.e1p[:], B.cspp[:], AF.Exp, scale=-1.0), after=[('v', vc1)])
    s3 = r('s', lambda s: s.activation(B.e2[:], B.csp[:], AF.Exp))
    s4 = r('s', lambda s: s.activation(B.e2p[:], B.cspp[:], AF.Exp))
    vgk = r('v', lambda v: v.scalar_tensor_tensor(B.gk[:], GB, -1.0, Kfm, ALU.mult, ALU.mult))
    vkb0 = r('v', lambda v: v.tensor_tensor(B.kb[:, 0::2], Kfm, B.e1p[:], ALU.mult), after=[('s', s2)])
    vkb1 = r('v', lambda v: v.tensor_tensor(B.kb[:, 1::2], Kfm, B.e1[:], ALU.mult), after=[('s', s1)])
    vku0 = r('v', lambda v: v.tensor_tensor(B.ku[:, 0::2], B.gk[:], B.e2p[:], ALU.mult), after=[('s', s4)])
    vku1 = r('v', lambda v: v.tensor_tensor(B.ku[:, 1::2], B.gk[:], B.e2[:], ALU.mult), after=[('s', s3)])
    vkn = r('v', lambda v: v.tensor_tensor(B.kn[:], Kfm, B.e2[:], ALU.mult))
    if Qfm is not None:
        vqt = r('v', lambda v: v.tensor_tensor(B.qt[:], Qfm, B.e1[:], ALU.mult))
    tvs = r('t', lambda t: t.matmul(B.ps3[:64, :128], cst["ident"][p:p+64, p:p+64], Vsl,
                                    start=True, stop=True, tile_position=(p, 0)))
    vvs = r('v', lambda v: v.tensor_copy(B.vst[:], B.ps3[:64, :128]), after=[('t', tvs)])
    tga = r('t', lambda t: t.matmul(B.ps1[:], B.ku[:64, :], B.kb[:64, :], start=True, stop=True,
                                    tile_position=(0, 0)), after=[('v', vku1), ('v', vkb1)])
    tgb = r('t', lambda t: t.matmul(B.ps2[:], B.ku[64:, :], B.kb[64:, :], start=True, stop=True,
                                    tile_position=(64, 0)))
    va = r('v', lambda v: v.tensor_tensor(B.atm[0][:], B.ps1[:], cst["mask_ut"][:], ALU.mult), after=[('t', tga)])
    vb = r('v', lambda v: v.tensor_tensor(B.atm[1][:], B.ps2[:], cst["mask_ut"][:], ALU.mult), after=[('t', tgb)])
    tka = r('t', lambda t: t.matmul(B.ps3[:64, :], B.kn[:64, :], B.kb[:64, :], start=True, stop=True,
                                    tile_position=(0, 0)), after=[('v', vkn), ('v', vvs)])
    vga = r('v', lambda v: v.tensor_tensor(B.gtm[0][:], B.ps3[:64, :], cst["mask_gt"][:], ALU.mult),
            after=[('t', tka)])
    tkb = r('t', lambda t: t.matmul(B.ps3[:64, :], B.kn[64:, :], B.kb[64:, :], start=True, stop=True,
                                    tile_position=(64, 0)), after=[('v', vga)])
    vgb = r('v', lambda v: v.tensor_tensor(B.gtm[1][:], B.ps3[:64, :], cst["mask_gt"][:], ALU.mult),
            after=[('t', tkb)])
    tt1 = r('t', lambda t: t.transpose(B.ps1[:], B.ku[:], cst["ident"][:]), after=[('v', va)])
    vkut = r('v', lambda v: v.tensor_copy(B.kut[:], B.ps1[:]), after=[('t', tt1)])
    tt2_ = r('t', lambda t: t.transpose(B.ps2[:64, :128], B.kn[:], cst["ident"][:]), after=[('v', vb)])
    vknt = r('v', lambda v: v.tensor_copy(B.knt[:], B.ps2[:64, :128]), after=[('t', tt2_)])
    tr1 = r('t', lambda t: t.matmul(B.ps3[:, :64], B.gtm[0][:], B.vst[:, :64], start=True, stop=False,
                                    tile_position=(0, 0)), after=[('v', vgb)])
    tr2 = r('t', lambda t: t.matmul(B.ps3[:, :64], B.kb[:64, :], Z[:64, :], start=False, stop=True,
                                    tile_position=(0, 0)))
    tr3 = r('t', lambda t: t.matmul(B.ps4[:, :64], B.gtm[1][:], B.vst[:, 64:], start=True, stop=True,
                                    tile_position=(0, 0)))
    tr4 = r('t', lambda t: t.matmul(B.ps5[:, :64], B.kb[64:, :], Z[64:, :], start=True, stop=True,
                                    tile_position=(64, 0)))
    vx = r('v', lambda v: v.tensor_copy(B.x[:, :64], B.ps3[:, :64]), after=[('t', tr4)])
    vx = r('v', lambda v: v.tensor_copy(B.x[:, 64:], B.ps4[:, :64]))
    vx = r('v', lambda v: v.tensor_add(B.x[:, 64:], B.x[:, 64:], B.ps5[:, :64]))
    if Qfm is not None:
        vq_prev, vn_prev = 0, 0
        for h in (0, 1):
            tq = r('t', lambda t, h=h: t.matmul(B.ps1[:, :64], B.ku[64*h:64*h+64, :], B.qt[64*h:64*h+64, :],
                                                start=True, stop=True, tile_position=(64*h, 0)),
                   after=[('v', vqt), ('v', vkut), ('v', vq_prev)])
            vq = r('v', lambda v, h=h: v.tensor_tensor(B.qum[h][:], B.ps1[:, :64], cst["mask_qu"][:], ALU.mult),
                   after=[('t', tq)])
            tn = r('t', lambda t, h=h: t.matmul(B.ps2[:64, :64], B.kn[64*h:64*h+64, :],
                                                B.qt[64*h:64*h+64, :], start=True, stop=True,
                                                tile_position=(64*h, 0)), after=[('v', vknt), ('v', vn_prev)])
            vn = r('v', lambda v, h=h: v.tensor_tensor(B.qnm[h][:], B.ps2[:64, :64],
                                                       cst["mask_qn"][:], ALU.mult), after=[('t', tn)])
            vq_prev, vn_prev = vq, vn
    pdone = [None, None]
    for h in (0, 1):
        atm = B.atm[h]
        vl = r('v', lambda v, atm=atm: v.tensor_tensor(B.ltp[:], atm[:], cst["mask_diag64"][:], ALU.mult),
               after=[('v', va if h == 0 else vb), ('t', ctx.mark('t'))])
        tl = r('t', lambda t: t.transpose(B.ps1[:], B.ltp[:], cst["ident"][:]), after=[('v', vl)])
        vlp = r('v', lambda v: v.tensor_copy(B.lp[:], B.ps1[:]), after=[('t', tl)])
        names = [(0, B.lp, B.ltp), (1, B.ltp, B.lp), (2, B.yz[1], B.yz[0]), (3, B.yz[0], B.yz[1]),
                 (4, B.yz[3], B.yz[2]), (5, B.yz[2], B.yz[3]), (6, B.yz[5], B.yz[4]),
                 (7, B.yz[4], B.yz[5]), (8, B.yz[6], B.yz[7])]
        last_v = vlp
        for (di, lh, rh) in names:
            tm = r('t', lambda t, lh=lh, rh=rh: t.matmul(B.ps1[:], lh[:], rh[:], start=True, stop=True),
                   after=[('v', last_v)])
            last_v = r('v', lambda v, di=di: v.tensor_copy(B.yz[di][:], B.ps1[:]), after=[('t', tm)])
        vt0 = r('v', lambda v: v.tensor_add(B.tt[:], B.ltp[:], cst["ident"][:]))
        cur, nxt = B.tt, B.tt2
        chain_lhs = [B.yz[1], B.yz[3], B.yz[5], B.yz[7], B.yz[8]]
        last = vt0
        for zk in chain_lhs:
            tm1 = r('t', lambda t, zk=zk, cur=cur: t.matmul(B.ps1[:], zk[:], cur[:], start=True, stop=False),
                    after=[('v', last)])
            tm2 = r('t', lambda t, cur=cur: t.matmul(B.ps1[:], cst["ident"][:], cur[:], start=False, stop=True))
            last = r('v', lambda v, nxt=nxt: v.tensor_copy(nxt[:], B.ps1[:]), after=[('t', tm2)])
            cur, nxt = nxt, cur
        ttfin = cur
        xcols = B.x[:, 64*h:64*h+64]
        pcols = B.p[:, 64*h:64*h+64]
        lastv = max(last, vx)
        for jb in range(2):
            r0 = 64*jb
            if jb == 0:
                tap = r('t', lambda t, r0=r0, ttfin=ttfin, xcols=xcols: t.matmul(
                    B.ps2[r0:r0+64, :64], ttfin[r0:r0+64, r0:r0+64], xcols[r0:r0+64, :],
                    start=True, stop=True, tile_position=(r0, r0)), after=[('v', lastv)])
            else:
                tx1 = r('t', lambda t, r0=r0, atm=atm, pcols=pcols: t.matmul(
                    B.ps3[r0:r0+64, :64], atm[:r0, r0:r0+64], pcols[:r0, :],
                    start=True, stop=True, tile_position=(0, r0)), after=[('v', lastv)])
                vxw = r('v', lambda v, r0=r0, xcols=xcols: v.tensor_add(
                    B.xw[r0:r0+64, :], xcols[r0:r0+64, :], B.ps3[r0:r0+64, :64]),
                        after=[('t', tx1)])
                tap = r('t', lambda t, r0=r0, ttfin=ttfin: t.matmul(
                    B.ps2[r0:r0+64, :64], ttfin[r0:r0+64, r0:r0+64], B.xw[r0:r0+64, :],
                    start=True, stop=True, tile_position=(r0, r0)), after=[('v', vxw)])
            lastv = r('v', lambda v, r0=r0, pcols=pcols: v.tensor_copy(pcols[r0:r0+64, :],
                                                                      B.ps2[r0:r0+64, :64]),
                      after=[('t', tap)])
        pdone[h] = lastv
    if Qfm is not None:
        for h in (0, 1):
            to1 = r('t', lambda t, h=h: t.matmul(B.ps3[64*h:64*h+64, :64], Z[64*h:64*h+64, :],
                                                 B.qt[64*h:64*h+64, :], start=True, stop=True,
                                                 tile_position=(64*h, 64*h)), after=[('v', pdone[h])])
            to2 = r('t', lambda t, h=h: t.matmul(B.pso[64*h:64*h+64, :], B.p[:, 64*h:64*h+64],
                                                 B.qum[h][:], start=True, stop=False,
                                                 tile_position=(0, 64*h)))
            to3 = r('t', lambda t, h=h: t.matmul(B.pso[64*h:64*h+64, :], B.vst[:, 64*h:64*h+64],
                                                 B.qnm[h][:], start=False, stop=True,
                                                 tile_position=(0, 64*h)))
        vo1 = r('v', lambda v: v.tensor_copy(Ofm[:], B.pso[:]),
                 after=[('t', ctx.mark('t'))] + list(ofm_after))
        vo = r('v', lambda v: v.tensor_add(Ofm[:], Ofm[:], B.ps3[:, :64]))
    tz = []
    for h in (0, 1):
        tz1 = r('t', lambda t, h=h: t.matmul(B.psz[64*h:64*h+64, :], B.kut[:, 64*h:64*h+64],
                                             B.p[:, 64*h:64*h+64], start=True, stop=False,
                                             tile_position=(0, 64*h)), after=[('v', pdone[h])])
        tz2 = r('t', lambda t, h=h: t.matmul(B.psz[64*h:64*h+64, :], B.knt[:, 64*h:64*h+64],
                                             B.vst[:, 64*h:64*h+64], start=False, stop=True,
                                             tile_position=(0, 64*h)))
        tz.append(tz2)
    vz1 = r('v', lambda v: v.tensor_add(Z[:], Z[:], B.psz[:]), after=[('t', tz[1])])
    vz2 = r('v', lambda v: v.tensor_scalar_mul(Z[:], Z[:], B.e1[:, 63:64]))
    return vz2



def dslice(t, r0, nr, c0, ncol, rowstride):
    return bass.AP(t, r0*rowstride + c0, [[rowstride, nr], [1, ncol]])

def dslab(t, r0, nblk, c0, ncol, rowstride):
    return bass.AP(t, r0*rowstride + c0, [[rowstride, 128], [128*rowstride, nblk], [1, ncol]])


def build_core_program(N=16384, NO_OVERRIDE=None):
    _dk = {}
    D = 1024
    MQ = ((N - 1)//3 + 1 + 3)//4
    MQ = MQ + (-MQ) % 2
    QP = 3*MQ
    TILE = QP//6
    NT = 6
    GRP = 512
    NG = N//GRP
    C = 64
    assert (2*N) % 3 == 2
    n0 = (2*N)//3
    NO = N - n0
    nc = bass.Bass("TRN2", target_bir_lowering=False)
    ap_in = {}
    def din(name, shape, dt=F32):
        ap_in[name] = nc.dram_tensor(name, list(shape), dt, kind="ExternalInput")
        return ap_in[name]
    x_fm = din("x_fm", [D, N + 16])
    xq_fm = din("xq_fm", [D, QP])
    xq3 = din("xq3", [D, MQ])
    cmask = din("cmask", [128, 3])
    actc = din("actc", [128, 8])
    gidx = din("gidx", [1, 8], I32)
    wk_s = din("wk_s", [D, 256]); wv_s = din("wv_s", [D, 256]); wq_s = din("wq_s", [D, 256])
    wfg = din("wfg", [D, 68])
    wf2_s = din("wf2_s", [64, 256])
    wo1 = din("wo1", [D, 64]); wo2 = din("wo2", [64, D])
    ln_w = din("ln_w", [D, 1])
    wout = din("wout", [D, D])
    ffn_g = din("ffn_g", [D, 3072], I32); ffn_u = din("ffn_u", [D, 3072], I32)
    ffn_d = din("ffn_d", [3072, D], I32)
    wr_t = din("wr_t", [D, D]); ww_t = din("ww_t", [D, D]); wkb_t = din("wkb_t", [D, D])
    wvb_t = din("wvb_t", [D, D]); wo_t = din("wo_t", [D, D])
    ln1g = din("ln1g", [D, 1]); ln1b = din("ln1b", [D, 1])
    ln2g = din("ln2g", [D, 1]); ln2b = din("ln2b", [D, 1])
    tdec = din("tdec", [D, 1])
    wkey_t = din("wkey_t", [D, 4096]); wval_t = din("wval_t", [4096, D]); wrec_t = din("wrec_t", [D, D])
    cstnp = scan_consts_np()
    for nme, arr in cstnp.items():
        din("cst_"+nme, arr.shape)
    out_fm = nc.dram_tensor("out_fm", [D, QP], F32, kind="ExternalOutput")

    MQ4 = 4*MQ
    obuf = nc.dram_tensor("obuf", [256, MQ4], F32, **_dk)
    ofull = nc.dram_tensor("ofull", [1024, MQ4], F32, **_dk)
    cbuf = nc.dram_tensor("cbuf", [D*2, 1], F32)
    cfull = nc.dram_tensor("cfull", [4*D*2, 1], F32)
    wbf = {}
    for nme, shp in [("ffn_g", (D, 3072)), ("ffn_u", (D, 3072)), ("ffn_d", (3072, D))]:
        wbf[nme] = nc.dram_tensor(nme+"_bf", list(shp), BF16, **_dk)
    HLW = [("wr_t", (D, D), "g1"), ("ww_t", (D, D), "g1"), ("wkb_t", (D, D), "g1"),
           ("wvb_t", (D, D), "g1"), ("wo_t", (D, D), None), ("wkey_t", (D, 4096), "g2"),
           ("wval_t", (4096, D), None), ("wrec_t", (D, D), "g2")]
    for nme, shp, _ in HLW:
        wbf[nme+"_h"] = nc.dram_tensor(nme+"_h", list(shp), BF16)
        wbf[nme+"_l"] = nc.dram_tensor(nme+"_l", list(shp), BF16)
    nd_scr = nc.dram_tensor("nd_scr", [2*D, QP], F32, **_dk)
    r_scr = nc.dram_tensor("r_scr", [D, QP], F32, **_dk)
    x2_scr = nc.dram_tensor("x2_scr", [D, QP], F32, **_dk)
    upd_scr = nc.dram_tensor("upd_scr", [D, MQ], F32, **_dk)
    x1_dbg = nc.dram_tensor("x1_dbg", [D, QP], F32, **_dk) if _dk else None
    xn_dbg = nc.dram_tensor("xn_dbg", [D, QP], BF16, **_dk) if _dk else None

    R = Rec()
    top = ExitStack()
    W12 = ExitStack()
    sbt = lambda nme, sh, dt=F32: top.enter_context(nc.sbuf_tensor(nme, sh, dt))
    w12sb = lambda nme, sh, dt=F32: W12.enter_context(nc.sbuf_tensor(nme, sh, dt))

    cst = {}
    for nme, arr in cstnp.items():
        cst[nme] = sbt("c_"+nme, list(arr.shape))
        R.dma(cst[nme][:], ap_in["cst_"+nme].ap())
    onesb = sbt("onesb", [128, 1], BF16)
    cmk = sbt("cmk", [128, 3])
    gidx_t = sbt("gidx_t", [1, 8], I32)
    actc_t = sbt("actc_t", [128, 8])
    ACT_CONSTS = [1e-5, 1.0, -1.0, float(QP), float(TILE), 1.0/127.0, 1e-8, 2.0]
    ngx = R.dma(gidx_t[:], ap_in["gidx"].ap())
    R.dma(actc_t[:], ap_in["actc"].ap())
    for ci_, cv_ in enumerate(ACT_CONSTS):
        nc.const_aps.register(F32, cv_, actc_t[:, ci_:ci_+1])
    R.dma(cmk[:], ap_in["cmask"].ap())
    nload0 = R.n['q']
    R.run('v', lambda v: v.tensor_copy(onesb[:], cst["ones_col"][:]), after=[('q', nload0)])

    wres = {}
    w1col = {}
    RES_W = [("wk_s", wk_s, 256), ("wv_s", wv_s, 256), ("wq_s", wq_s, 256),
             ("wfg", wfg, 68), ("wo1", wo1, 64), ("wout", wout, 1024)]
    for nme in ["ln_w", "ln1g", "ln1b", "ln2g", "ln2b", "tdec"]:
        wres[nme] = sbt("r_"+nme, [128, 8])
    for nme, shp, gsel in HLW:
        if gsel is not None:
            w1col[nme] = sbt(nme+"_w1c", [128, max(1, shp[1]//128)])
            w1col[nme+"_b"] = sbt(nme+"_bc", [128, max(1, shp[1]//128)])
    d4096 = sbt("d4096", [128, 8])
    dcol = sbt("dcol", [128, 8])
    d512t = sbt("d512t", [128, 8])
    tpow = sbt("tpow", [128, 8*TILE])
    hcol = sbt("hcol", [128, 16])
    hbase = sbt("hbase", [128, 16])
    zst = [sbt("zst0", [128, 64]), sbt("zst1", [128, 64])]
    hs2bT = sbt("hs2bT", [128, 2], BF16)
    ztile = sbt("ztile", [128, 16])
    carry_n = sbt("carry_n", [128, 8])
    carry_d = sbt("carry_d", [128, 8])
    hall = sbt("hall", [128, 64])
    dqm1 = sbt("dqm1", [128, 8])
    dmr = sbt("dmr", [128, 8])
    for nme, ap, cols in RES_W:
        wres[nme+"_h"] = w12sb(nme+"_h", [128, 8*cols], BF16)
        wres[nme+"_l"] = w12sb(nme+"_l", [128, 8*cols], BF16)
        if nme != "wout":
            w1col[nme] = w12sb(nme+"_w1c", [128, max(1, cols//128)])
            w1col[nme+"_row"] = w12sb(nme+"_w1r", [1, cols])
    wres["wo2_h"] = w12sb("wo2_h", [64, 1024], BF16)
    wres["wo2_l"] = w12sb("wo2_l", [64, 1024], BF16)
    wres["wf2_h"] = w12sb("wf2_h", [64, 256], BF16)
    wres["wf2_l"] = w12sb("wf2_l", [64, 256], BF16)

    def w1_transposes(wrow_psum_ap, dstcol, ncc, cbase, neg, tmp_sb):
        for j in range(ncc):
            tv = R.run('v', lambda v, j=j: v.tensor_copy(tmp_sb[:1, 128*j:128*(j+1)],
                                                         wrow_psum_ap[:1, 128*j:128*(j+1)]),
                       after=[('t', R.mark('t'))])
            tt = R.run('t', lambda t, j=j: t.transpose(psc_g[0][:128, :1],
                                                       tmp_sb[:1, 128*j:128*(j+1)],
                                                       cst["ident"][:1, :1]), after=[('v', tv)])
            R.run('v', lambda v, j=j: v.tensor_scalar_mul(dstcol[:, cbase+j:cbase+j+1],
                                                          psc_g[0][:128, :1],
                                                          -1.0 if neg else 1.0),
                  after=[('t', tt)])

    with ExitStack() as P0:
        p0sb = lambda nme, sh, dt=F32: P0.enter_context(nc.sbuf_tensor(nme, sh, dt))
        tmp = p0sb("p0tmp", [128, 512])
        tmp2 = p0sb("p0tmp2", [128, 512])
        tmpi = p0sb("p0tmpi", [128, 512], I32)
        tmpb = p0sb("p0tmpb", [128, 512], BF16)
        tmpb2 = p0sb("p0tmpb2", [128, 512], BF16)
        rowsb = p0sb("p0rowsb", [1, 512])
        psr1 = P0.enter_context(nc.psum_tensor("p0psr1", [1, 512], F32))
        psr2 = P0.enter_context(nc.psum_tensor("p0psr2", [1, 512], F32))
        psc_g = [P0.enter_context(nc.psum_tensor("p0psc", [128, 128], F32))]
        for nme, ap, cols in RES_W:
            for c0 in range(0, cols, 512):
                cw = min(512, cols - c0)
                for i in range(8):
                    nq = R.dma(tmp[:, :cw], dslice(ap, 128*i, 128, c0, cw, cols),
                               after=[('v', R.mark('v')), ('t', R.mark('t'))])
                    R.run('v', lambda v, i=i, cols=cols, c0=c0, cw=cw, nme=nme: v.tensor_copy(
                        wres[nme+"_h"][:, cols*i+c0:cols*i+c0+cw], tmp[:, :cw]), after=[('q', nq)])
                    R.run('v', lambda v, i=i, cols=cols, c0=c0, cw=cw, nme=nme: v.tensor_sub(
                        wres[nme+"_l"][:, cols*i+c0:cols*i+c0+cw], tmp[:, :cw],
                        wres[nme+"_h"][:, cols*i+c0:cols*i+c0+cw]))
                    if nme != "wout":
                        R.run('t', lambda t, i=i, cw=cw: t.matmul(
                            psr1[:, :cw], cst["ones_col"][:], tmp[:, :cw],
                            start=(i == 0), stop=(i == 7)), after=[('v', R.mark('v'))])
                if nme == "wout":
                    continue
                R.run('v', lambda v, nme=nme, c0=c0, cw=cw: v.tensor_scalar_mul(
                    w1col[nme+"_row"][:, c0:c0+cw], psr1[:, :cw], -1.0),
                      after=[('t', R.mark('t'))])
                if cw >= 128:
                    w1_transposes(psr1, w1col[nme], cw//128, c0//128, True, rowsb)
                else:
                    tv = R.run('v', lambda v, cw=cw: v.tensor_copy(rowsb[:1, :cw], psr1[:1, :cw]),
                               after=[('t', R.mark('t'))])
                    tt = R.run('t', lambda t, cw=cw: t.transpose(psc_g[0][:cw, :1],
                                                                 rowsb[:1, :cw], cst["ident"][:1, :1]),
                               after=[('v', tv)])
                    R.run('v', lambda v, cw=cw, nme=nme: v.tensor_scalar_mul(
                        w1col[nme][:cw, :1], psc_g[0][:cw, :1], -1.0), after=[('t', tt)])
        nq = R.dma(tmp[:64, :512], dslice(wo2, 0, 64, 0, 512, 1024), after=[('v', R.mark('v'))])
        nq2 = R.dma(tmp2[:64, :512], dslice(wo2, 0, 64, 512, 512, 1024))
        R.run('v', lambda v: v.tensor_copy(wres["wo2_h"][:, :512], tmp[:64, :512]), after=[('q', nq2)])
        R.run('v', lambda v: v.tensor_sub(wres["wo2_l"][:, :512], tmp[:64, :512], wres["wo2_h"][:, :512]))
        R.run('v', lambda v: v.tensor_copy(wres["wo2_h"][:, 512:], tmp2[:64, :512]))
        R.run('v', lambda v: v.tensor_sub(wres["wo2_l"][:, 512:], tmp2[:64, :512], wres["wo2_h"][:, 512:]))
        nq = R.dma(tmp[:64, :256], dslice(wf2_s, 0, 64, 0, 256, 256), after=[('v', R.mark('v'))])
        R.run('v', lambda v: v.tensor_copy(wres["wf2_h"][:], tmp[:64, :256]), after=[('q', nq)])
        R.run('v', lambda v: v.tensor_sub(wres["wf2_l"][:], tmp[:64, :256], wres["wf2_h"][:]))
        nl = 0
        for nme, ap in [("ln_w", ln_w), ("ln1g", ln1g), ("ln1b", ln1b), ("ln2g", ln2g),
                        ("ln2b", ln2b), ("tdec", tdec)]:
            nl = R.dma(wres[nme][:], bass.AP(ap, 0, [[1, 128], [128, 8]]), slow=True)
        R.run('s', lambda s: s.activation(d4096[:], wres["tdec"][:], AF.Exp, scale=float(QP)),
              after=[('q', nl)])
        R.run('s', lambda s: s.activation(d512t[:], wres["tdec"][:], AF.Exp, scale=float(TILE)))
        sdec = R.run('s', lambda s: s.activation(dcol[:], wres["tdec"][:], AF.Exp))
        for j in range(8):
            R.run('v', lambda v, j=j: v.tensor_tensor_scan(
                tpow[:, TILE*j:TILE*(j+1)], dcol[:, j:j+1].broadcast_to((128, TILE)),
                cst["zeros_col"][:, :1].broadcast_to((128, TILE)), 1.0, ALU.mult, ALU.add),
                  after=[('s', sdec)])
        for nme, src, rows, cols in [("ffn_g", ffn_g, D, 3072), ("ffn_u", ffn_u, D, 3072),
                                     ("ffn_d", ffn_d, 3072, D)]:
            for i in range(rows//128):
                for c0 in range(0, cols, 512):
                    nq = R.dma(tmpi[:], dslice(src, 128*i, 128, c0, 512, cols),
                               after=[('v', R.mark('v'))])
                    nv = R.run('v', lambda v: v.tensor_scalar_mul(tmpb[:], tmpi[:], 1.0),
                               after=[('q', R.n['q'])])
                    R.dma(dslice(wbf[nme], 128*i, 128, c0, 512, cols), tmpb[:],
                          after=[('v', nv)])
        for nme, shp, gsel in HLW:
            rows, cols = shp
            src = {"wr_t": wr_t, "ww_t": ww_t, "wkb_t": wkb_t, "wvb_t": wvb_t, "wo_t": wo_t,
                   "wkey_t": wkey_t, "wval_t": wval_t, "wrec_t": wrec_t}[nme]
            gv = {"g1": "ln1g", "g2": "ln2g", None: None}[gsel]
            bv = {"g1": "ln1b", "g2": "ln2b", None: None}[gsel]
            for c0 in range(0, cols, 512):
                for i in range(rows//128):
                    nq = R.dma(tmp[:], dslice(src, 128*i, 128, c0, 512, cols),
                               after=[('v', R.mark('v')), ('t', R.mark('t'))])
                    if gv is not None:
                        R.run('v', lambda v, gv=gv, i=i: v.tensor_scalar_mul(
                            tmp2[:], tmp[:], wres[gv][:, i:i+1]), after=[('q', nq)])
                        R.run('v', lambda v: v.tensor_copy(tmpb[:], tmp2[:]))
                        tga = R.run('t', lambda t, i=i, rows=rows: t.matmul(
                            psr1[:], cst["ones_col"][:], tmp2[:],
                            start=(i == 0), stop=(i == rows//128-1)),
                              after=[('v', R.mark('v'))])
                        bva = R.run('v', lambda v, bv=bv, i=i: v.tensor_scalar_mul(
                            tmp2[:], tmp[:], wres[bv][:, i:i+1]), after=[('t', tga)])
                        R.run('t', lambda t, i=i, rows=rows: t.matmul(
                            psr2[:], cst["ones_col"][:], tmp2[:],
                            start=(i == 0), stop=(i == rows//128-1)),
                              after=[('v', bva)])
                        src_for_lo = tmp2
                        R.run('v', lambda v, gv=gv, i=i: v.tensor_scalar_mul(
                            tmp2[:], tmp[:], wres[gv][:, i:i+1]), after=[('t', R.mark('t'))])
                        nst = R.dma(dslice(wbf[nme+"_h"], 128*i, 128, c0, 512, cols), tmpb[:],
                                    after=[('v', R.mark('v'))])
                        R.run('v', lambda v: v.tensor_sub(tmpb2[:], tmp2[:], tmpb[:]),
                              after=[('q', nst)])
                    else:
                        R.run('v', lambda v: v.tensor_copy(tmpb[:], tmp[:]), after=[('q', nq)])
                        nst = R.dma(dslice(wbf[nme+"_h"], 128*i, 128, c0, 512, cols), tmpb[:],
                                    after=[('v', R.mark('v'))])
                        R.run('v', lambda v: v.tensor_sub(tmpb2[:], tmp[:], tmpb[:]),
                              after=[('q', nst)])
                    R.dma(dslice(wbf[nme+"_l"], 128*i, 128, c0, 512, cols), tmpb2[:],
                          after=[('v', R.mark('v'))])
                if gv is not None:
                    w1_transposes(psr1, w1col[nme], 4, c0//128, True, rowsb)
                    w1_transposes(psr2, w1col[nme+"_b"], 4, c0//128, False, rowsb)
    R.barrier()

    R.run('v', lambda v: v.tensor_copy(hs2bT[:], cst["headsel2T"][:]))
    zv = R.run('v', lambda v: v.memset(ztile[:], 0.0))
    if MQ4 > NO:
        for rb in range(2):
            R.dma(dslice(obuf, 128*rb, 128, NO, MQ4 - NO, MQ4),
                  ztile[:, :MQ4 - NO], after=[('v', zv)])
    with ExitStack() as P1:
        p1sb = lambda nme, sh, dt=F32: P1.enter_context(nc.sbuf_tensor(nme, sh, dt))
        xg = p1sb("xg", [128, 4096])
        xh = p1sb("xh", [128, 4096], BF16)
        xl = p1sb("xl", [128, 4096], BF16)
        sqb = p1sb("sqb", [128, 4096], BF16)
        ptmp = p1sb("ptmp", [128, 512])
        kraw = p1sb("kraw", [128, 1024])
        kfm = p1sb("kfm", [128, 1024])
        spfm = p1sb("spfm", [128, 1024])
        qfm = p1sb("qfm", [128, 1024])
        vtm = p1sb("vtm", [128, 1024])
        fg = p1sb("fg", [128, 512])
        gb = p1sb("gb", [128, 1024])
        z1bh = p1sb("z1bh", [64, 512], BF16)
        z1bl = p1sb("z1bl", [64, 512], BF16)
        sqk = p1sb("sqk", [128, 512], BF16)
        strow = p1sb("strow", [1, 2048])
        rstdb = p1sb("rstdb", [128, 512])
        mrb = p1sb("mrb", [128, 512])
        rstdc = p1sb("rstdc", [128, 4])
        rn2 = p1sb("rn2", [2, 512])
        ofm = [p1sb("ofm0", [128, 64]), p1sb("ofm1", [128, 64])]
        psA = P1.enter_context(nc.psum_tensor("p1psA", [1, 512], F32))
        psP = P1.enter_context(nc.psum_tensor("p1psP", [128, 512], F32))
        psP2 = P1.enter_context(nc.psum_tensor("p1psP2", [128, 512], F32))
        SB = [ScanABufs(nc, P1, tag="0")]
        odma_last = {0: 0, 1: 0}
        R.run('v', lambda v: v.memset(zst[0][:], 0.0))
        R.run('v', lambda v: v.memset(zst[1][:], 0.0))

        def stats512(do_cols):
            for i in range(8):
                R.run('s', lambda s, i=i: s.activation(sqb[:, 512*i:512*(i+1)],
                                                       xg[:, 512*i:512*(i+1)], AF.Square),
                      after=[('v', R.mark('v'))])
            for i in range(8):
                R.run('t', lambda t, i=i: t.matmul(psA[:], onesb[:], xh[:, 512*i:512*(i+1)],
                                                   start=(i == 0), stop=False),
                      after=[('v', R.mark('v')), ('s', R.mark('s'))])
                R.run('t', lambda t, i=i: t.matmul(psA[:], onesb[:], xl[:, 512*i:512*(i+1)],
                                                   start=False, stop=(i == 7)))
            mu = R.run('v', lambda v: v.tensor_scalar_mul(strow[:, 0:512], psA[:], 1.0/D),
                       after=[('t', R.mark('t'))])
            for i in range(8):
                R.run('t', lambda t, i=i: t.matmul(psA[:], onesb[:], sqb[:, 512*i:512*(i+1)],
                                                   start=(i == 0), stop=(i == 7)),
                      after=[('v', mu)] if i == 0 else ())
            R.run('v', lambda v: v.tensor_scalar_mul(strow[:, 1536:2048], psA[:], 1.0/D),
                  after=[('t', R.mark('t'))])
            R.run('v', lambda v: v.tensor_tensor(strow[:, 1024:1536], strow[:, 0:512],
                                                 strow[:, 0:512], ALU.mult))
            R.run('v', lambda v: v.tensor_sub(strow[:, 1536:2048], strow[:, 1536:2048],
                                              strow[:, 1024:1536]))
            rst = R.run('s', lambda s: s.activation(strow[:, 512:1024], strow[:, 1536:2048],
                                                    AF.Sqrt, bias=1e-5), after=[('v', R.mark('v'))])
            R.run('v', lambda v: v.reciprocal(strow[:, 512:1024], strow[:, 512:1024]),
                  after=[('s', rst)])
            mr = R.run('v', lambda v: v.tensor_tensor(strow[:, 1024:1536], strow[:, 0:512],
                                                      strow[:, 512:1024], ALU.mult))
            t1 = R.run('t', lambda t: t.matmul(psP[:], cst["ones_row"][:1, :128],
                                               strow[:1, 512:1024], start=True, stop=True),
                       after=[('v', mr)])
            v1 = R.run('v', lambda v: v.tensor_copy(rstdb[:], psP[:]), after=[('t', t1)])
            t2 = R.run('t', lambda t: t.matmul(psP[:], cst["ones_row"][:1, :128],
                                               strow[:1, 1024:1536], start=True, stop=True),
                       after=[('v', v1)])
            R.run('v', lambda v: v.tensor_copy(mrb[:], psP[:]), after=[('t', t2)])
            if do_cols:
                for tb in range(4):
                    tt = R.run('t', lambda t, tb=tb: t.transpose(
                        psP2[:128, :1], strow[:1, 512+128*tb:512+128*(tb+1)], cst["ident"][:1, :1]),
                        after=[('v', R.mark('v'))])
                    R.run('v', lambda v, tb=tb: v.tensor_copy(rstdc[:, tb:tb+1], psP2[:128, :1]),
                          after=[('t', tt)])

        def proj_fm(wname, ocols, dests, act=None):
            for oc, (dst_ap, ow) in enumerate(dests):
                wh = wres[wname+"_h"]; wl = wres[wname+"_l"]
                for i in range(8):
                    cb = ocols*i + 128*oc
                    R.run('t', lambda t, i=i, cb=cb, ow=ow, wh=wh: t.matmul(
                        psP[:ow, :], wh[:, cb:cb+ow], xh[:, 512*i:512*(i+1)],
                        start=(i == 0), stop=False), after=[('v', R.mark('v'))] if i == 0 else ())
                    R.run('t', lambda t, i=i, cb=cb, ow=ow, wh=wh: t.matmul(
                        psP[:ow, :], wh[:, cb:cb+ow], xl[:, 512*i:512*(i+1)],
                        start=False, stop=False))
                    R.run('t', lambda t, i=i, cb=cb, ow=ow, wl=wl: t.matmul(
                        psP[:ow, :], wl[:, cb:cb+ow], xh[:, 512*i:512*(i+1)],
                        start=False, stop=(i == 7)))
                e1 = R.run('v', lambda v, ow=ow: v.tensor_tensor(ptmp[:ow, :], psP[:ow, :],
                                                                 rstdb[:ow, :], ALU.mult),
                           after=[('t', R.mark('t')), ('s', R.mark('s'))])
                e2 = R.run('v', lambda v, ow=ow, oc=oc, dst_ap=dst_ap, wname=wname:
                           v.scalar_tensor_tensor(dst_ap, mrb[:ow, :], w1col[wname][:ow, oc:oc+1],
                                                  ptmp[:ow, :], ALU.mult, ALU.add))
                if act is not None:
                    R.run('s', lambda s, dst_ap=dst_ap, act=act: s.activation(dst_ap, dst_ap, act),
                          after=[('v', e2)])
            return R.mark('v')

        for gi in range(NG):
            t0g = GRP*gi
            need_q = (t0g + GRP) > n0
            R.barrier(with_q=True)
            nq = R.dma(xg[:], dslab(x_fm, 0, 8, t0g, 512, N + 16))
            for i in range(8):
                R.run('v', lambda v, i=i: v.tensor_copy(xh[:, 512*i:512*(i+1)],
                                                        xg[:, 512*i:512*(i+1)]),
                      after=[('q', nq)] if i == 0 else ())
                R.run('v', lambda v, i=i: v.tensor_sub(xl[:, 512*i:512*(i+1)],
                                                       xg[:, 512*i:512*(i+1)],
                                                       xh[:, 512*i:512*(i+1)]))
            stats512(True)
            proj_fm("wk_s", 256, [(kraw[0:128, 0:512], 128), (kraw[0:128, 512:1024], 128)])
            for p in range(2):
                sA = R.run('s', lambda s, p=p: s.activation(kraw[:, 512*p:512*(p+1)],
                                                            kraw[:, 512*p:512*(p+1)], AF.Silu),
                           after=[('v', R.mark('v'))])
                sQ = R.run('s', lambda s, p=p: s.activation(sqk[:], kraw[:, 512*p:512*(p+1)],
                                                            AF.Square), after=[('t', R.mark('t'))])
                tn = R.run('t', lambda t: t.matmul(psP2[:2, :], hs2bT[:], sqk[:],
                                                   start=True, stop=True),
                           after=[('s', sQ), ('v', R.mark('v'))])
                sR = R.run('s', lambda s: s.activation(rn2[:], psP2[:2, :], AF.Sqrt),
                           after=[('t', tn)])
                vm = R.run('v', lambda v: v.tensor_scalar_max(rn2[:], rn2[:], 1e-12),
                           after=[('s', sR)])
                vm = R.run('v', lambda v: v.reciprocal(rn2[:], rn2[:]))
                tb2 = R.run('t', lambda t: t.matmul(psP2[:], cst["headsel2"][:], rn2[:],
                                                    start=True, stop=True), after=[('v', vm)])
                R.run('v', lambda v, p=p: v.tensor_tensor(kfm[:, 512*p:512*(p+1)],
                                                          kraw[:, 512*p:512*(p+1)],
                                                          psP2[:], ALU.mult), after=[('t', tb2)])
            for tb in range(4):
                for i in range(8):
                    R.run('t', lambda t, tb=tb, i=i: t.matmul(
                        psP[:, :256], xh[:, 512*i+128*tb:512*i+128*tb+128],
                        wres["wv_s_h"][:, 256*i:256*(i+1)], start=(i == 0), stop=False),
                        after=[('v', R.mark('v'))] if i == 0 else ())
                    R.run('t', lambda t, tb=tb, i=i: t.matmul(
                        psP[:, :256], xl[:, 512*i+128*tb:512*i+128*tb+128],
                        wres["wv_s_h"][:, 256*i:256*(i+1)], start=False, stop=False))
                    R.run('t', lambda t, tb=tb, i=i: t.matmul(
                        psP[:, :256], xh[:, 512*i+128*tb:512*i+128*tb+128],
                        wres["wv_s_l"][:, 256*i:256*(i+1)], start=False, stop=False))
                tk1 = R.run('t', lambda t, tb=tb: t.matmul(
                    psP[:, :256], strow[:1, 128*tb:128*tb+128],
                    w1col["wv_s_row"][:1, :256], start=False, stop=True))
                ev = R.run('v', lambda v, tb=tb: v.tensor_scalar_mul(
                    ptmp[:, :256], psP[:, :256], rstdc[:, tb:tb+1]),
                    after=[('t', tk1), ('s', R.mark('s'))])
                R.run('s', lambda s, tb=tb: s.activation(vtm[:, 256*tb:256*(tb+1)],
                                                         ptmp[:, :256], AF.Silu),
                      after=[('v', ev)])
            if need_q:
                proj_fm("wq_s", 256, [(qfm[0:128, 0:512], 128), (qfm[0:128, 512:1024], 128)],
                        act=AF.Silu)
            proj_fm("wfg", 68, [(fg[0:68, :], 68)])
            for p_ in range(2):
                tgp = R.run('t', lambda t, p_=p_: t.matmul(
                    psP2[:], cst["hsg"][64:68, 128*p_:128*p_+128], fg[64:68, :],
                    start=True, stop=True, tile_position=(64, 0)),
                    after=[('v', R.mark('v')), ('s', R.mark('s'))])
                R.run('s', lambda s, p_=p_: s.activation(gb[:, 512*p_:512*(p_+1)], psP2[:],
                                                         AF.Sigmoid), after=[('t', tgp)])
            vzh = R.run('v', lambda v: v.tensor_copy(z1bh[:], fg[:64, :]))
            vzl = R.run('v', lambda v: v.tensor_sub(z1bl[:], fg[:64, :], z1bh[:]))
            for p in range(2):
                tA = R.run('t', lambda t, p=p: t.matmul(psP[:], wres["wf2_h"][:, 128*p:128*(p+1)],
                                                        z1bh[:], start=True, stop=False),
                           after=[('v', vzl), ('s', R.mark('s'))])
                R.run('t', lambda t, p=p: t.matmul(psP[:], wres["wf2_h"][:, 128*p:128*(p+1)],
                                                   z1bl[:], start=False, stop=False))
                tC = R.run('t', lambda t, p=p: t.matmul(psP[:], wres["wf2_l"][:, 128*p:128*(p+1)],
                                                        z1bh[:], start=False, stop=True))
                sE = R.run('s', lambda s: s.activation(ptmp[:], psP[:], AF.Exp, scale=-1.0),
                           after=[('t', tC)])
                R.run('s', lambda s, p=p: s.activation(spfm[:, 512*p:512*(p+1)], ptmp[:],
                                                       AF.Ln, bias=1.0))
            R.barrier()
            for ci in range(8):
                t0c = t0g + C*ci
                par = ci % 2
                tb = ci // 2
                for p in range(2):
                    R.barrier()
                    Vsl = vtm[64*par:64*par+64, 256*tb+128*p:256*tb+128*(p+1)]
                    needo = (t0c + C) > n0
                    lastv = emit_scan_chunk(
                        R, cst, SB[0],
                        kfm[:, 512*p+C*ci:512*p+C*ci+C],
                        spfm[:, 512*p+C*ci:512*p+C*ci+C],
                        gb[:, 512*p+C*ci:512*p+C*ci+C],
                        Vsl, 64*par, zst[p],
                        Qfm=(qfm[:, 512*p+C*ci:512*p+C*ci+C] if needo else None),
                        Ofm=(ofm[p] if needo else None),
                        ofm_after=((('q', odma_last[p]),) if needo else ()))
                    if needo:
                        mlo = max(t0c, n0)
                        cw = t0c + C - mlo
                        R.dma(dslice(obuf, 128*p, 128, mlo - n0, cw, MQ4),
                              ofm[p][:, (mlo - t0c):(mlo - t0c) + cw],
                              after=[('v', lastv)])
                        odma_last[p] = R.n['q']
    R.barrier(with_q=True)
    R.run('q2', lambda h: h.collective_compute(
        "AllGather", ALU.bypass, replica_groups=[[0, 1, 2, 3], [4, 5, 6, 7]],
        ins=[obuf.ap().opt()], outs=[ofull.ap().opt()]), after=[('q', R.n['q'])])
    naa = R.n['q2']
    with ExitStack() as P2:
        p2sb = lambda nme, sh, dt=F32: P2.enter_context(nc.sbuf_tensor(nme, sh, dt))
        SL = []
        c0 = 0
        while c0 < MQ:
            SL.append((c0, min(512, MQ - c0)))
            c0 += 512
        z1g = p2sb("z1g", [64, MQ])
        z1gh = p2sb("z1gh", [64, MQ], BF16)
        z1gl = p2sb("z1gl", [64, MQ], BF16)
        xq = p2sb("xq", [128, 8*512])
        xqh = p2sb("xqh", [128, 8*512], BF16)
        xql = p2sb("xql", [128, 8*512], BF16)
        sqb2 = p2sb("sqb2", [128, 8*512], BF16)
        ptm = p2sb("ptm", [128, 512])
        xln = p2sb("xln", [128, 8*512])
        xlnh = p2sb("xlnh", [128, 8*512], BF16)
        xlnl = p2sb("xlnl", [128, 8*512], BF16)
        gt8 = p2sb("gt8", [128, 512])
        strow2 = p2sb("strow2", [1, 2048])
        rstdb2 = p2sb("rstdb2", [128, 512])
        mrb2 = p2sb("mrb2", [128, 512])
        psA2 = P2.enter_context(nc.psum_tensor("p2psA", [1, 512], F32))
        psC2 = P2.enter_context(nc.psum_tensor("p2psC", [128, 512], F32))
        psD2 = P2.enter_context(nc.psum_tensor("p2psD", [128, 512], F32))
        for (c0, cw) in SL:
            R.barrier(with_q=True)
            nq = R.dma(xq[:, :8*cw].rearrange("p (i c) -> p (i c)", i=8) if False else xq[:, :8*cw],
                       dslab(xq3, 0, 8, c0, cw, MQ))
            for i in range(8):
                R.run('v', lambda v, i=i, cw=cw: v.tensor_copy(xqh[:, cw*i:cw*(i+1)],
                                                               xq[:, cw*i:cw*(i+1)]),
                      after=[('q', nq)] if i == 0 else ())
                R.run('v', lambda v, i=i, cw=cw: v.tensor_sub(xql[:, cw*i:cw*(i+1)],
                                                              xq[:, cw*i:cw*(i+1)],
                                                              xqh[:, cw*i:cw*(i+1)]))
            for i in range(8):
                R.run('s', lambda s, i=i, cw=cw: s.activation(sqb2[:, cw*i:cw*(i+1)],
                                                              xq[:, cw*i:cw*(i+1)], AF.Square),
                      after=[('v', R.mark('v'))])
            for i in range(8):
                R.run('t', lambda t, i=i, cw=cw: t.matmul(psA2[:, :cw], onesb[:],
                                                          xqh[:, cw*i:cw*(i+1)],
                                                          start=(i == 0), stop=False),
                      after=[('v', R.mark('v')), ('s', R.mark('s'))])
                R.run('t', lambda t, i=i, cw=cw: t.matmul(psA2[:, :cw], onesb[:],
                                                          xql[:, cw*i:cw*(i+1)],
                                                          start=False, stop=(i == 7)))
            nmu = R.run('v', lambda v, cw=cw: v.tensor_scalar_mul(strow2[:, 0:cw], psA2[:, :cw], 1.0/D),
                  after=[('t', R.mark('t'))])
            for i in range(8):
                R.run('t', lambda t, i=i, cw=cw: t.matmul(psA2[:, :cw], onesb[:],
                                                          sqb2[:, cw*i:cw*(i+1)],
                                                          start=(i == 0), stop=(i == 7)),
                      after=[('v', nmu)] if i == 0 else ())
            R.run('v', lambda v, cw=cw: v.tensor_scalar_mul(strow2[:, 1536:1536+cw], psA2[:, :cw], 1.0/D),
                  after=[('t', R.mark('t'))])
            R.run('v', lambda v, cw=cw: v.tensor_tensor(strow2[:, 1024:1024+cw], strow2[:, 0:cw],
                                                        strow2[:, 0:cw], ALU.mult))
            R.run('v', lambda v, cw=cw: v.tensor_sub(strow2[:, 1536:1536+cw], strow2[:, 1536:1536+cw],
                                                     strow2[:, 1024:1024+cw]))
            rs2 = R.run('s', lambda s, cw=cw: s.activation(strow2[:, 512:512+cw],
                                                           strow2[:, 1536:1536+cw],
                                                           AF.Sqrt, bias=1e-5),
                        after=[('v', R.mark('v'))])
            R.run('v', lambda v, cw=cw: v.reciprocal(strow2[:, 512:512+cw],
                                                     strow2[:, 512:512+cw]),
                  after=[('s', rs2)])
            mr2 = R.run('v', lambda v, cw=cw: v.tensor_tensor(strow2[:, 1024:1024+cw],
                                                              strow2[:, 0:cw],
                                                              strow2[:, 512:512+cw], ALU.mult))
            tb1 = R.run('t', lambda t, cw=cw: t.matmul(psC2[:, :cw], cst["ones_row"][:1, :128],
                                                       strow2[:1, 512:512+cw], start=True, stop=True),
                        after=[('v', mr2)])
            vb1 = R.run('v', lambda v, cw=cw: v.tensor_copy(rstdb2[:, :cw], psC2[:, :cw]),
                        after=[('t', tb1)])
            tb2 = R.run('t', lambda t, cw=cw: t.matmul(psC2[:, :cw], cst["ones_row"][:1, :128],
                                                       strow2[:1, 1024:1024+cw], start=True, stop=True),
                        after=[('v', vb1)])
            R.run('v', lambda v, cw=cw: v.tensor_copy(mrb2[:, :cw], psC2[:, :cw]), after=[('t', tb2)])
            for i in range(8):
                R.run('t', lambda t, i=i, cw=cw: t.matmul(
                    psC2[:64, :cw], wres["wo1_h"][:, 64*i:64*(i+1)], xqh[:, cw*i:cw*(i+1)],
                    start=(i == 0), stop=False), after=[('v', R.mark('v'))] if i == 0 else ())
                R.run('t', lambda t, i=i, cw=cw: t.matmul(
                    psC2[:64, :cw], wres["wo1_h"][:, 64*i:64*(i+1)], xql[:, cw*i:cw*(i+1)],
                    start=False, stop=False))
                R.run('t', lambda t, i=i, cw=cw: t.matmul(
                    psC2[:64, :cw], wres["wo1_l"][:, 64*i:64*(i+1)], xqh[:, cw*i:cw*(i+1)],
                    start=False, stop=(i == 7)))
            ez = R.run('v', lambda v, cw=cw: v.tensor_tensor(ptm[:64, :cw], psC2[:64, :cw],
                                                             rstdb2[:64, :cw], ALU.mult),
                       after=[('t', R.mark('t'))])
            ez2 = R.run('v', lambda v, cw=cw, c0=c0: v.scalar_tensor_tensor(
                z1g[:, c0:c0+cw], mrb2[:64, :cw], w1col["wo1"][:64, 0:1], ptm[:64, :cw],
                ALU.mult, ALU.add))
        vgh = R.run('v', lambda v: v.tensor_copy(z1gh[:], z1g[:]))
        vgl = R.run('v', lambda v: v.tensor_sub(z1gl[:], z1g[:], z1gh[:]))
        for (c0, cw) in SL:
            R.barrier(with_q=True)
            nq = R.dma_reg(xq[:, :8*cw], ofull, SL.index((c0, cw)),
                           [[MQ4, 128], [128*MQ4, 8], [1, cw]],
                           after=[('q2', naa), ('q', ngx)])
            for i in range(8):
                tg1 = R.run('t', lambda t, i=i, cw=cw, c0=c0: t.matmul(
                    psC2[:, :cw], wres["wo2_h"][:, 128*i:128*(i+1)], z1gh[:, c0:c0+cw],
                    start=True, stop=False),
                    after=[('v', vgl), ('v', R.mark('v')), ('s', R.mark('s'))])
                R.run('t', lambda t, i=i, cw=cw, c0=c0: t.matmul(
                    psC2[:, :cw], wres["wo2_h"][:, 128*i:128*(i+1)], z1gl[:, c0:c0+cw],
                    start=False, stop=False))
                tg3 = R.run('t', lambda t, i=i, cw=cw, c0=c0: t.matmul(
                    psC2[:, :cw], wres["wo2_l"][:, 128*i:128*(i+1)], z1gh[:, c0:c0+cw],
                    start=False, stop=True))
                sg = R.run('s', lambda s, cw=cw: s.activation(gt8[:, :cw], psC2[:, :cw], AF.Sigmoid),
                           after=[('t', tg3), ('v', R.mark('v'))])
                R.run('v', lambda v, i=i, cw=cw: v.tensor_tensor(
                    xq[:, cw*i:cw*(i+1)], xq[:, cw*i:cw*(i+1)], gt8[:, :cw], ALU.mult),
                      after=[('s', sg), ('q', nq)])
            for i in range(8):
                R.run('s', lambda s, i=i, cw=cw: s.activation(sqb2[:, cw*i:cw*(i+1)],
                                                              xq[:, cw*i:cw*(i+1)], AF.Square),
                      after=[('v', R.mark('v'))] + ([('t', R.mark('t'))] if i == 0 else []))
                R.run('v', lambda v, i=i, cw=cw: v.tensor_copy(xqh[:, cw*i:cw*(i+1)],
                                                               xq[:, cw*i:cw*(i+1)]),
                      after=[('t', R.mark('t'))] if i == 0 else ())
                R.run('v', lambda v, i=i, cw=cw: v.tensor_sub(xql[:, cw*i:cw*(i+1)],
                                                              xq[:, cw*i:cw*(i+1)],
                                                              xqh[:, cw*i:cw*(i+1)]))
            for i in range(8):
                R.run('t', lambda t, i=i, cw=cw: t.matmul(psA2[:, :cw], onesb[:],
                                                          xqh[:, cw*i:cw*(i+1)],
                                                          start=(i == 0), stop=False),
                      after=[('v', R.mark('v')), ('s', R.mark('s'))])
                R.run('t', lambda t, i=i, cw=cw: t.matmul(psA2[:, :cw], onesb[:],
                                                          xql[:, cw*i:cw*(i+1)],
                                                          start=False, stop=(i == 7)))
            nmu = R.run('v', lambda v, cw=cw: v.tensor_scalar_mul(strow2[:, 0:cw], psA2[:, :cw], 1.0/D),
                  after=[('t', R.mark('t'))])
            for i in range(8):
                R.run('t', lambda t, i=i, cw=cw: t.matmul(psA2[:, :cw], onesb[:],
                                                          sqb2[:, cw*i:cw*(i+1)],
                                                          start=(i == 0), stop=(i == 7)),
                      after=[('v', nmu)] if i == 0 else ())
            R.run('v', lambda v, cw=cw: v.tensor_scalar_mul(strow2[:, 1536:1536+cw], psA2[:, :cw], 1.0/D),
                  after=[('t', R.mark('t'))])
            R.run('v', lambda v, cw=cw: v.tensor_tensor(strow2[:, 1024:1024+cw], strow2[:, 0:cw],
                                                        strow2[:, 0:cw], ALU.mult))
            R.run('v', lambda v, cw=cw: v.tensor_sub(strow2[:, 1536:1536+cw],
                                                     strow2[:, 1536:1536+cw],
                                                     strow2[:, 1024:1024+cw]))
            rs3 = R.run('s', lambda s, cw=cw: s.activation(strow2[:, 512:512+cw],
                                                           strow2[:, 1536:1536+cw],
                                                           AF.Sqrt, bias=1e-5),
                        after=[('v', R.mark('v'))])
            rs3 = R.run('v', lambda v, cw=cw: v.reciprocal(strow2[:, 512:512+cw],
                                                           strow2[:, 512:512+cw]),
                        after=[('s', rs3)])
            tb1 = R.run('t', lambda t, cw=cw: t.matmul(psC2[:, :cw], cst["ones_row"][:1, :128],
                                                       strow2[:1, 512:512+cw], start=True, stop=True),
                        after=[('v', rs3)])
            vb1 = R.run('v', lambda v, cw=cw: v.tensor_copy(rstdb2[:, :cw], psC2[:, :cw]),
                        after=[('t', tb1)])
            tb2 = R.run('t', lambda t, cw=cw: t.matmul(psC2[:, :cw], cst["ones_row"][:1, :128],
                                                       strow2[:1, 0:cw], start=True, stop=True),
                        after=[('v', vb1)])
            vb2 = R.run('v', lambda v, cw=cw: v.tensor_copy(mrb2[:, :cw], psC2[:, :cw]),
                        after=[('t', tb2)])
            for i in range(8):
                R.run('v', lambda v, i=i, cw=cw: v.tensor_sub(xln[:, cw*i:cw*(i+1)],
                                                              xq[:, cw*i:cw*(i+1)], mrb2[:, :cw]))
                R.run('v', lambda v, i=i, cw=cw: v.tensor_tensor(xln[:, cw*i:cw*(i+1)],
                                                                 xln[:, cw*i:cw*(i+1)],
                                                                 rstdb2[:, :cw], ALU.mult))
                R.run('v', lambda v, i=i, cw=cw: v.tensor_scalar_mul(xln[:, cw*i:cw*(i+1)],
                                                                     xln[:, cw*i:cw*(i+1)],
                                                                     wres["ln_w"][:, i:i+1]))
                R.run('v', lambda v, i=i, cw=cw: v.tensor_copy(xlnh[:, cw*i:cw*(i+1)],
                                                               xln[:, cw*i:cw*(i+1)]))
                R.run('v', lambda v, i=i, cw=cw: v.tensor_sub(xlnl[:, cw*i:cw*(i+1)],
                                                              xln[:, cw*i:cw*(i+1)],
                                                              xlnh[:, cw*i:cw*(i+1)]))
            for oc in range(8):
                for i in range(8):
                    R.run('t', lambda t, oc=oc, i=i, cw=cw: t.matmul(
                        psD2[:, :cw], wres["wout_h"][:, 1024*i+128*oc:1024*i+128*oc+128],
                        xlnh[:, cw*i:cw*(i+1)], start=(i == 0), stop=False),
                        after=[('v', R.mark('v'))] if i == 0 else ())
                    R.run('t', lambda t, oc=oc, i=i, cw=cw: t.matmul(
                        psD2[:, :cw], wres["wout_h"][:, 1024*i+128*oc:1024*i+128*oc+128],
                        xlnl[:, cw*i:cw*(i+1)], start=False, stop=False))
                    R.run('t', lambda t, oc=oc, i=i, cw=cw: t.matmul(
                        psD2[:, :cw], wres["wout_l"][:, 1024*i+128*oc:1024*i+128*oc+128],
                        xlnh[:, cw*i:cw*(i+1)], start=False, stop=(i == 7)))
                ev = R.run('v', lambda v, cw=cw: v.tensor_copy(ptm[:, :cw], psD2[:, :cw]),
                           after=[('t', R.mark('t')), ('q', R.n['q'])])
                R.dma(dslice(upd_scr, 128*oc, 128, c0, cw, MQ), ptm[:, :cw], after=[('v', ev)])
    R.barrier(with_q=True)
    W12.close()
    TSL = []
    c0 = 0
    while c0 < TILE:
        TSL.append((c0, min(512, TILE - c0)))
        c0 += 512
    R.run('v', lambda v: v.memset(carry_n[:], 0.0))
    R.run('v', lambda v: v.memset(carry_d[:], 0.0))
    with ExitStack() as P3:
        p3sb = lambda nme, sh, dt=F32: P3.enter_context(nc.sbuf_tensor(nme, sh, dt))
        xt = p3sb("xt", [128, 8*TILE])
        xnh = p3sb("xnh", [128, 8*TILE], BF16)
        xnl = p3sb("xnl", [128, 8*TILE], BF16)
        sqb3 = p3sb("sqb3", [128, 8*512], BF16)
        ut = p3sb("ut", [128, 8*256])
        hbh = p3sb("hbh", [128, 12*512], BF16)
        hbl = p3sb("hbl", [128, 12*512], BF16)
        dacc = p3sb("dacc", [128, 8*512])
        wstr = p3sb("wstr", [128, 1024], BF16)
        wstr2 = p3sb("wstr2", [128, 1024], BF16)
        wdstr = p3sb("wdstr", [128, 3072], BF16)
        ptm3 = p3sb("ptm3", [128, 512])
        ptm3b = p3sb("ptm3b", [128, 512])
        strow3 = p3sb("strow3", [1, 2048])
        rstdb3 = p3sb("rstdb3", [128, 512])
        mub3 = p3sb("mub3", [128, 512])
        wst1 = p3sb("wst1", [128, 512])
        kst1 = p3sb("kst1", [128, 512])
        vst1 = p3sb("vst1", [128, 512])
        psA3 = P3.enter_context(nc.psum_tensor("p3psA", [1, 512], F32))
        psC3 = P3.enter_context(nc.psum_tensor("p3psC", [128, 512], F32))
        psD3 = P3.enter_context(nc.psum_tensor("p3psD", [128, 512], F32))

        def stats3(srcbuf, c0, cw, with_sq_split=True):
            for i in range(8):
                R.run('s', lambda s, i=i, c0=c0, cw=cw: s.activation(
                    sqb3[:, 512*i:512*i+cw], srcbuf[:, TILE*i+c0:TILE*i+c0+cw], AF.Square),
                      after=[('v', R.mark('v'))])
            for i in range(8):
                R.run('t', lambda t, i=i, c0=c0, cw=cw: t.matmul(
                    psA3[:, :cw], onesb[:], xnh[:, TILE*i+c0:TILE*i+c0+cw],
                    start=(i == 0), stop=False),
                      after=[('v', R.mark('v')), ('s', R.mark('s'))])
                R.run('t', lambda t, i=i, c0=c0, cw=cw: t.matmul(
                    psA3[:, :cw], onesb[:], xnl[:, TILE*i+c0:TILE*i+c0+cw],
                    start=False, stop=(i == 7)))
            nmu = R.run('v', lambda v, cw=cw: v.tensor_scalar_mul(strow3[:, 0:cw], psA3[:, :cw], 1.0/D),
                  after=[('t', R.mark('t'))])
            for i in range(8):
                R.run('t', lambda t, i=i, cw=cw: t.matmul(
                    psA3[:, :cw], onesb[:], sqb3[:, 512*i:512*i+cw],
                    start=(i == 0), stop=(i == 7)), after=[('v', nmu)] if i == 0 else ())
            R.run('v', lambda v, cw=cw: v.tensor_scalar_mul(strow3[:, 1536:1536+cw], psA3[:, :cw], 1.0/D),
                  after=[('t', R.mark('t'))])
            R.run('v', lambda v, cw=cw: v.tensor_tensor(strow3[:, 1024:1024+cw], strow3[:, 0:cw],
                                                        strow3[:, 0:cw], ALU.mult))
            R.run('v', lambda v, cw=cw: v.tensor_sub(strow3[:, 1536:1536+cw],
                                                     strow3[:, 1536:1536+cw],
                                                     strow3[:, 1024:1024+cw]))
            rs = R.run('s', lambda s, cw=cw: s.activation(strow3[:, 512:512+cw],
                                                          strow3[:, 1536:1536+cw],
                                                          AF.Sqrt, bias=1e-5),
                       after=[('v', R.mark('v'))])
            rs = R.run('v', lambda v, cw=cw: v.reciprocal(strow3[:, 512:512+cw],
                                                          strow3[:, 512:512+cw]),
                       after=[('s', rs)])
            t1 = R.run('t', lambda t, cw=cw: t.matmul(psC3[:, :cw], cst["ones_row"][:1, :128],
                                                      strow3[:1, 512:512+cw], start=True, stop=True),
                       after=[('v', rs)])
            v1 = R.run('v', lambda v, cw=cw: v.tensor_copy(rstdb3[:, :cw], psC3[:, :cw]),
                       after=[('t', t1)])
            t2 = R.run('t', lambda t, cw=cw: t.matmul(psC3[:, :cw], cst["ones_row"][:1, :128],
                                                      strow3[:1, 0:cw], start=True, stop=True),
                       after=[('v', v1)])
            R.run('v', lambda v, cw=cw: v.tensor_copy(mub3[:, :cw], psC3[:, :cw]), after=[('t', t2)])

        def split_xn(c0, cw, from_norm=False):
            for i in range(8):
                if from_norm:
                    R.run('v', lambda v, i=i, c0=c0, cw=cw: v.tensor_sub(
                        ptm3[:, :cw], xt[:, TILE*i+c0:TILE*i+c0+cw], mub3[:, :cw]))
                    R.run('v', lambda v, i=i, c0=c0, cw=cw: v.tensor_tensor(
                        ptm3[:, :cw], ptm3[:, :cw], rstdb3[:, :cw], ALU.mult))
                    src = ptm3
                    R.run('v', lambda v, i=i, c0=c0, cw=cw, src=src: v.tensor_copy(
                        xnh[:, TILE*i+c0:TILE*i+c0+cw], src[:, :cw]))
                    R.run('v', lambda v, i=i, c0=c0, cw=cw, src=src: v.tensor_sub(
                        xnl[:, TILE*i+c0:TILE*i+c0+cw], src[:, :cw],
                        xnh[:, TILE*i+c0:TILE*i+c0+cw]))
                else:
                    R.run('v', lambda v, i=i, c0=c0, cw=cw: v.tensor_copy(
                        xnh[:, TILE*i+c0:TILE*i+c0+cw], xt[:, TILE*i+c0:TILE*i+c0+cw]))
                    R.run('v', lambda v, i=i, c0=c0, cw=cw: v.tensor_sub(
                        xnl[:, TILE*i+c0:TILE*i+c0+cw], xt[:, TILE*i+c0:TILE*i+c0+cw],
                        xnh[:, TILE*i+c0:TILE*i+c0+cw]))

        for ti in range(NT):
            tc0 = TILE*ti
            R.barrier(with_q=True)
            nq = R.dma(xt[:], dslab(xq_fm, 0, 8, tc0, TILE, QP))
            mfirst = (tc0 + 2)//3
            mlast = (tc0 + TILE - 1)//3
            cnt = mlast - mfirst + 1
            nu = R.dma(ut[:, :8*cnt], dslab(upd_scr, 0, 8, mfirst, cnt, MQ))
            off = 3*mfirst - tc0
            for i in range(8):
                R.run('v', lambda v, i=i, off=off, cnt=cnt: v.tensor_add(
                    bass.AP(xt, TILE*i + off, [[8*TILE, 128], [3, cnt]]),
                    bass.AP(xt, TILE*i + off, [[8*TILE, 128], [3, cnt]]),
                    ut[:, cnt*i:cnt*(i+1)]),
                      after=[('q', max(nq, nu))] if i == 0 else ())
            if x1_dbg is not None:
                R.dma(dslab(x1_dbg, 0, 8, tc0, TILE, QP), xt[:], after=[('v', R.mark('v'))])
            for (c0, cw) in TSL:
                stats_done = False
                split_xn(c0, cw, from_norm=False)
                stats3(xt, c0, cw)
                split_xn(c0, cw, from_norm=True)
                if xn_dbg is not None:
                    R.dma(dslab(xn_dbg, 0, 8, tc0+c0, cw, QP),
                          bass.AP(xnh, c0, [[8*TILE, 128], [TILE, 8], [1, cw]]),
                          after=[('v', R.mark('v'))])
                for hf in range(2):
                    for hj12 in range(12):
                        hj = 12*hf + hj12
                        nw = R.dma(wstr[:], dslab(wbf["ffn_g"], 0, 8, 128*hj, 128, 3072),
                                   after=[('t', R.mark('t'))])
                        nw2 = R.dma(wstr2[:], dslab(wbf["ffn_u"], 0, 8, 128*hj, 128, 3072))
                        for i in range(8):
                            R.run('t', lambda t, i=i, c0=c0, cw=cw: t.matmul(
                                psC3[:, :cw], wstr[:, 128*i:128*(i+1)],
                                xnh[:, TILE*i+c0:TILE*i+c0+cw], start=(i == 0), stop=False),
                                after=[('q', nw2), ('v', R.mark('v'))] if i == 0 else ())
                            R.run('t', lambda t, i=i, c0=c0, cw=cw: t.matmul(
                                psC3[:, :cw], wstr[:, 128*i:128*(i+1)],
                                xnl[:, TILE*i+c0:TILE*i+c0+cw], start=False, stop=(i == 7)))
                        for i in range(8):
                            R.run('t', lambda t, i=i, c0=c0, cw=cw: t.matmul(
                                psD3[:, :cw], wstr2[:, 128*i:128*(i+1)],
                                xnh[:, TILE*i+c0:TILE*i+c0+cw], start=(i == 0), stop=False))
                            R.run('t', lambda t, i=i, c0=c0, cw=cw: t.matmul(
                                psD3[:, :cw], wstr2[:, 128*i:128*(i+1)],
                                xnl[:, TILE*i+c0:TILE*i+c0+cw], start=False, stop=(i == 7)))
                        sg = R.run('s', lambda s, cw=cw: s.activation(ptm3[:, :cw], psC3[:, :cw],
                                                                      AF.Silu, scale=1.0/127.0),
                                   after=[('t', R.mark('t')), ('v', R.mark('v'))])
                        hv = R.run('v', lambda v, cw=cw: v.scalar_tensor_tensor(
                            ptm3b[:, :cw], psD3[:, :cw], 1.0/127.0, ptm3[:, :cw], ALU.mult, ALU.mult),
                                   after=[('s', sg)])
                        R.run('v', lambda v, hj12=hj12, cw=cw: v.tensor_copy(
                            hbh[:, 512*hj12:512*hj12+cw], ptm3b[:, :cw]))
                        R.run('v', lambda v, hj12=hj12, cw=cw: v.tensor_sub(
                            hbl[:, 512*hj12:512*hj12+cw], ptm3b[:, :cw],
                            hbh[:, 512*hj12:512*hj12+cw]))
                    for oc in range(8):
                        nw = R.dma(wdstr[:, :12*128],
                                   dslab(wbf["ffn_d"], 1536*hf, 12, 128*oc, 128, 1024),
                                   after=[('t', R.mark('t'))])
                        for hj12 in range(12):
                            R.run('t', lambda t, hj12=hj12, oc=oc, cw=cw: t.matmul(
                                psC3[:, :cw], wdstr[:, 128*hj12:128*(hj12+1)],
                                hbh[:, 512*hj12:512*hj12+cw], start=(hj12 == 0), stop=False),
                                after=[('q', nw), ('v', R.mark('v'))] if hj12 == 0 else ())
                            R.run('t', lambda t, hj12=hj12, oc=oc, cw=cw: t.matmul(
                                psC3[:, :cw], wdstr[:, 128*hj12:128*(hj12+1)],
                                hbl[:, 512*hj12:512*hj12+cw], start=False, stop=(hj12 == 11)))
                        if hf == 0:
                            R.run('v', lambda v, oc=oc, cw=cw: v.tensor_copy(
                                dacc[:, 512*oc:512*oc+cw], psC3[:, :cw]), after=[('t', R.mark('t'))])
                        else:
                            R.run('v', lambda v, oc=oc, cw=cw: v.tensor_add(
                                ptm3[:, :cw], dacc[:, 512*oc:512*oc+cw], psC3[:, :cw]),
                                  after=[('t', R.mark('t'))])
                            R.run('v', lambda v, oc=oc, c0=c0, cw=cw: v.scalar_tensor_tensor(
                                xt[:, TILE*oc+c0:TILE*oc+c0+cw], ptm3[:, :cw], 1.0/127.0,
                                xt[:, TILE*oc+c0:TILE*oc+c0+cw], ALU.mult, ALU.add))
            nst = R.dma(dslab(x2_scr, 0, 8, tc0, TILE, QP), xt[:], after=[('v', R.mark('v'))])
            for (c0, cw) in TSL:
                split_xn(c0, cw, from_norm=False)
                stats3(xt, c0, cw)
                si = R.run('s', lambda s, cw=cw: s.activation(strow3[:, 1024:1024+cw],
                                                              strow3[:, 1536:1536+cw],
                                                              AF.Sqrt, bias=1e-5),
                           after=[('v', R.mark('v'))])
                mr3 = R.run('v', lambda v, cw=cw: v.tensor_tensor(strow3[:, 1536:1536+cw],
                                                                  strow3[:, 0:cw],
                                                                  strow3[:, 512:512+cw], ALU.mult),
                            after=[('s', si)])
                tb = R.run('t', lambda t, cw=cw: t.matmul(psC3[:, :cw], cst["ones_row"][:1, :128],
                                                          strow3[:1, 1536:1536+cw],
                                                          start=True, stop=True), after=[('v', mr3)])
                vmr = R.run('v', lambda v, cw=cw: v.tensor_copy(mub3[:, :cw], psC3[:, :cw]),
                            after=[('t', tb)])
                tb2 = R.run('t', lambda t, cw=cw: t.matmul(psC3[:, :cw], cst["ones_row"][:1, :128],
                                                           strow3[:1, 1024:1024+cw],
                                                           start=True, stop=True),
                            after=[('v', vmr)])
                vir = R.run('v', lambda v, cw=cw: v.tensor_copy(ptm3b[:, :cw], psC3[:, :cw]),
                            after=[('t', tb2)])
                for oc in range(8):
                    for wn, dstb in [("wr_t", None), ("ww_t", wst1), ("wkb_t", kst1),
                                     ("wvb_t", vst1)]:
                        nw = R.dma(wstr[:], dslab(wbf[wn+"_h"], 0, 8, 128*oc, 128, D),
                                   after=[('t', R.mark('t'))])
                        nw2 = R.dma(wstr2[:], dslab(wbf[wn+"_l"], 0, 8, 128*oc, 128, D))
                        for i in range(8):
                            R.run('t', lambda t, i=i, c0=c0, cw=cw: t.matmul(
                                psC3[:, :cw], wstr[:, 128*i:128*(i+1)],
                                xnh[:, TILE*i+c0:TILE*i+c0+cw], start=(i == 0), stop=False),
                                after=[('q', nw2), ('v', R.mark('v'))] if i == 0 else ())
                            R.run('t', lambda t, i=i, c0=c0, cw=cw: t.matmul(
                                psC3[:, :cw], wstr[:, 128*i:128*(i+1)],
                                xnl[:, TILE*i+c0:TILE*i+c0+cw], start=False, stop=False))
                            R.run('t', lambda t, i=i, c0=c0, cw=cw: t.matmul(
                                psC3[:, :cw], wstr2[:, 128*i:128*(i+1)],
                                xnh[:, TILE*i+c0:TILE*i+c0+cw], start=False, stop=(i == 7)))
                        e1 = R.run('v', lambda v, cw=cw: v.tensor_tensor(
                            ptm3[:, :cw], psC3[:, :cw], rstdb3[:, :cw], ALU.mult),
                                   after=[('t', R.mark('t')), ('s', R.mark('s'))])
                        e2 = R.run('v', lambda v, cw=cw, wn=wn, oc=oc: v.scalar_tensor_tensor(
                            ptm3[:, :cw], mub3[:, :cw], w1col[wn][:, oc:oc+1], ptm3[:, :cw],
                            ALU.mult, ALU.add))
                        e3 = R.run('v', lambda v, cw=cw, wn=wn, oc=oc: v.tensor_scalar_add(
                            ptm3[:, :cw], ptm3[:, :cw], w1col[wn+"_b"][:, oc:oc+1]))
                        if wn == "wr_t":
                            sgr = R.run('s', lambda s, cw=cw: s.activation(
                                ptm3[:, :cw], ptm3[:, :cw], AF.Sigmoid), after=[('v', e3)])
                            R.dma(dslice(r_scr, 128*oc, 128, tc0 + c0, cw, QP), ptm3[:, :cw],
                                  after=[('s', sgr)])
                        else:
                            R.run('v', lambda v, cw=cw, dstb=dstb: v.tensor_copy(
                                dstb[:, :cw], ptm3[:, :cw]), after=[('v', e3), ('q', R.n['q'])])
                    s1 = R.run('s', lambda s, cw=cw: s.activation(ptm3[:, :cw], wst1[:, :cw],
                                                                  AF.Exp),
                               after=[('v', R.mark('v')), ('q', R.n['q'])])
                    v1 = R.run('v', lambda v, cw=cw: v.tensor_sub(ptm3[:, :cw], kst1[:, :cw],
                                                                  ptm3[:, :cw]), after=[('s', s1)])
                    s2 = R.run('s', lambda s, cw=cw: s.activation(ptm3[:, :cw], ptm3[:, :cw],
                                                                  AF.Exp), after=[('v', v1)])
                    v2 = R.run('v', lambda v, cw=cw: v.tensor_tensor(ptm3b[:, :cw], ptm3[:, :cw],
                                                                     vst1[:, :cw], ALU.mult),
                               after=[('s', s2)])
                    vn = R.run('v', lambda v, oc=oc, cw=cw: v.tensor_tensor_scan(
                        wst1[:, :cw], dcol[:, oc:oc+1].broadcast_to((128, cw)),
                        ptm3b[:, :cw], carry_n[:, oc:oc+1], ALU.mult, ALU.add))
                    vd = R.run('v', lambda v, oc=oc, cw=cw: v.tensor_tensor_scan(
                        kst1[:, :cw], dcol[:, oc:oc+1].broadcast_to((128, cw)),
                        ptm3[:, :cw], carry_d[:, oc:oc+1], ALU.mult, ALU.add))
                    R.run('v', lambda v, oc=oc, cw=cw: v.tensor_copy(
                        carry_n[:, oc:oc+1], wst1[:, cw-1:cw]))
                    R.run('v', lambda v, oc=oc, cw=cw: v.tensor_copy(
                        carry_d[:, oc:oc+1], kst1[:, cw-1:cw]))
                    R.dma(dslice(nd_scr, 128*oc, 128, tc0 + c0, cw, QP), wst1[:, :cw],
                          after=[('v', R.mark('v'))])
                    R.dma(dslice(nd_scr, 1024 + 128*oc, 128, tc0 + c0, cw, QP), kst1[:, :cw])
    R.barrier(with_q=True)
    ncb1 = R.dma(bass.AP(cbuf, 0, [[1, 128], [128, 8]]), carry_n[:], after=[('v', R.mark('v'))], slow=True)
    ncb2 = R.dma(bass.AP(cbuf, 1024, [[1, 128], [128, 8]]), carry_d[:], slow=True)
    R.run('q2', lambda h: h.collective_compute(
        "AllGather", ALU.bypass, replica_groups=[[0, 1, 2, 3], [4, 5, 6, 7]],
        ins=[cbuf.ap().opt()], outs=[cfull.ap().opt()]), after=[('q', R.n['q'])])
    ncc2 = R.n['q2']
    nhl = R.dma(hall[:], bass.AP(cfull, 0, [[1, 128], [2048, 4], [1024, 2], [128, 8]]),
                after=[('q2', ncc2)], slow=True)
    R.run('v', lambda v: v.tensor_scalar_add(dqm1[:], d4096[:], -1.0), after=[('q', nhl)])
    R.run('v', lambda v: v.memset(hcol[:], 0.0))
    for r in range(3):
        R.run('v', lambda v, r=r: v.tensor_scalar_mul(dmr[:], dqm1[:], cmk[:, r:r+1]))
        R.run('v', lambda v: v.tensor_scalar_add(dmr[:], dmr[:], 1.0))
        for t in range(2):
            R.run('v', lambda v, r=r, t=t: v.tensor_tensor(hcol[:, 8*t:8*(t+1)],
                                                           hcol[:, 8*t:8*(t+1)], dmr[:], ALU.mult))
            R.run('v', lambda v, r=r, t=t: v.scalar_tensor_tensor(
                hcol[:, 8*t:8*(t+1)], hall[:, 16*r+8*t:16*r+8*(t+1)], cmk[:, r:r+1],
                hcol[:, 8*t:8*(t+1)], ALU.mult, ALU.add))
    R.run('v', lambda v: v.tensor_copy(hbase[:], hcol[:]))
    R.barrier(with_q=True)

    with ExitStack() as P6:
        p6sb = lambda nme, sh, dt=F32: P6.enter_context(nc.sbuf_tensor(nme, sh, dt))
        x3t = p6sb("x3t", [128, 8*TILE])
        xnh6 = p6sb("xnh6", [128, 8*TILE], BF16)
        xnl6 = p6sb("xnl6", [128, 8*TILE], BF16)
        sqb6 = p6sb("sqb6", [128, 8*512], BF16)
        wstr6 = p6sb("wstr6", [128, 4096], BF16)
        wstr6b = p6sb("wstr6b", [128, 4096], BF16)
        ptm6 = p6sb("ptm6", [128, 512])
        ptm6b = p6sb("ptm6b", [128, 512])
        strow6 = p6sb("strow6", [1, 2048])
        rstdb6 = p6sb("rstdb6", [128, 512])
        mub6 = p6sb("mub6", [128, 512])
        psA6 = P6.enter_context(nc.psum_tensor("p6psA", [1, 512], F32))
        psC6 = P6.enter_context(nc.psum_tensor("p6psC", [128, 512], F32))
        psD6 = P6.enter_context(nc.psum_tensor("p6psD", [128, 512], F32))

        def stats6(c0, cw, src):
            for i in range(8):
                R.run('s', lambda s, i=i, c0=c0, cw=cw, src=src: s.activation(
                    sqb6[:, 512*i:512*i+cw], src[:, TILE*i+c0:TILE*i+c0+cw], AF.Square),
                      after=[('v', R.mark('v'))])
            for i in range(8):
                R.run('t', lambda t, i=i, c0=c0, cw=cw: t.matmul(
                    psA6[:, :cw], onesb[:], xnh6[:, TILE*i+c0:TILE*i+c0+cw],
                    start=(i == 0), stop=False),
                      after=[('v', R.mark('v')), ('s', R.mark('s'))])
                R.run('t', lambda t, i=i, c0=c0, cw=cw: t.matmul(
                    psA6[:, :cw], onesb[:], xnl6[:, TILE*i+c0:TILE*i+c0+cw],
                    start=False, stop=(i == 7)))
            nmu = R.run('v', lambda v, cw=cw: v.tensor_scalar_mul(strow6[:, 0:cw], psA6[:, :cw], 1.0/D),
                  after=[('t', R.mark('t'))])
            for i in range(8):
                R.run('t', lambda t, i=i, cw=cw: t.matmul(
                    psA6[:, :cw], onesb[:], sqb6[:, 512*i:512*i+cw],
                    start=(i == 0), stop=(i == 7)), after=[('v', nmu)] if i == 0 else ())
            R.run('v', lambda v, cw=cw: v.tensor_scalar_mul(strow6[:, 1536:1536+cw], psA6[:, :cw], 1.0/D),
                  after=[('t', R.mark('t'))])
            R.run('v', lambda v, cw=cw: v.tensor_tensor(strow6[:, 1024:1024+cw], strow6[:, 0:cw],
                                                        strow6[:, 0:cw], ALU.mult))
            R.run('v', lambda v, cw=cw: v.tensor_sub(strow6[:, 1536:1536+cw],
                                                     strow6[:, 1536:1536+cw],
                                                     strow6[:, 1024:1024+cw]))
            rs = R.run('s', lambda s, cw=cw: s.activation(strow6[:, 512:512+cw],
                                                          strow6[:, 1536:1536+cw],
                                                          AF.Sqrt, bias=1e-5),
                       after=[('v', R.mark('v'))])
            rs = R.run('v', lambda v, cw=cw: v.reciprocal(strow6[:, 512:512+cw],
                                                          strow6[:, 512:512+cw]),
                       after=[('s', rs)])
            si = R.run('s', lambda s, cw=cw: s.activation(strow6[:, 1024:1024+cw],
                                                          strow6[:, 1536:1536+cw],
                                                          AF.Sqrt, bias=1e-5))
            mr6 = R.run('v', lambda v, cw=cw: v.tensor_tensor(strow6[:, 1536:1536+cw],
                                                              strow6[:, 0:cw],
                                                              strow6[:, 512:512+cw], ALU.mult),
                        after=[('s', si)])
            t1 = R.run('t', lambda t, cw=cw: t.matmul(psC6[:, :cw], cst["ones_row"][:1, :128],
                                                      strow6[:1, 512:512+cw], start=True, stop=True),
                       after=[('v', mr6)])
            v1 = R.run('v', lambda v, cw=cw: v.tensor_copy(rstdb6[:, :cw], psC6[:, :cw]),
                       after=[('t', t1)])
            t2 = R.run('t', lambda t, cw=cw: t.matmul(psC6[:, :cw], cst["ones_row"][:1, :128],
                                                      strow6[:1, 1536:1536+cw], start=True, stop=True),
                       after=[('v', v1)])
            v2 = R.run('v', lambda v, cw=cw: v.tensor_copy(mub6[:, :cw], psC6[:, :cw]),
                       after=[('t', t2)])
            t3 = R.run('t', lambda t, cw=cw: t.matmul(psC6[:, :cw], cst["ones_row"][:1, :128],
                                                      strow6[:1, 1024:1024+cw], start=True, stop=True),
                       after=[('v', v2)])
            R.run('v', lambda v, cw=cw: v.tensor_copy(ptm6b[:, :cw], psC6[:, :cw]),
                  after=[('t', t3)])

        def split6(c0, cw, src):
            for i in range(8):
                R.run('v', lambda v, i=i, c0=c0, cw=cw, src=src: v.tensor_copy(
                    xnh6[:, TILE*i+c0:TILE*i+c0+cw], src[:, TILE*i+c0:TILE*i+c0+cw]),
                      after=[('t', R.mark('t'))] if i == 0 else ())
                R.run('v', lambda v, i=i, c0=c0, cw=cw, src=src: v.tensor_sub(
                    xnl6[:, TILE*i+c0:TILE*i+c0+cw], src[:, TILE*i+c0:TILE*i+c0+cw],
                    xnh6[:, TILE*i+c0:TILE*i+c0+cw]))

        def proj_folded6(wn, oc, c0, cw, with_beta=True):
            nw = R.dma(wstr6[:, :1024], dslab(wbf[wn+"_h"], 0, 8, 128*oc, 128, D),
                       after=[('t', R.mark('t'))])
            nw2 = R.dma(wstr6b[:, :1024], dslab(wbf[wn+"_l"], 0, 8, 128*oc, 128, D))
            for i in range(8):
                R.run('t', lambda t, i=i, c0=c0, cw=cw: t.matmul(
                    psC6[:, :cw], wstr6[:, 128*i:128*(i+1)],
                    xnh6[:, TILE*i+c0:TILE*i+c0+cw], start=(i == 0), stop=False),
                    after=[('q', nw2), ('v', R.mark('v'))] if i == 0 else ())
                R.run('t', lambda t, i=i, c0=c0, cw=cw: t.matmul(
                    psC6[:, :cw], wstr6[:, 128*i:128*(i+1)],
                    xnl6[:, TILE*i+c0:TILE*i+c0+cw], start=False, stop=False))
                R.run('t', lambda t, i=i, c0=c0, cw=cw: t.matmul(
                    psC6[:, :cw], wstr6b[:, 128*i:128*(i+1)],
                    xnh6[:, TILE*i+c0:TILE*i+c0+cw], start=False, stop=(i == 7)))
            e1 = R.run('v', lambda v, cw=cw: v.tensor_tensor(ptm6[:, :cw], psC6[:, :cw],
                                                             rstdb6[:, :cw], ALU.mult),
                       after=[('t', R.mark('t')), ('s', R.mark('s'))])
            R.run('v', lambda v, cw=cw, wn=wn, oc=oc: v.scalar_tensor_tensor(
                ptm6[:, :cw], mub6[:, :cw], w1col[wn][:, oc:oc+1], ptm6[:, :cw],
                ALU.mult, ALU.add))
            if with_beta:
                R.run('v', lambda v, cw=cw, wn=wn, oc=oc: v.tensor_scalar_add(
                    ptm6[:, :cw], ptm6[:, :cw], w1col[wn+"_b"][:, oc:oc+1]))
            return R.mark('v')

        for ti in range(NT):
            tc0 = TILE*ti
            R.barrier(with_q=True)
            with ExitStack() as S1:
                s1sb = lambda nme, sh, dt=F32, ti=ti: S1.enter_context(nc.sbuf_tensor(f"{nme}_{ti}", sh, dt))
                ndn = s1sb("ndn", [128, 8*TILE])
                ndd = s1sb("ndd", [128, 8*TILE])
                rt = s1sb("rt", [128, 8*TILE])
                rwh = s1sb("rwh", [128, 8*TILE], BF16)
                rwl = s1sb("rwl", [128, 8*TILE], BF16)
                n1 = R.dma(ndn[:], dslab(nd_scr, 0, 8, tc0, TILE, QP))
                n2 = R.dma(ndd[:], dslab(nd_scr, 1024, 8, tc0, TILE, QP))
                n3 = R.dma(rt[:], dslab(r_scr, 0, 8, tc0, TILE, QP))
                n4 = R.dma(x3t[:], dslab(x2_scr, 0, 8, tc0, TILE, QP))
                for i in range(8):
                    vn = R.run('v', lambda v, i=i, ndn=ndn: v.scalar_tensor_tensor(
                        ndn[:, TILE*i:TILE*(i+1)], tpow[:, TILE*i:TILE*(i+1)],
                        hbase[:, i:i+1], ndn[:, TILE*i:TILE*(i+1)], ALU.mult, ALU.add),
                               after=[('q', n2)] if i == 0 else ())
                    vd = R.run('v', lambda v, i=i, ndd=ndd: v.scalar_tensor_tensor(
                        ndd[:, TILE*i:TILE*(i+1)], tpow[:, TILE*i:TILE*(i+1)],
                        hbase[:, 8+i:9+i], ndd[:, TILE*i:TILE*(i+1)], ALU.mult, ALU.add))
                    sr = R.run('v', lambda v, i=i, ndd=ndd: v.tensor_scalar_add(
                        ndd[:, TILE*i:TILE*(i+1)], ndd[:, TILE*i:TILE*(i+1)], 1e-8),
                               after=[('v', vd)])
                    sr = R.run('v', lambda v, i=i, ndd=ndd: v.reciprocal(
                        ndd[:, TILE*i:TILE*(i+1)], ndd[:, TILE*i:TILE*(i+1)]))
                    v1 = R.run('v', lambda v, i=i, ndn=ndn, ndd=ndd: v.tensor_tensor(
                        ndn[:, TILE*i:TILE*(i+1)], ndn[:, TILE*i:TILE*(i+1)],
                        ndd[:, TILE*i:TILE*(i+1)], ALU.mult), after=[('q', n3)])
                    v2 = R.run('v', lambda v, i=i, ndn=ndn, rt=rt: v.tensor_tensor(
                        ndn[:, TILE*i:TILE*(i+1)], ndn[:, TILE*i:TILE*(i+1)],
                        rt[:, TILE*i:TILE*(i+1)], ALU.mult))
                    R.run('v', lambda v, i=i, rwh=rwh, ndn=ndn: v.tensor_copy(
                        rwh[:, TILE*i:TILE*(i+1)], ndn[:, TILE*i:TILE*(i+1)]))
                    R.run('v', lambda v, i=i, rwl=rwl, ndn=ndn, rwh=rwh: v.tensor_sub(
                        rwl[:, TILE*i:TILE*(i+1)], ndn[:, TILE*i:TILE*(i+1)],
                        rwh[:, TILE*i:TILE*(i+1)]))
                R.run('v', lambda v: v.tensor_tensor(hbase[:, :8], hbase[:, :8], d512t[:], ALU.mult))
                R.run('v', lambda v: v.tensor_tensor(hbase[:, 8:], hbase[:, 8:], d512t[:], ALU.mult))
                for (c0, cw) in TSL:
                    for oc in range(8):
                        nw = R.dma(wstr6[:, :1024], dslab(wbf["wo_t_h"], 0, 8, 128*oc, 128, D),
                                   after=[('t', R.mark('t'))])
                        nw2 = R.dma(wstr6b[:, :1024], dslab(wbf["wo_t_l"], 0, 8, 128*oc, 128, D))
                        for i in range(8):
                            R.run('t', lambda t, i=i, c0=c0, cw=cw, rwh=rwh: t.matmul(
                                psC6[:, :cw], wstr6[:, 128*i:128*(i+1)],
                                rwh[:, TILE*i+c0:TILE*i+c0+cw], start=(i == 0), stop=False),
                                after=[('q', nw2), ('v', R.mark('v'))] if i == 0 else ())
                            R.run('t', lambda t, i=i, c0=c0, cw=cw, rwl=rwl: t.matmul(
                                psC6[:, :cw], wstr6[:, 128*i:128*(i+1)],
                                rwl[:, TILE*i+c0:TILE*i+c0+cw], start=False, stop=False))
                            R.run('t', lambda t, i=i, c0=c0, cw=cw, rwh=rwh: t.matmul(
                                psC6[:, :cw], wstr6b[:, 128*i:128*(i+1)],
                                rwh[:, TILE*i+c0:TILE*i+c0+cw], start=False, stop=(i == 7)))
                        R.run('v', lambda v, oc=oc, c0=c0, cw=cw: v.tensor_add(
                            x3t[:, TILE*oc+c0:TILE*oc+c0+cw], x3t[:, TILE*oc+c0:TILE*oc+c0+cw],
                            psC6[:, :cw]), after=[('t', R.mark('t')), ('q', n4)])
            S2 = ExitStack()
            kqh = S2.enter_context(nc.sbuf_tensor(f"kqh_{ti}", [128, 32*512], BF16))
            kql = S2.enter_context(nc.sbuf_tensor(f"kql_{ti}", [128, 32*512], BF16))
            rrst = S2.enter_context(nc.sbuf_tensor(f"rrst_{ti}", [128, 8*TILE], F32))
            for (c0, cw) in TSL:
                split6(c0, cw, x3t)
                stats6(c0, cw, x3t)
                for oc in range(8):
                    pv = proj_folded6("wrec_t", oc, c0, cw)
                    R.run('s', lambda s, oc=oc, c0=c0, cw=cw, rrst=rrst: s.activation(
                        rrst[:, TILE*oc+c0:TILE*oc+c0+cw], ptm6[:, :cw], AF.Sigmoid),
                          after=[('v', pv)])
                for hj in range(32):
                    nw = R.dma(wstr6[:, :1024], dslab(wbf["wkey_t_h"], 0, 8, 128*hj, 128, 4096),
                               after=[('t', R.mark('t'))])
                    nw2 = R.dma(wstr6b[:, :1024], dslab(wbf["wkey_t_l"], 0, 8, 128*hj, 128, 4096))
                    for i in range(8):
                        R.run('t', lambda t, i=i, c0=c0, cw=cw: t.matmul(
                            psC6[:, :cw], wstr6[:, 128*i:128*(i+1)],
                            xnh6[:, TILE*i+c0:TILE*i+c0+cw], start=(i == 0), stop=False),
                            after=[('q', nw2), ('v', R.mark('v'))] if i == 0 else ())
                        R.run('t', lambda t, i=i, c0=c0, cw=cw: t.matmul(
                            psC6[:, :cw], wstr6[:, 128*i:128*(i+1)],
                            xnl6[:, TILE*i+c0:TILE*i+c0+cw], start=False, stop=False))
                        R.run('t', lambda t, i=i, c0=c0, cw=cw: t.matmul(
                            psC6[:, :cw], wstr6b[:, 128*i:128*(i+1)],
                            xnh6[:, TILE*i+c0:TILE*i+c0+cw], start=False, stop=(i == 7)))
                    e1 = R.run('v', lambda v, cw=cw: v.tensor_tensor(ptm6[:, :cw], psC6[:, :cw],
                                                                     rstdb6[:, :cw], ALU.mult),
                               after=[('t', R.mark('t')), ('s', R.mark('s'))])
                    R.run('v', lambda v, cw=cw, hj=hj: v.scalar_tensor_tensor(
                        ptm6[:, :cw], mub6[:, :cw], w1col["wkey_t"][:, hj:hj+1], ptm6[:, :cw],
                        ALU.mult, ALU.add))
                    e3 = R.run('v', lambda v, cw=cw, hj=hj: v.tensor_scalar_add(
                        ptm6[:, :cw], ptm6[:, :cw], w1col["wkey_t_b"][:, hj:hj+1]))
                    sr = R.run('s', lambda s, cw=cw: s.activation(ptm6[:, :cw], ptm6[:, :cw],
                                                                  AF.Relu), after=[('v', e3)])
                    sq = R.run('s', lambda s, cw=cw: s.activation(ptm6[:, :cw], ptm6[:, :cw],
                                                                  AF.Square))
                    R.run('v', lambda v, hj=hj, cw=cw, kqh=kqh: v.tensor_copy(
                        kqh[:, 512*hj:512*hj+cw], ptm6[:, :cw]), after=[('s', sq)])
                    R.run('v', lambda v, hj=hj, cw=cw, kqh=kqh, kql=kql: v.tensor_sub(
                        kql[:, 512*hj:512*hj+cw], ptm6[:, :cw], kqh[:, 512*hj:512*hj+cw]))
                for oc in range(8):
                    nw = R.dma(wstr6[:], dslab(wbf["wval_t_h"], 0, 32, 128*oc, 128, D),
                               after=[('t', R.mark('t'))])
                    nw2 = R.dma(wstr6b[:], dslab(wbf["wval_t_l"], 0, 32, 128*oc, 128, D))
                    for hj in range(32):
                        R.run('t', lambda t, hj=hj, cw=cw, kqh=kqh: t.matmul(
                            psC6[:, :cw], wstr6[:, 128*hj:128*(hj+1)], kqh[:, 512*hj:512*hj+cw],
                            start=(hj == 0), stop=False),
                            after=[('q', nw2), ('v', R.mark('v')),
                                   ('s', R.mark('s'))] if hj == 0 else ())
                        R.run('t', lambda t, hj=hj, cw=cw, kql=kql: t.matmul(
                            psC6[:, :cw], wstr6[:, 128*hj:128*(hj+1)], kql[:, 512*hj:512*hj+cw],
                            start=False, stop=False))
                        R.run('t', lambda t, hj=hj, cw=cw, kqh=kqh: t.matmul(
                            psC6[:, :cw], wstr6b[:, 128*hj:128*(hj+1)], kqh[:, 512*hj:512*hj+cw],
                            start=False, stop=(hj == 31)))
                    v1 = R.run('v', lambda v, oc=oc, c0=c0, cw=cw, rrst=rrst: v.tensor_tensor(
                        ptm6[:, :cw], rrst[:, TILE*oc+c0:TILE*oc+c0+cw], psC6[:, :cw], ALU.mult),
                               after=[('t', R.mark('t')), ('s', R.mark('s'))])
                    R.run('v', lambda v, oc=oc, c0=c0, cw=cw: v.tensor_add(
                        x3t[:, TILE*oc+c0:TILE*oc+c0+cw], x3t[:, TILE*oc+c0:TILE*oc+c0+cw],
                        ptm6[:, :cw]))
            for (c0, cw) in TSL:
                split6(c0, cw, x3t)
                stats6(c0, cw, x3t)
                for i in range(8):
                    R.run('v', lambda v, i=i, c0=c0, cw=cw: v.tensor_tensor(
                        ptm6[:, :cw], x3t[:, TILE*i+c0:TILE*i+c0+cw], rstdb6[:, :cw], ALU.mult),
                          after=[('t', R.mark('t')), ('s', R.mark('s'))] if i == 0 else ())
                    R.run('v', lambda v, i=i, c0=c0, cw=cw: v.tensor_sub(
                        ptm6[:, :cw], ptm6[:, :cw], mub6[:, :cw]))
                    R.run('v', lambda v, i=i, c0=c0, cw=cw: v.tensor_copy(
                        xnh6[:, TILE*i+c0:TILE*i+c0+cw], ptm6[:, :cw]))
                    R.run('v', lambda v, i=i, c0=c0, cw=cw: v.tensor_sub(
                        xnl6[:, TILE*i+c0:TILE*i+c0+cw], ptm6[:, :cw],
                        xnh6[:, TILE*i+c0:TILE*i+c0+cw]))
                for hj in range(24):
                    nw = R.dma(wstr6[:, :1024], dslab(wbf["ffn_g"], 0, 8, 128*hj, 128, 3072),
                               after=[('t', R.mark('t'))])
                    nw2 = R.dma(wstr6b[:, :1024], dslab(wbf["ffn_u"], 0, 8, 128*hj, 128, 3072))
                    for i in range(8):
                        R.run('t', lambda t, i=i, c0=c0, cw=cw: t.matmul(
                            psC6[:, :cw], wstr6[:, 128*i:128*(i+1)],
                            xnh6[:, TILE*i+c0:TILE*i+c0+cw], start=(i == 0), stop=False),
                            after=[('q', nw2), ('v', R.mark('v')),
                                   ('s', R.mark('s'))] if i == 0 else ())
                        R.run('t', lambda t, i=i, c0=c0, cw=cw: t.matmul(
                            psC6[:, :cw], wstr6[:, 128*i:128*(i+1)],
                            xnl6[:, TILE*i+c0:TILE*i+c0+cw], start=False, stop=(i == 7)))
                    for i in range(8):
                        R.run('t', lambda t, i=i, c0=c0, cw=cw: t.matmul(
                            psD6[:, :cw], wstr6b[:, 128*i:128*(i+1)],
                            xnh6[:, TILE*i+c0:TILE*i+c0+cw], start=(i == 0), stop=False))
                        R.run('t', lambda t, i=i, c0=c0, cw=cw: t.matmul(
                            psD6[:, :cw], wstr6b[:, 128*i:128*(i+1)],
                            xnl6[:, TILE*i+c0:TILE*i+c0+cw], start=False, stop=(i == 7)))
                    sg = R.run('s', lambda s, cw=cw: s.activation(ptm6[:, :cw], psC6[:, :cw],
                                                                  AF.Silu, scale=1.0/127.0),
                               after=[('t', R.mark('t')), ('v', R.mark('v'))])
                    hv = R.run('v', lambda v, cw=cw: v.scalar_tensor_tensor(
                        ptm6b[:, :cw], psD6[:, :cw], 1.0/127.0, ptm6[:, :cw], ALU.mult, ALU.mult),
                               after=[('s', sg)])
                    R.run('v', lambda v, hj=hj, cw=cw, kqh=kqh: v.tensor_copy(
                        kqh[:, 512*hj:512*hj+cw], ptm6b[:, :cw]))
                    R.run('v', lambda v, hj=hj, cw=cw, kqh=kqh, kql=kql: v.tensor_sub(
                        kql[:, 512*hj:512*hj+cw], ptm6b[:, :cw], kqh[:, 512*hj:512*hj+cw]))
                for oc in range(8):
                    nw = R.dma(wstr6[:, :3072], dslab(wbf["ffn_d"], 0, 24, 128*oc, 128, 1024),
                               after=[('t', R.mark('t'))])
                    for hj in range(24):
                        R.run('t', lambda t, hj=hj, cw=cw, kqh=kqh: t.matmul(
                            psC6[:, :cw], wstr6[:, 128*hj:128*(hj+1)], kqh[:, 512*hj:512*hj+cw],
                            start=(hj == 0), stop=False),
                            after=[('q', nw), ('v', R.mark('v')),
                                   ('s', R.mark('s'))] if hj == 0 else ())
                        R.run('t', lambda t, hj=hj, cw=cw, kql=kql: t.matmul(
                            psC6[:, :cw], wstr6[:, 128*hj:128*(hj+1)], kql[:, 512*hj:512*hj+cw],
                            start=False, stop=(hj == 23)))
                    ev = R.run('v', lambda v, oc=oc, c0=c0, cw=cw: v.scalar_tensor_tensor(
                        ptm6[:, :cw], psC6[:, :cw], 1.0/127.0,
                        x3t[:, TILE*oc+c0:TILE*oc+c0+cw], ALU.mult, ALU.add),
                          after=[('t', R.mark('t')), ('q', R.n['q'])])
                    R.dma(dslice(out_fm, 128*oc, 128, tc0 + c0, cw, QP), ptm6[:, :cw],
                          after=[('v', ev)])
            S2.close()
    with ExitStack() as FIN:
        semT = FIN.enter_context(nc.semaphore("semT"))
        semV = FIN.enter_context(nc.semaphore("semV"))
        semS = FIN.enter_context(nc.semaphore("semS"))
        semQ = FIN.enter_context(nc.semaphore("semQ"))
        semQ2 = FIN.enter_context(nc.semaphore("semQ2"))
        block = FIN.enter_context(nc.Block())
        regref = {'t': gidx_t}
        R.replay(nc, block, {'t': semT, 'v': semV, 's': semS, 'q': semQ, 'q2': semQ2},
                 regref=regref)
    top.close()
    return nc, R, ap_in



def host_maps(inputs, N=16384):
    D = 1024
    MQ = (((N - 1)//3 + 1 + 3)//4)
    MQ = MQ + (-MQ) % 2
    QP = 3*MQ
    xx = np.asarray(inputs['L0_attn_a_x'])
    cstnp = scan_consts_np()
    xT = [np.zeros((D, N + 16), np.float32) for _ in range(2)]
    for b in range(2):
        xT[b][:, :N] = np.ascontiguousarray(xx[b].T)
    T = lambda a: np.ascontiguousarray(np.asarray(a).T)
    CC = np.ascontiguousarray
    common = dict(
        wf1=None,
        ffn_g=T(inputs['L0_ffn_w_gate']), ffn_u=T(inputs['L0_ffn_w_up']),
        ffn_d=T(inputs['L0_ffn_w_down']),
        wr_t=T(inputs['L0_attn_b_Wr']), ww_t=T(inputs['L0_attn_b_Ww']),
        wkb_t=T(inputs['L0_attn_b_Wk']), wvb_t=T(inputs['L0_attn_b_Wv']),
        wo_t=T(inputs['L0_attn_b_Wo']), wkey_t=T(inputs['L0_attn_b_Wkey']),
        wval_t=T(inputs['L0_attn_b_Wval']), wrec_t=T(inputs['L0_attn_b_Wrec']),
        ln_w=CC(np.asarray(inputs['L0_attn_a_ln_w']).reshape(D, 1)),
        ln1g=CC(np.asarray(inputs['L0_attn_b_ln1_g']).reshape(D, 1)),
        ln1b=CC(np.asarray(inputs['L0_attn_b_ln1_b']).reshape(D, 1)),
        ln2g=CC(np.asarray(inputs['L0_attn_b_ln2_g']).reshape(D, 1)),
        ln2b=CC(np.asarray(inputs['L0_attn_b_ln2_b']).reshape(D, 1)),
        tdec=CC(np.asarray(inputs['L0_attn_b_time_decay']).reshape(D, 1)),
        wo1=CC(np.asarray(inputs['L0_attn_a_Wo1'])),
        wo2=CC(np.asarray(inputs['L0_attn_a_Wo2'])),
        wout=CC(np.asarray(inputs['L0_attn_a_Wout'])),
        wf2=None,
    )
    del common['wf1'], common['wf2']
    for nme, arr in cstnp.items():
        common["cst_"+nme] = arr
    MQh = (((N - 1)//3 + 1 + 3)//4); MQh = MQh + (-MQh) % 2
    ac = np.zeros((128, 8), np.float32)
    for ci_, cv_ in enumerate([1e-5, 1.0, -1.0, float(3*MQh), float(3*MQh//6), 1.0/127.0, 1e-8, 2.0]):
        ac[:, ci_] = cv_
    common["actc"] = ac
    maps = []
    for c in range(8):
        b, g = c//4, c % 4
        qs = QP*g
        qe = min(qs + QP, N)
        m = dict(common)
        m['x_fm'] = xT[b]
        xqf = np.zeros((D, QP), np.float32)
        xqf[:, :qe-qs] = xT[b][:, qs:qe]
        m['xq_fm'] = xqf
        x3 = np.zeros((D, MQ), np.float32)
        sl = xT[b][:, qs:qe:3]
        x3[:, :sl.shape[1]] = sl
        m['xq3'] = x3
        m['wk_s'] = CC(np.asarray(inputs['L0_attn_a_Wk'])[:, 256*g:256*(g+1)])
        m['wv_s'] = CC(np.asarray(inputs['L0_attn_a_Wv'])[:, 256*g:256*(g+1)])
        m['wq_s'] = CC(np.asarray(inputs['L0_attn_a_Wq'])[:, 256*g:256*(g+1)])
        m['wfg'] = CC(np.concatenate([np.asarray(inputs['L0_attn_a_Wf1']),
                                      np.asarray(inputs['L0_attn_a_Wg'])[:, 4*g:4*(g+1)]], axis=1))
        m['wf2_s'] = CC(np.asarray(inputs['L0_attn_a_Wf2'])[:, 256*g:256*(g+1)])
        cm = np.zeros((128, 3), np.float32)
        for r in range(3):
            cm[:, r] = 1.0 if r < g else 0.0
        m['cmask'] = cm
        gvec = np.zeros((1, 8), np.int32)
        SLh = []
        c0h = 0
        while c0h < MQ:
            SLh.append(c0h)
            c0h += 512
        for kk, c0h in enumerate(SLh):
            gvec[0, kk] = c0h + MQ*g
        m['gidx'] = gvec
        maps.append(m)
    return maps, MQ, QP


def kernel(**inputs):
    import os
    os.environ["TRN_ALLOW_ICI_ALL_TO_ALL"] = "1"
    from concourse.bass_utils import run_bass_kernel_spmd
    N = int(np.asarray(inputs['L0_attn_a_x']).shape[1])
    maps, MQ, QP = host_maps(inputs, N)
    nc, R, _ = build_core_program(N)
    res = run_bass_kernel_spmd(nc, maps, core_ids=list(range(8)))
    D = 1024
    out = np.zeros((2, N, D), np.float32)
    for c in range(8):
        b, g = c//4, c % 4
        qs = QP*g
        qe = min(qs + QP, N)
        out[b, qs:qe, :] = res.results[c]['out_fm'][:, :qe-qs].T
    return out


# version 03
# speedup vs baseline: 1.0344x; 1.0344x over previous
import numpy as np
from contextlib import ExitStack
import concourse.bass as bass
import concourse.mybir as mybir

F32 = mybir.dt.float32
BF16 = mybir.dt.bfloat16
I32 = mybir.dt.int32
AF = mybir.ActivationFunctionType
ALU = mybir.AluOpType

def scan_consts_np():
    C = 64
    j = np.arange(2*C); t = np.arange(C)
    cst = {}
    cst["ident"] = np.eye(128, dtype=np.float32)
    cst["mask_ut"] = (j[:, None] < j[None, :]).astype(np.float32)
    cst["mask_gt"] = (t[:, None] < (j[None, :]//2)).astype(np.float32)
    md = (j[:, None]//64 == j[None, :]//64) & (j[:, None] < j[None, :])
    cst["mask_diag64"] = md.astype(np.float32)
    cst["mask_qu"] = ((j[:, None]//2) <= t[None, :]).astype(np.float32)
    cst["mask_qn"] = (t[:, None] <= t[None, :]).astype(np.float32)
    hs = np.zeros((2, 128), np.float32); hs[0, :64] = 1; hs[1, 64:] = 1
    cst["headsel2"] = hs
    cst["headsel2T"] = hs.T.copy()
    hsg = np.zeros((128, 256), np.float32)
    for p_ in range(2):
        hsg[64 + 2*p_, 128*p_:128*p_+64] = 1
        hsg[65 + 2*p_, 128*p_+64:128*p_+128] = 1
    cst["hsg"] = hsg
    cst["ones_col"] = np.ones((128, 1), np.float32)
    cst["zeros_col"] = np.zeros((128, 1), np.float32)
    cst["ones_row"] = np.ones((1, 512), np.float32)
    return cst


class Rec:
    def __init__(self):
        self.rec = []
        self.n = {k: 0 for k in ('t', 'v', 's', 'q', 'q2')}

    def run(self, eng, fn, after=()):
        self.rec.append((eng, fn, tuple(after)))
        self.n[eng] += 1
        return self.n[eng]

    def q2(self, fn, after=()):
        self.rec.append(('q2', fn, tuple(after)))
        self.n['q2'] += 1
        return self.n['q2']

    def mark(self, eng):
        return self.n[eng]

    def barrier(self, with_q=False):
        pt, ps = self.n['t'], self.n['s']
        extra = ((('q', self.n['q']),) if with_q else ())
        self.rec.append(('v', 'BARRIER', (('t', pt), ('s', ps)) + extra))
        self.n['v'] += 1
        nv = self.n['v']
        self.rec.append(('t', 'BARRIER', (('v', nv),)))
        self.n['t'] += 1
        self.rec.append(('s', 'BARRIER', (('v', nv),)))
        self.n['s'] += 1
        self.rec.append(('q', 'QWAIT', (('v', nv),)))

    def dma_reg(self, out_ap, dten, k, dims, after=()):
        self.rec.append(('q', ('DMAREG', out_ap, dten, k, dims), tuple(after)))
        self.n['q'] += 1
        return self.n['q']

    def dma(self, out, in_, after=(), slow=False):
        self.rec.append(('q', lambda h, out=out, in_=in_, slow=slow: h.dma_start(
            out, in_, allow_slow_non_contiguous=slow), tuple(after)))
        self.n['q'] += 1
        return self.n['q']

    def dma2(self, out, in_, eng='q2', after=()):
        self.rec.append((eng, lambda h, out=out, in_=in_: h.dma_start(out, in_), tuple(after)))
        self.n.setdefault(eng, 0)
        self.n[eng] += 1
        return self.n[eng]

    def replay(self, nc, block, sems, regref=None):
        handles = {}
        waitedq = {k: set() for k in self.n}
        for (_, fn, after) in self.rec:
            for (e2, c2) in after:
                if e2 == 'q' and c2 > 0:
                    waitedq[e2].add(c2)
        def mk(eng_key, sect):
            def emit(h):
                cnt = 0
                for (eng, fn, after) in self.rec:
                    if eng != eng_key:
                        continue
                    if isinstance(fn, tuple) and fn[0] == 'DMAREG':
                        _, out_ap, dten, k, dims = fn
                        for (e2, c2) in after:
                            if c2 > 0:
                                h.wait_ge(sems[e2], c2*16 if e2 == 'q' else c2)
                        if k not in regref:
                            regref[k] = h.alloc_register(f"goff{k}")
                        h.reg_load(regref[k], regref['t'][:1, k:k+1])
                        ins = h.dma_start(out_ap, bass.AP(dten, regref[k], dims))
                        ins.then_inc(sems[eng_key], 16)
                        cnt += 1
                        if cnt in waitedq[eng_key]:
                            h.wait_ge(sems[eng_key], cnt*16)
                        continue
                    if fn == 'QWAIT':
                        for (e2, c2) in after:
                            if c2 > 0:
                                h.wait_ge(sems[e2], c2*16 if e2 == 'q' else c2)
                        continue
                    if cnt > 0 and (eng_key in ('v', 's') or fn == 'BARRIER'):
                        h.wait_ge(sems[eng_key], cnt)
                    for (e2, c2) in after:
                        if c2 > 0 and e2 != eng_key:
                            h.wait_ge(sems[e2], c2*16 if e2 == 'q' else c2)
                    if fn == 'BARRIER':
                        h.sem_inc(sems[eng_key], 1)
                        cnt += 1
                        continue
                    ins = fn(h)
                    cnt += 1
                    if eng_key == 'q':
                        ins.then_inc(sems[eng_key], 16)
                        if cnt in waitedq[eng_key]:
                            h.wait_ge(sems[eng_key], cnt*16)
                    else:
                        ins.then_inc(sems[eng_key], 1)
                if eng_key == 'q' and cnt:
                    h.wait_ge(sems[eng_key], cnt*16)
                elif eng_key == 'q2' and cnt:
                    h.wait_ge(sems[eng_key], cnt)
            return emit
        block.sync(mk('q', None))
        block.tensor(mk('t', None))
        block.vector(mk('v', None))
        block.scalar(mk('s', None))
        if self.n.get('q2'):
            block.gpsimd(mk('q2', None))


class ScanABufs:
    def __init__(self, nc, stack, tag=""):
        sb = lambda n, sh: stack.enter_context(nc.sbuf_tensor(f"sa{tag}_{n}", sh, F32))
        ps = lambda n, sh: stack.enter_context(nc.psum_tensor(f"sa{tag}_{n}", sh, F32))
        self.csp = sb("csp", [128, 64]); self.cspp = sb("cspp", [128, 64])
        self.e1 = sb("e1", [128, 64]); self.e1p = sb("e1p", [128, 64])
        self.e2 = sb("e2", [128, 64]); self.e2p = sb("e2p", [128, 64])
        self.gk = sb("gk", [128, 64]); self.qt = sb("qt", [128, 64])
        self.kb = sb("kb", [128, 128]); self.ku = sb("ku", [128, 128])
        self.kn = sb("kn", [128, 64])
        self.atm = [sb("atma", [128, 128]), sb("atmb", [128, 128])]
        self.gtm = [sb("gtma", [64, 128]), sb("gtmb", [64, 128])]
        self.vst = sb("vst", [64, 128])
        self.x = sb("x", [128, 128])
        self.p = sb("p", [128, 128])
        self.xw = sb("xw", [128, 64])
        self.ltp = sb("ltp", [128, 128]); self.lp = sb("lp", [128, 128])
        self.yz = [sb(f"yz{i}", [128, 128]) for i in range(9)]
        self.tt = sb("tt", [128, 128]); self.tt2 = sb("tt2", [128, 128])
        self.kut = sb("kut", [128, 128]); self.knt = sb("knt", [64, 128])
        self.qum = [sb("quma", [128, 64]), sb("qumb", [128, 64])]
        self.qnm = [sb("qnma", [64, 64]), sb("qnmb", [64, 64])]
        self.ps1 = ps("ps1", [128, 128])
        self.ps2 = ps("ps2", [128, 128])
        self.ps3 = ps("ps3", [128, 128])
        self.ps4 = ps("ps4", [128, 64])
        self.ps5 = ps("ps5", [128, 64])
        self.pso = self.ps5
        self.psz = self.ps4


def emit_scan_chunk(ctx, cst, B, Kfm, SP, GB, Vsl, p, Z, Qfm=None, Ofm=None, ofm_after=()):
    r = ctx.run
    vc0 = r('v', lambda v: v.tensor_tensor_scan(B.csp[:], SP, SP, 0.0, ALU.add, ALU.bypass))
    vc1 = r('v', lambda v: v.tensor_sub(B.cspp[:], B.csp[:], SP))
    s1 = r('s', lambda s: s.activation(B.e1[:], B.csp[:], AF.Exp, scale=-1.0), after=[('v', vc0)])
    s2 = r('s', lambda s: s.activation(B.e1p[:], B.cspp[:], AF.Exp, scale=-1.0), after=[('v', vc1)])
    s3 = r('s', lambda s: s.activation(B.e2[:], B.csp[:], AF.Exp))
    s4 = r('s', lambda s: s.activation(B.e2p[:], B.cspp[:], AF.Exp))
    vgk = r('v', lambda v: v.scalar_tensor_tensor(B.gk[:], GB, -1.0, Kfm, ALU.mult, ALU.mult))
    vkb0 = r('v', lambda v: v.tensor_tensor(B.kb[:, 0::2], Kfm, B.e1p[:], ALU.mult), after=[('s', s2)])
    vkb1 = r('v', lambda v: v.tensor_tensor(B.kb[:, 1::2], Kfm, B.e1[:], ALU.mult), after=[('s', s1)])
    vku0 = r('v', lambda v: v.tensor_tensor(B.ku[:, 0::2], B.gk[:], B.e2p[:], ALU.mult), after=[('s', s4)])
    vku1 = r('v', lambda v: v.tensor_tensor(B.ku[:, 1::2], B.gk[:], B.e2[:], ALU.mult), after=[('s', s3)])
    vkn = r('v', lambda v: v.tensor_tensor(B.kn[:], Kfm, B.e2[:], ALU.mult))
    if Qfm is not None:
        vqt = r('v', lambda v: v.tensor_tensor(B.qt[:], Qfm, B.e1[:], ALU.mult))
    tvs = r('t', lambda t: t.matmul(B.ps3[:64, :128], cst["ident"][p:p+64, p:p+64], Vsl,
                                    start=True, stop=True, tile_position=(p, 0)))
    vvs = r('v', lambda v: v.tensor_copy(B.vst[:], B.ps3[:64, :128]), after=[('t', tvs)])
    tga = r('t', lambda t: t.matmul(B.ps1[:], B.ku[:64, :], B.kb[:64, :], start=True, stop=True,
                                    tile_position=(0, 0)), after=[('v', vku1), ('v', vkb1)])
    tgb = r('t', lambda t: t.matmul(B.ps2[:], B.ku[64:, :], B.kb[64:, :], start=True, stop=True,
                                    tile_position=(64, 0)))
    va = r('v', lambda v: v.tensor_tensor(B.atm[0][:], B.ps1[:], cst["mask_ut"][:], ALU.mult), after=[('t', tga)])
    vb = r('v', lambda v: v.tensor_tensor(B.atm[1][:], B.ps2[:], cst["mask_ut"][:], ALU.mult), after=[('t', tgb)])
    tka = r('t', lambda t: t.matmul(B.ps3[:64, :], B.kn[:64, :], B.kb[:64, :], start=True, stop=True,
                                    tile_position=(0, 0)), after=[('v', vkn), ('v', vvs)])
    vga = r('v', lambda v: v.tensor_tensor(B.gtm[0][:], B.ps3[:64, :], cst["mask_gt"][:], ALU.mult),
            after=[('t', tka)])
    tkb = r('t', lambda t: t.matmul(B.ps3[:64, :], B.kn[64:, :], B.kb[64:, :], start=True, stop=True,
                                    tile_position=(64, 0)), after=[('v', vga)])
    vgb = r('v', lambda v: v.tensor_tensor(B.gtm[1][:], B.ps3[:64, :], cst["mask_gt"][:], ALU.mult),
            after=[('t', tkb)])
    tt1 = r('t', lambda t: t.transpose(B.ps1[:], B.ku[:], cst["ident"][:]), after=[('v', va)])
    vkut = r('v', lambda v: v.tensor_copy(B.kut[:], B.ps1[:]), after=[('t', tt1)])
    tt2_ = r('t', lambda t: t.transpose(B.ps2[:64, :128], B.kn[:], cst["ident"][:]), after=[('v', vb)])
    vknt = r('v', lambda v: v.tensor_copy(B.knt[:], B.ps2[:64, :128]), after=[('t', tt2_)])
    tr1 = r('t', lambda t: t.matmul(B.ps3[:, :64], B.gtm[0][:], B.vst[:, :64], start=True, stop=False,
                                    tile_position=(0, 0)), after=[('v', vgb)])
    tr2 = r('t', lambda t: t.matmul(B.ps3[:, :64], B.kb[:64, :], Z[:64, :], start=False, stop=True,
                                    tile_position=(0, 0)))
    tr3 = r('t', lambda t: t.matmul(B.ps4[:, :64], B.gtm[1][:], B.vst[:, 64:], start=True, stop=True,
                                    tile_position=(0, 0)))
    tr4 = r('t', lambda t: t.matmul(B.ps5[:, :64], B.kb[64:, :], Z[64:, :], start=True, stop=True,
                                    tile_position=(64, 0)))
    vx = r('v', lambda v: v.tensor_copy(B.x[:, :64], B.ps3[:, :64]), after=[('t', tr4)])
    vx = r('v', lambda v: v.tensor_copy(B.x[:, 64:], B.ps4[:, :64]))
    vx = r('v', lambda v: v.tensor_add(B.x[:, 64:], B.x[:, 64:], B.ps5[:, :64]))
    if Qfm is not None:
        vq_prev, vn_prev = 0, 0
        for h in (0, 1):
            tq = r('t', lambda t, h=h: t.matmul(B.ps1[:, :64], B.ku[64*h:64*h+64, :], B.qt[64*h:64*h+64, :],
                                                start=True, stop=True, tile_position=(64*h, 0)),
                   after=[('v', vqt), ('v', vkut), ('v', vq_prev)])
            vq = r('v', lambda v, h=h: v.tensor_tensor(B.qum[h][:], B.ps1[:, :64], cst["mask_qu"][:], ALU.mult),
                   after=[('t', tq)])
            tn = r('t', lambda t, h=h: t.matmul(B.ps2[:64, :64], B.kn[64*h:64*h+64, :],
                                                B.qt[64*h:64*h+64, :], start=True, stop=True,
                                                tile_position=(64*h, 0)), after=[('v', vknt), ('v', vn_prev)])
            vn = r('v', lambda v, h=h: v.tensor_tensor(B.qnm[h][:], B.ps2[:64, :64],
                                                       cst["mask_qn"][:], ALU.mult), after=[('t', tn)])
            vq_prev, vn_prev = vq, vn
    pdone = [None, None]
    for h in (0, 1):
        atm = B.atm[h]
        vl = r('v', lambda v, atm=atm: v.tensor_tensor(B.ltp[:], atm[:], cst["mask_diag64"][:], ALU.mult),
               after=[('v', va if h == 0 else vb), ('t', ctx.mark('t'))])
        tl = r('t', lambda t: t.transpose(B.ps1[:], B.ltp[:], cst["ident"][:]), after=[('v', vl)])
        vlp = r('v', lambda v: v.tensor_copy(B.lp[:], B.ps1[:]), after=[('t', tl)])
        names = [(0, B.lp, B.ltp), (1, B.ltp, B.lp), (2, B.yz[1], B.yz[0]), (3, B.yz[0], B.yz[1]),
                 (4, B.yz[3], B.yz[2]), (5, B.yz[2], B.yz[3]), (6, B.yz[5], B.yz[4]),
                 (7, B.yz[4], B.yz[5]), (8, B.yz[6], B.yz[7])]
        last_v = vlp
        for (di, lh, rh) in names:
            tm = r('t', lambda t, lh=lh, rh=rh: t.matmul(B.ps1[:], lh[:], rh[:], start=True, stop=True),
                   after=[('v', last_v)])
            last_v = r('v', lambda v, di=di: v.tensor_copy(B.yz[di][:], B.ps1[:]), after=[('t', tm)])
        vt0 = r('v', lambda v: v.tensor_add(B.tt[:], B.ltp[:], cst["ident"][:]))
        cur, nxt = B.tt, B.tt2
        chain_lhs = [B.yz[1], B.yz[3], B.yz[5], B.yz[7], B.yz[8]]
        last = vt0
        for zk in chain_lhs:
            tm1 = r('t', lambda t, zk=zk, cur=cur: t.matmul(B.ps1[:], zk[:], cur[:], start=True, stop=False),
                    after=[('v', last)])
            tm2 = r('t', lambda t, cur=cur: t.matmul(B.ps1[:], cst["ident"][:], cur[:], start=False, stop=True))
            last = r('v', lambda v, nxt=nxt: v.tensor_copy(nxt[:], B.ps1[:]), after=[('t', tm2)])
            cur, nxt = nxt, cur
        ttfin = cur
        xcols = B.x[:, 64*h:64*h+64]
        pcols = B.p[:, 64*h:64*h+64]
        lastv = max(last, vx)
        for jb in range(2):
            r0 = 64*jb
            if jb == 0:
                tap = r('t', lambda t, r0=r0, ttfin=ttfin, xcols=xcols: t.matmul(
                    B.ps2[r0:r0+64, :64], ttfin[r0:r0+64, r0:r0+64], xcols[r0:r0+64, :],
                    start=True, stop=True, tile_position=(r0, r0)), after=[('v', lastv)])
            else:
                tx1 = r('t', lambda t, r0=r0, atm=atm, pcols=pcols: t.matmul(
                    B.ps3[r0:r0+64, :64], atm[:r0, r0:r0+64], pcols[:r0, :],
                    start=True, stop=True, tile_position=(0, r0)), after=[('v', lastv)])
                vxw = r('v', lambda v, r0=r0, xcols=xcols: v.tensor_add(
                    B.xw[r0:r0+64, :], xcols[r0:r0+64, :], B.ps3[r0:r0+64, :64]),
                        after=[('t', tx1)])
                tap = r('t', lambda t, r0=r0, ttfin=ttfin: t.matmul(
                    B.ps2[r0:r0+64, :64], ttfin[r0:r0+64, r0:r0+64], B.xw[r0:r0+64, :],
                    start=True, stop=True, tile_position=(r0, r0)), after=[('v', vxw)])
            lastv = r('v', lambda v, r0=r0, pcols=pcols: v.tensor_copy(pcols[r0:r0+64, :],
                                                                      B.ps2[r0:r0+64, :64]),
                      after=[('t', tap)])
        pdone[h] = lastv
    if Qfm is not None:
        for h in (0, 1):
            to1 = r('t', lambda t, h=h: t.matmul(B.ps3[64*h:64*h+64, :64], Z[64*h:64*h+64, :],
                                                 B.qt[64*h:64*h+64, :], start=True, stop=True,
                                                 tile_position=(64*h, 64*h)), after=[('v', pdone[h])])
            to2 = r('t', lambda t, h=h: t.matmul(B.pso[64*h:64*h+64, :], B.p[:, 64*h:64*h+64],
                                                 B.qum[h][:], start=True, stop=False,
                                                 tile_position=(0, 64*h)))
            to3 = r('t', lambda t, h=h: t.matmul(B.pso[64*h:64*h+64, :], B.vst[:, 64*h:64*h+64],
                                                 B.qnm[h][:], start=False, stop=True,
                                                 tile_position=(0, 64*h)))
        vo1 = r('v', lambda v: v.tensor_copy(Ofm[:], B.pso[:]),
                 after=[('t', ctx.mark('t'))] + list(ofm_after))
        vo = r('v', lambda v: v.tensor_add(Ofm[:], Ofm[:], B.ps3[:, :64]))
    tz = []
    for h in (0, 1):
        tz1 = r('t', lambda t, h=h: t.matmul(B.psz[64*h:64*h+64, :], B.kut[:, 64*h:64*h+64],
                                             B.p[:, 64*h:64*h+64], start=True, stop=False,
                                             tile_position=(0, 64*h)), after=[('v', pdone[h])])
        tz2 = r('t', lambda t, h=h: t.matmul(B.psz[64*h:64*h+64, :], B.knt[:, 64*h:64*h+64],
                                             B.vst[:, 64*h:64*h+64], start=False, stop=True,
                                             tile_position=(0, 64*h)))
        tz.append(tz2)
    vz1 = r('v', lambda v: v.tensor_add(Z[:], Z[:], B.psz[:]), after=[('t', tz[1])])
    vz2 = r('v', lambda v: v.tensor_scalar_mul(Z[:], Z[:], B.e1[:, 63:64]))
    return vz2



def dslice(t, r0, nr, c0, ncol, rowstride):
    return bass.AP(t, r0*rowstride + c0, [[rowstride, nr], [1, ncol]])

def dslab(t, r0, nblk, c0, ncol, rowstride):
    return bass.AP(t, r0*rowstride + c0, [[rowstride, 128], [128*rowstride, nblk], [1, ncol]])


def build_core_program(N=16384, NO_OVERRIDE=None):
    _dk = {}
    D = 1024
    MQ = ((N - 1)//3 + 1 + 3)//4
    MQ = MQ + (-MQ) % 2
    QP = 3*MQ
    TILE = QP//6
    NT = 6
    GRP = 512
    NG = N//GRP
    C = 64
    assert (2*N) % 3 == 2
    n0 = (2*N)//3
    NO = N - n0
    nc = bass.Bass("TRN2", target_bir_lowering=False)
    ap_in = {}
    def din(name, shape, dt=F32):
        ap_in[name] = nc.dram_tensor(name, list(shape), dt, kind="ExternalInput")
        return ap_in[name]
    x_fm = din("x_fm", [D, N + 16])
    xq_fm = din("xq_fm", [D, QP])
    xq3 = din("xq3", [D, MQ])
    cmask = din("cmask", [128, 3])
    actc = din("actc", [128, 8])
    gidx = din("gidx", [1, 8], I32)
    wk_s = din("wk_s", [D, 256]); wv_s = din("wv_s", [D, 256]); wq_s = din("wq_s", [D, 256])
    wfg = din("wfg", [D, 68])
    wf2_s = din("wf2_s", [64, 256])
    wo1 = din("wo1", [D, 64]); wo2 = din("wo2", [64, D])
    ln_w = din("ln_w", [D, 1])
    wout = din("wout", [D, D])
    ffn_g = din("ffn_g", [D, 3072], I32); ffn_u = din("ffn_u", [D, 3072], I32)
    ffn_d = din("ffn_d", [3072, D], I32)
    wr_t = din("wr_t", [D, D]); ww_t = din("ww_t", [D, D]); wkb_t = din("wkb_t", [D, D])
    wvb_t = din("wvb_t", [D, D]); wo_t = din("wo_t", [D, D])
    ln1g = din("ln1g", [D, 1]); ln1b = din("ln1b", [D, 1])
    ln2g = din("ln2g", [D, 1]); ln2b = din("ln2b", [D, 1])
    tdec = din("tdec", [D, 1])
    wkey_t = din("wkey_t", [D, 4096]); wval_t = din("wval_t", [4096, D]); wrec_t = din("wrec_t", [D, D])
    cstnp = scan_consts_np()
    for nme, arr in cstnp.items():
        din("cst_"+nme, arr.shape)
    out_fm = nc.dram_tensor("out_fm", [D, QP], F32, kind="ExternalOutput")

    MQ4 = 4*MQ
    obuf = nc.dram_tensor("obuf", [256, MQ4], F32, **_dk)
    ofull = nc.dram_tensor("ofull", [1024, MQ4], F32, **_dk)
    cbuf = nc.dram_tensor("cbuf", [D*2, 1], F32)
    cfull = nc.dram_tensor("cfull", [4*D*2, 1], F32)
    wbf = {}
    for nme, shp in [("ffn_g", (D, 3072)), ("ffn_u", (D, 3072)), ("ffn_d", (3072, D))]:
        wbf[nme] = nc.dram_tensor(nme+"_bf", list(shp), BF16, **_dk)
    HLW = [("wr_t", (D, D), "g1"), ("ww_t", (D, D), "g1"), ("wkb_t", (D, D), "g1"),
           ("wvb_t", (D, D), "g1"), ("wo_t", (D, D), None), ("wkey_t", (D, 4096), "g2"),
           ("wval_t", (4096, D), None), ("wrec_t", (D, D), "g2")]
    for nme, shp, _ in HLW:
        wbf[nme+"_h"] = nc.dram_tensor(nme+"_h", list(shp), BF16)
        wbf[nme+"_l"] = nc.dram_tensor(nme+"_l", list(shp), BF16)
    nd_scr = nc.dram_tensor("nd_scr", [2*D, QP], F32, **_dk)
    r_scr = nc.dram_tensor("r_scr", [D, QP], F32, **_dk)
    x2_scr = nc.dram_tensor("x2_scr", [D, QP], F32, **_dk)
    upd_scr = nc.dram_tensor("upd_scr", [D, MQ], F32, **_dk)
    x1_dbg = nc.dram_tensor("x1_dbg", [D, QP], F32, **_dk) if _dk else None
    xn_dbg = nc.dram_tensor("xn_dbg", [D, QP], BF16, **_dk) if _dk else None

    R = Rec()
    top = ExitStack()
    W12 = ExitStack()
    sbt = lambda nme, sh, dt=F32: top.enter_context(nc.sbuf_tensor(nme, sh, dt))
    w12sb = lambda nme, sh, dt=F32: W12.enter_context(nc.sbuf_tensor(nme, sh, dt))

    cst = {}
    for nme, arr in cstnp.items():
        cst[nme] = sbt("c_"+nme, list(arr.shape))
        R.dma(cst[nme][:], ap_in["cst_"+nme].ap())
    onesb = sbt("onesb", [128, 1], BF16)
    cmk = sbt("cmk", [128, 3])
    gidx_t = sbt("gidx_t", [1, 8], I32)
    actc_t = sbt("actc_t", [128, 8])
    ACT_CONSTS = [1e-5, 1.0, -1.0, float(QP), float(TILE), 1.0/127.0, 1e-8, 2.0]
    ngx = R.dma(gidx_t[:], ap_in["gidx"].ap())
    R.dma(actc_t[:], ap_in["actc"].ap())
    for ci_, cv_ in enumerate(ACT_CONSTS):
        nc.const_aps.register(F32, cv_, actc_t[:, ci_:ci_+1])
    R.dma(cmk[:], ap_in["cmask"].ap())
    nload0 = R.n['q']
    R.run('v', lambda v: v.tensor_copy(onesb[:], cst["ones_col"][:]), after=[('q', nload0)])

    wres = {}
    w1col = {}
    RES_W = [("wk_s", wk_s, 256), ("wv_s", wv_s, 256), ("wq_s", wq_s, 256),
             ("wfg", wfg, 68), ("wo1", wo1, 64), ("wout", wout, 1024)]
    for nme in ["ln_w", "ln1g", "ln1b", "ln2g", "ln2b", "tdec"]:
        wres[nme] = sbt("r_"+nme, [128, 8])
    for nme, shp, gsel in HLW:
        if gsel is not None:
            w1col[nme] = sbt(nme+"_w1c", [128, max(1, shp[1]//128)])
            w1col[nme+"_b"] = sbt(nme+"_bc", [128, max(1, shp[1]//128)])
    d4096 = sbt("d4096", [128, 8])
    dcol = sbt("dcol", [128, 8])
    d512t = sbt("d512t", [128, 8])
    tpow = sbt("tpow", [128, 8*TILE])
    hcol = sbt("hcol", [128, 16])
    hbase = sbt("hbase", [128, 16])
    zst = [sbt("zst0", [128, 64]), sbt("zst1", [128, 64])]
    hs2bT = sbt("hs2bT", [128, 2], BF16)
    ztile = sbt("ztile", [128, 16])
    carry_n = sbt("carry_n", [128, 8])
    carry_d = sbt("carry_d", [128, 8])
    hall = sbt("hall", [128, 64])
    dqm1 = sbt("dqm1", [128, 8])
    dmr = sbt("dmr", [128, 8])
    for nme, ap, cols in RES_W:
        wres[nme+"_h"] = w12sb(nme+"_h", [128, 8*cols], BF16)
        wres[nme+"_l"] = w12sb(nme+"_l", [128, 8*cols], BF16)
        if nme != "wout":
            w1col[nme] = w12sb(nme+"_w1c", [128, max(1, cols//128)])
            w1col[nme+"_row"] = w12sb(nme+"_w1r", [1, cols])
    wres["wo2_h"] = w12sb("wo2_h", [64, 1024], BF16)
    wres["wo2_l"] = w12sb("wo2_l", [64, 1024], BF16)
    wres["wf2_h"] = w12sb("wf2_h", [64, 256], BF16)
    wres["wf2_l"] = w12sb("wf2_l", [64, 256], BF16)

    def w1_transposes(wrow_psum_ap, dstcol, ncc, cbase, neg, tmp_sb):
        for j in range(ncc):
            tv = R.run('v', lambda v, j=j: v.tensor_copy(tmp_sb[:1, 128*j:128*(j+1)],
                                                         wrow_psum_ap[:1, 128*j:128*(j+1)]),
                       after=[('t', R.mark('t'))])
            tt = R.run('t', lambda t, j=j: t.transpose(psc_g[0][:128, :1],
                                                       tmp_sb[:1, 128*j:128*(j+1)],
                                                       cst["ident"][:1, :1]), after=[('v', tv)])
            R.run('v', lambda v, j=j: v.tensor_scalar_mul(dstcol[:, cbase+j:cbase+j+1],
                                                          psc_g[0][:128, :1],
                                                          -1.0 if neg else 1.0),
                  after=[('t', tt)])

    with ExitStack() as P0:
        p0sb = lambda nme, sh, dt=F32: P0.enter_context(nc.sbuf_tensor(nme, sh, dt))
        tmp = p0sb("p0tmp", [128, 512])
        tmp2 = p0sb("p0tmp2", [128, 512])
        tmpi = p0sb("p0tmpi", [128, 512], I32)
        tmpb = p0sb("p0tmpb", [128, 512], BF16)
        tmpb2 = p0sb("p0tmpb2", [128, 512], BF16)
        rowsb = p0sb("p0rowsb", [1, 512])
        psr1 = P0.enter_context(nc.psum_tensor("p0psr1", [1, 512], F32))
        psr2 = P0.enter_context(nc.psum_tensor("p0psr2", [1, 512], F32))
        psc_g = [P0.enter_context(nc.psum_tensor("p0psc", [128, 128], F32))]
        for nme, ap, cols in RES_W:
            for c0 in range(0, cols, 512):
                cw = min(512, cols - c0)
                for i in range(8):
                    nq = R.dma(tmp[:, :cw], dslice(ap, 128*i, 128, c0, cw, cols),
                               after=[('v', R.mark('v')), ('t', R.mark('t'))])
                    R.run('v', lambda v, i=i, cols=cols, c0=c0, cw=cw, nme=nme: v.tensor_copy(
                        wres[nme+"_h"][:, cols*i+c0:cols*i+c0+cw], tmp[:, :cw]), after=[('q', nq)])
                    R.run('v', lambda v, i=i, cols=cols, c0=c0, cw=cw, nme=nme: v.tensor_sub(
                        wres[nme+"_l"][:, cols*i+c0:cols*i+c0+cw], tmp[:, :cw],
                        wres[nme+"_h"][:, cols*i+c0:cols*i+c0+cw]))
                    if nme != "wout":
                        R.run('t', lambda t, i=i, cw=cw: t.matmul(
                            psr1[:, :cw], cst["ones_col"][:], tmp[:, :cw],
                            start=(i == 0), stop=(i == 7)), after=[('v', R.mark('v'))])
                if nme == "wout":
                    continue
                R.run('v', lambda v, nme=nme, c0=c0, cw=cw: v.tensor_scalar_mul(
                    w1col[nme+"_row"][:, c0:c0+cw], psr1[:, :cw], -1.0),
                      after=[('t', R.mark('t'))])
                if cw >= 128:
                    w1_transposes(psr1, w1col[nme], cw//128, c0//128, True, rowsb)
                else:
                    tv = R.run('v', lambda v, cw=cw: v.tensor_copy(rowsb[:1, :cw], psr1[:1, :cw]),
                               after=[('t', R.mark('t'))])
                    tt = R.run('t', lambda t, cw=cw: t.transpose(psc_g[0][:cw, :1],
                                                                 rowsb[:1, :cw], cst["ident"][:1, :1]),
                               after=[('v', tv)])
                    R.run('v', lambda v, cw=cw, nme=nme: v.tensor_scalar_mul(
                        w1col[nme][:cw, :1], psc_g[0][:cw, :1], -1.0), after=[('t', tt)])
        nq = R.dma(tmp[:64, :512], dslice(wo2, 0, 64, 0, 512, 1024), after=[('v', R.mark('v'))])
        nq2 = R.dma(tmp2[:64, :512], dslice(wo2, 0, 64, 512, 512, 1024))
        R.run('v', lambda v: v.tensor_copy(wres["wo2_h"][:, :512], tmp[:64, :512]), after=[('q', nq2)])
        R.run('v', lambda v: v.tensor_sub(wres["wo2_l"][:, :512], tmp[:64, :512], wres["wo2_h"][:, :512]))
        R.run('v', lambda v: v.tensor_copy(wres["wo2_h"][:, 512:], tmp2[:64, :512]))
        R.run('v', lambda v: v.tensor_sub(wres["wo2_l"][:, 512:], tmp2[:64, :512], wres["wo2_h"][:, 512:]))
        nq = R.dma(tmp[:64, :256], dslice(wf2_s, 0, 64, 0, 256, 256), after=[('v', R.mark('v'))])
        R.run('v', lambda v: v.tensor_copy(wres["wf2_h"][:], tmp[:64, :256]), after=[('q', nq)])
        R.run('v', lambda v: v.tensor_sub(wres["wf2_l"][:], tmp[:64, :256], wres["wf2_h"][:]))
        nl = 0
        for nme, ap in [("ln_w", ln_w), ("ln1g", ln1g), ("ln1b", ln1b), ("ln2g", ln2g),
                        ("ln2b", ln2b), ("tdec", tdec)]:
            nl = R.dma(wres[nme][:], bass.AP(ap, 0, [[1, 128], [128, 8]]), slow=True)
        R.run('s', lambda s: s.activation(d4096[:], wres["tdec"][:], AF.Exp, scale=float(QP)),
              after=[('q', nl)])
        R.run('s', lambda s: s.activation(d512t[:], wres["tdec"][:], AF.Exp, scale=float(TILE)))
        sdec = R.run('s', lambda s: s.activation(dcol[:], wres["tdec"][:], AF.Exp))
        for j in range(8):
            R.run('v', lambda v, j=j: v.tensor_tensor_scan(
                tpow[:, TILE*j:TILE*(j+1)], dcol[:, j:j+1].broadcast_to((128, TILE)),
                cst["zeros_col"][:, :1].broadcast_to((128, TILE)), 1.0, ALU.mult, ALU.add),
                  after=[('s', sdec)])
        for nme, src, rows, cols in [("ffn_g", ffn_g, D, 3072), ("ffn_u", ffn_u, D, 3072),
                                     ("ffn_d", ffn_d, 3072, D)]:
            for i in range(rows//128):
                for c0 in range(0, cols, 512):
                    nq = R.dma(tmpi[:], dslice(src, 128*i, 128, c0, 512, cols),
                               after=[('v', R.mark('v'))])
                    nv = R.run('v', lambda v: v.tensor_scalar_mul(tmpb[:], tmpi[:], 1.0),
                               after=[('q', R.n['q'])])
                    R.dma(dslice(wbf[nme], 128*i, 128, c0, 512, cols), tmpb[:],
                          after=[('v', nv)])
        for nme, shp, gsel in HLW:
            rows, cols = shp
            src = {"wr_t": wr_t, "ww_t": ww_t, "wkb_t": wkb_t, "wvb_t": wvb_t, "wo_t": wo_t,
                   "wkey_t": wkey_t, "wval_t": wval_t, "wrec_t": wrec_t}[nme]
            gv = {"g1": "ln1g", "g2": "ln2g", None: None}[gsel]
            bv = {"g1": "ln1b", "g2": "ln2b", None: None}[gsel]
            for c0 in range(0, cols, 512):
                for i in range(rows//128):
                    nq = R.dma(tmp[:], dslice(src, 128*i, 128, c0, 512, cols),
                               after=[('v', R.mark('v')), ('t', R.mark('t'))])
                    if gv is not None:
                        R.run('v', lambda v, gv=gv, i=i: v.tensor_scalar_mul(
                            tmp2[:], tmp[:], wres[gv][:, i:i+1]), after=[('q', nq)])
                        R.run('v', lambda v: v.tensor_copy(tmpb[:], tmp2[:]))
                        tga = R.run('t', lambda t, i=i, rows=rows: t.matmul(
                            psr1[:], cst["ones_col"][:], tmp2[:],
                            start=(i == 0), stop=(i == rows//128-1)),
                              after=[('v', R.mark('v'))])
                        bva = R.run('v', lambda v, bv=bv, i=i: v.tensor_scalar_mul(
                            tmp2[:], tmp[:], wres[bv][:, i:i+1]), after=[('t', tga)])
                        R.run('t', lambda t, i=i, rows=rows: t.matmul(
                            psr2[:], cst["ones_col"][:], tmp2[:],
                            start=(i == 0), stop=(i == rows//128-1)),
                              after=[('v', bva)])
                        src_for_lo = tmp2
                        R.run('v', lambda v, gv=gv, i=i: v.tensor_scalar_mul(
                            tmp2[:], tmp[:], wres[gv][:, i:i+1]), after=[('t', R.mark('t'))])
                        nst = R.dma(dslice(wbf[nme+"_h"], 128*i, 128, c0, 512, cols), tmpb[:],
                                    after=[('v', R.mark('v'))])
                        R.run('v', lambda v: v.tensor_sub(tmpb2[:], tmp2[:], tmpb[:]),
                              after=[('q', nst)])
                    else:
                        R.run('v', lambda v: v.tensor_copy(tmpb[:], tmp[:]), after=[('q', nq)])
                        nst = R.dma(dslice(wbf[nme+"_h"], 128*i, 128, c0, 512, cols), tmpb[:],
                                    after=[('v', R.mark('v'))])
                        R.run('v', lambda v: v.tensor_sub(tmpb2[:], tmp[:], tmpb[:]),
                              after=[('q', nst)])
                    R.dma(dslice(wbf[nme+"_l"], 128*i, 128, c0, 512, cols), tmpb2[:],
                          after=[('v', R.mark('v'))])
                if gv is not None:
                    w1_transposes(psr1, w1col[nme], 4, c0//128, True, rowsb)
                    w1_transposes(psr2, w1col[nme+"_b"], 4, c0//128, False, rowsb)
    R.barrier()

    R.run('v', lambda v: v.tensor_copy(hs2bT[:], cst["headsel2T"][:]))
    zv = R.run('v', lambda v: v.memset(ztile[:], 0.0))
    if MQ4 > NO:
        for rb in range(2):
            R.dma(dslice(obuf, 128*rb, 128, NO, MQ4 - NO, MQ4),
                  ztile[:, :MQ4 - NO], after=[('v', zv)])
    with ExitStack() as P1:
        p1sb = lambda nme, sh, dt=F32: P1.enter_context(nc.sbuf_tensor(nme, sh, dt))
        xg = p1sb("xg", [128, 4096])
        xh = p1sb("xh", [128, 4096], BF16)
        xl = p1sb("xl", [128, 4096], BF16)
        sqb = p1sb("sqb", [128, 4096], BF16)
        ptmp = p1sb("ptmp", [128, 512])
        kraw = p1sb("kraw", [128, 1024])
        kfm = p1sb("kfm", [128, 1024])
        spfm = p1sb("spfm", [128, 1024])
        qfm = p1sb("qfm", [128, 1024])
        vtm = p1sb("vtm", [128, 1024])
        fg = p1sb("fg", [128, 512])
        gb = p1sb("gb", [128, 1024])
        z1bh = p1sb("z1bh", [64, 512], BF16)
        z1bl = p1sb("z1bl", [64, 512], BF16)
        sqk = p1sb("sqk", [128, 512], BF16)
        strow = p1sb("strow", [1, 2048])
        rstdb = p1sb("rstdb", [128, 512])
        mrb = p1sb("mrb", [128, 512])
        rstdc = p1sb("rstdc", [128, 4])
        rn2 = p1sb("rn2", [2, 512])
        ofm = [p1sb("ofm0", [128, 64]), p1sb("ofm1", [128, 64])]
        psA = P1.enter_context(nc.psum_tensor("p1psA", [1, 512], F32))
        psP = P1.enter_context(nc.psum_tensor("p1psP", [128, 512], F32))
        psP2 = P1.enter_context(nc.psum_tensor("p1psP2", [128, 512], F32))
        SB = [ScanABufs(nc, P1, tag="0")]
        odma_last = {0: 0, 1: 0}
        R.run('v', lambda v: v.memset(zst[0][:], 0.0))
        R.run('v', lambda v: v.memset(zst[1][:], 0.0))

        def stats512(do_cols):
            for i in range(8):
                R.run('s', lambda s, i=i: s.activation(sqb[:, 512*i:512*(i+1)],
                                                       xg[:, 512*i:512*(i+1)], AF.Square),
                      after=[('v', R.mark('v'))])
            for i in range(8):
                R.run('t', lambda t, i=i: t.matmul(psA[:], onesb[:], xh[:, 512*i:512*(i+1)],
                                                   start=(i == 0), stop=False),
                      after=[('v', R.mark('v')), ('s', R.mark('s'))])
                R.run('t', lambda t, i=i: t.matmul(psA[:], onesb[:], xl[:, 512*i:512*(i+1)],
                                                   start=False, stop=(i == 7)))
            mu = R.run('v', lambda v: v.tensor_scalar_mul(strow[:, 0:512], psA[:], 1.0/D),
                       after=[('t', R.mark('t'))])
            for i in range(8):
                R.run('t', lambda t, i=i: t.matmul(psA[:], onesb[:], sqb[:, 512*i:512*(i+1)],
                                                   start=(i == 0), stop=(i == 7)),
                      after=[('v', mu)] if i == 0 else ())
            R.run('v', lambda v: v.tensor_scalar_mul(strow[:, 1536:2048], psA[:], 1.0/D),
                  after=[('t', R.mark('t'))])
            R.run('v', lambda v: v.tensor_tensor(strow[:, 1024:1536], strow[:, 0:512],
                                                 strow[:, 0:512], ALU.mult))
            R.run('v', lambda v: v.tensor_sub(strow[:, 1536:2048], strow[:, 1536:2048],
                                              strow[:, 1024:1536]))
            rst = R.run('s', lambda s: s.activation(strow[:, 512:1024], strow[:, 1536:2048],
                                                    AF.Sqrt, bias=1e-5), after=[('v', R.mark('v'))])
            R.run('v', lambda v: v.reciprocal(strow[:, 512:1024], strow[:, 512:1024]),
                  after=[('s', rst)])
            mr = R.run('v', lambda v: v.tensor_tensor(strow[:, 1024:1536], strow[:, 0:512],
                                                      strow[:, 512:1024], ALU.mult))
            t1 = R.run('t', lambda t: t.matmul(psP[:], cst["ones_row"][:1, :128],
                                               strow[:1, 512:1024], start=True, stop=True),
                       after=[('v', mr)])
            v1 = R.run('v', lambda v: v.tensor_copy(rstdb[:], psP[:]), after=[('t', t1)])
            t2 = R.run('t', lambda t: t.matmul(psP[:], cst["ones_row"][:1, :128],
                                               strow[:1, 1024:1536], start=True, stop=True),
                       after=[('v', v1)])
            R.run('v', lambda v: v.tensor_copy(mrb[:], psP[:]), after=[('t', t2)])
            if do_cols:
                for tb in range(4):
                    tt = R.run('t', lambda t, tb=tb: t.transpose(
                        psP2[:128, :1], strow[:1, 512+128*tb:512+128*(tb+1)], cst["ident"][:1, :1]),
                        after=[('v', R.mark('v'))])
                    R.run('v', lambda v, tb=tb: v.tensor_copy(rstdc[:, tb:tb+1], psP2[:128, :1]),
                          after=[('t', tt)])

        def proj_fm(wname, ocols, dests, act=None):
            for oc, (dst_ap, ow) in enumerate(dests):
                wh = wres[wname+"_h"]; wl = wres[wname+"_l"]
                for i in range(8):
                    cb = ocols*i + 128*oc
                    R.run('t', lambda t, i=i, cb=cb, ow=ow, wh=wh: t.matmul(
                        psP[:ow, :], wh[:, cb:cb+ow], xh[:, 512*i:512*(i+1)],
                        start=(i == 0), stop=False), after=[('v', R.mark('v'))] if i == 0 else ())
                    R.run('t', lambda t, i=i, cb=cb, ow=ow, wh=wh: t.matmul(
                        psP[:ow, :], wh[:, cb:cb+ow], xl[:, 512*i:512*(i+1)],
                        start=False, stop=False))
                    R.run('t', lambda t, i=i, cb=cb, ow=ow, wl=wl: t.matmul(
                        psP[:ow, :], wl[:, cb:cb+ow], xh[:, 512*i:512*(i+1)],
                        start=False, stop=(i == 7)))
                e1 = R.run('v', lambda v, ow=ow: v.tensor_tensor(ptmp[:ow, :], psP[:ow, :],
                                                                 rstdb[:ow, :], ALU.mult),
                           after=[('t', R.mark('t')), ('s', R.mark('s'))])
                e2 = R.run('v', lambda v, ow=ow, oc=oc, dst_ap=dst_ap, wname=wname:
                           v.scalar_tensor_tensor(dst_ap, mrb[:ow, :], w1col[wname][:ow, oc:oc+1],
                                                  ptmp[:ow, :], ALU.mult, ALU.add))
                if act is not None:
                    R.run('s', lambda s, dst_ap=dst_ap, act=act: s.activation(dst_ap, dst_ap, act),
                          after=[('v', e2)])
            return R.mark('v')

        for gi in range(NG):
            t0g = GRP*gi
            need_q = (t0g + GRP) > n0
            R.barrier(with_q=True)
            nq = R.dma(xg[:], dslab(x_fm, 0, 8, t0g, 512, N + 16))
            for i in range(8):
                R.run('v', lambda v, i=i: v.tensor_copy(xh[:, 512*i:512*(i+1)],
                                                        xg[:, 512*i:512*(i+1)]),
                      after=[('q', nq)] if i == 0 else ())
                R.run('v', lambda v, i=i: v.tensor_sub(xl[:, 512*i:512*(i+1)],
                                                       xg[:, 512*i:512*(i+1)],
                                                       xh[:, 512*i:512*(i+1)]))
            stats512(True)
            proj_fm("wk_s", 256, [(kraw[0:128, 0:512], 128), (kraw[0:128, 512:1024], 128)])
            for p in range(2):
                sA = R.run('s', lambda s, p=p: s.activation(kraw[:, 512*p:512*(p+1)],
                                                            kraw[:, 512*p:512*(p+1)], AF.Silu),
                           after=[('v', R.mark('v'))])
                sQ = R.run('s', lambda s, p=p: s.activation(sqk[:], kraw[:, 512*p:512*(p+1)],
                                                            AF.Square), after=[('t', R.mark('t'))])
                tn = R.run('t', lambda t: t.matmul(psP2[:2, :], hs2bT[:], sqk[:],
                                                   start=True, stop=True),
                           after=[('s', sQ), ('v', R.mark('v'))])
                sR = R.run('s', lambda s: s.activation(rn2[:], psP2[:2, :], AF.Sqrt),
                           after=[('t', tn)])
                vm = R.run('v', lambda v: v.tensor_scalar_max(rn2[:], rn2[:], 1e-12),
                           after=[('s', sR)])
                vm = R.run('v', lambda v: v.reciprocal(rn2[:], rn2[:]))
                tb2 = R.run('t', lambda t: t.matmul(psP2[:], cst["headsel2"][:], rn2[:],
                                                    start=True, stop=True), after=[('v', vm)])
                R.run('v', lambda v, p=p: v.tensor_tensor(kfm[:, 512*p:512*(p+1)],
                                                          kraw[:, 512*p:512*(p+1)],
                                                          psP2[:], ALU.mult), after=[('t', tb2)])
            for tb in range(4):
                for i in range(8):
                    R.run('t', lambda t, tb=tb, i=i: t.matmul(
                        psP[:, :256], xh[:, 512*i+128*tb:512*i+128*tb+128],
                        wres["wv_s_h"][:, 256*i:256*(i+1)], start=(i == 0), stop=False),
                        after=[('v', R.mark('v'))] if i == 0 else ())
                    R.run('t', lambda t, tb=tb, i=i: t.matmul(
                        psP[:, :256], xl[:, 512*i+128*tb:512*i+128*tb+128],
                        wres["wv_s_h"][:, 256*i:256*(i+1)], start=False, stop=False))
                    R.run('t', lambda t, tb=tb, i=i: t.matmul(
                        psP[:, :256], xh[:, 512*i+128*tb:512*i+128*tb+128],
                        wres["wv_s_l"][:, 256*i:256*(i+1)], start=False, stop=False))
                tk1 = R.run('t', lambda t, tb=tb: t.matmul(
                    psP[:, :256], strow[:1, 128*tb:128*tb+128],
                    w1col["wv_s_row"][:1, :256], start=False, stop=True))
                ev = R.run('v', lambda v, tb=tb: v.tensor_scalar_mul(
                    ptmp[:, :256], psP[:, :256], rstdc[:, tb:tb+1]),
                    after=[('t', tk1), ('s', R.mark('s'))])
                R.run('s', lambda s, tb=tb: s.activation(vtm[:, 256*tb:256*(tb+1)],
                                                         ptmp[:, :256], AF.Silu),
                      after=[('v', ev)])
            if need_q:
                proj_fm("wq_s", 256, [(qfm[0:128, 0:512], 128), (qfm[0:128, 512:1024], 128)],
                        act=AF.Silu)
            proj_fm("wfg", 68, [(fg[0:68, :], 68)])
            for p_ in range(2):
                tgp = R.run('t', lambda t, p_=p_: t.matmul(
                    psP2[:], cst["hsg"][64:68, 128*p_:128*p_+128], fg[64:68, :],
                    start=True, stop=True, tile_position=(64, 0)),
                    after=[('v', R.mark('v')), ('s', R.mark('s'))])
                R.run('s', lambda s, p_=p_: s.activation(gb[:, 512*p_:512*(p_+1)], psP2[:],
                                                         AF.Sigmoid), after=[('t', tgp)])
            vzh = R.run('v', lambda v: v.tensor_copy(z1bh[:], fg[:64, :]))
            vzl = R.run('v', lambda v: v.tensor_sub(z1bl[:], fg[:64, :], z1bh[:]))
            for p in range(2):
                tA = R.run('t', lambda t, p=p: t.matmul(psP[:], wres["wf2_h"][:, 128*p:128*(p+1)],
                                                        z1bh[:], start=True, stop=False),
                           after=[('v', vzl), ('s', R.mark('s'))])
                R.run('t', lambda t, p=p: t.matmul(psP[:], wres["wf2_h"][:, 128*p:128*(p+1)],
                                                   z1bl[:], start=False, stop=False))
                tC = R.run('t', lambda t, p=p: t.matmul(psP[:], wres["wf2_l"][:, 128*p:128*(p+1)],
                                                        z1bh[:], start=False, stop=True))
                sE = R.run('s', lambda s: s.activation(ptmp[:], psP[:], AF.Exp, scale=-1.0),
                           after=[('t', tC)])
                R.run('s', lambda s, p=p: s.activation(spfm[:, 512*p:512*(p+1)], ptmp[:],
                                                       AF.Ln, bias=1.0))
            R.barrier()
            for ci in range(8):
                t0c = t0g + C*ci
                par = ci % 2
                tb = ci // 2
                for p in range(2):
                    R.barrier()
                    Vsl = vtm[64*par:64*par+64, 256*tb+128*p:256*tb+128*(p+1)]
                    needo = (t0c + C) > n0
                    lastv = emit_scan_chunk(
                        R, cst, SB[0],
                        kfm[:, 512*p+C*ci:512*p+C*ci+C],
                        spfm[:, 512*p+C*ci:512*p+C*ci+C],
                        gb[:, 512*p+C*ci:512*p+C*ci+C],
                        Vsl, 64*par, zst[p],
                        Qfm=(qfm[:, 512*p+C*ci:512*p+C*ci+C] if needo else None),
                        Ofm=(ofm[p] if needo else None),
                        ofm_after=((('q', odma_last[p]),) if needo else ()))
                    if needo:
                        mlo = max(t0c, n0)
                        cw = t0c + C - mlo
                        R.dma(dslice(obuf, 128*p, 128, mlo - n0, cw, MQ4),
                              ofm[p][:, (mlo - t0c):(mlo - t0c) + cw],
                              after=[('v', lastv)])
                        odma_last[p] = R.n['q']
    R.barrier(with_q=True)
    R.run('q2', lambda h: h.collective_compute(
        "AllGather", ALU.bypass, replica_groups=[[0, 1, 2, 3], [4, 5, 6, 7]],
        ins=[obuf.ap().opt()], outs=[ofull.ap().opt()]), after=[('q', R.n['q'])])
    naa = R.n['q2']
    with ExitStack() as P2:
        p2sb = lambda nme, sh, dt=F32: P2.enter_context(nc.sbuf_tensor(nme, sh, dt))
        SL = []
        c0 = 0
        while c0 < MQ:
            SL.append((c0, min(512, MQ - c0)))
            c0 += 512
        z1g = p2sb("z1g", [64, MQ])
        z1gh = p2sb("z1gh", [64, MQ], BF16)
        z1gl = p2sb("z1gl", [64, MQ], BF16)
        xq = p2sb("xq", [128, 8*512])
        xqh = p2sb("xqh", [128, 8*512], BF16)
        xql = p2sb("xql", [128, 8*512], BF16)
        sqb2 = p2sb("sqb2", [128, 8*512], BF16)
        ptm = p2sb("ptm", [128, 512])
        xln = p2sb("xln", [128, 8*512])
        xlnh = p2sb("xlnh", [128, 8*512], BF16)
        xlnl = p2sb("xlnl", [128, 8*512], BF16)
        gt8 = p2sb("gt8", [128, 512])
        strow2 = p2sb("strow2", [1, 2048])
        rstdb2 = p2sb("rstdb2", [128, 512])
        mrb2 = p2sb("mrb2", [128, 512])
        psA2 = P2.enter_context(nc.psum_tensor("p2psA", [1, 512], F32))
        psC2 = P2.enter_context(nc.psum_tensor("p2psC", [128, 512], F32))
        psD2 = P2.enter_context(nc.psum_tensor("p2psD", [128, 512], F32))
        for (c0, cw) in SL:
            R.barrier(with_q=True)
            nq = R.dma(xq[:, :8*cw].rearrange("p (i c) -> p (i c)", i=8) if False else xq[:, :8*cw],
                       dslab(xq3, 0, 8, c0, cw, MQ))
            for i in range(8):
                R.run('v', lambda v, i=i, cw=cw: v.tensor_copy(xqh[:, cw*i:cw*(i+1)],
                                                               xq[:, cw*i:cw*(i+1)]),
                      after=[('q', nq)] if i == 0 else ())
                R.run('v', lambda v, i=i, cw=cw: v.tensor_sub(xql[:, cw*i:cw*(i+1)],
                                                              xq[:, cw*i:cw*(i+1)],
                                                              xqh[:, cw*i:cw*(i+1)]))
            for i in range(8):
                R.run('s', lambda s, i=i, cw=cw: s.activation(sqb2[:, cw*i:cw*(i+1)],
                                                              xq[:, cw*i:cw*(i+1)], AF.Square),
                      after=[('v', R.mark('v'))])
            for i in range(8):
                R.run('t', lambda t, i=i, cw=cw: t.matmul(psA2[:, :cw], onesb[:],
                                                          xqh[:, cw*i:cw*(i+1)],
                                                          start=(i == 0), stop=False),
                      after=[('v', R.mark('v')), ('s', R.mark('s'))])
                R.run('t', lambda t, i=i, cw=cw: t.matmul(psA2[:, :cw], onesb[:],
                                                          xql[:, cw*i:cw*(i+1)],
                                                          start=False, stop=(i == 7)))
            nmu = R.run('v', lambda v, cw=cw: v.tensor_scalar_mul(strow2[:, 0:cw], psA2[:, :cw], 1.0/D),
                  after=[('t', R.mark('t'))])
            for i in range(8):
                R.run('t', lambda t, i=i, cw=cw: t.matmul(psA2[:, :cw], onesb[:],
                                                          sqb2[:, cw*i:cw*(i+1)],
                                                          start=(i == 0), stop=(i == 7)),
                      after=[('v', nmu)] if i == 0 else ())
            R.run('v', lambda v, cw=cw: v.tensor_scalar_mul(strow2[:, 1536:1536+cw], psA2[:, :cw], 1.0/D),
                  after=[('t', R.mark('t'))])
            R.run('v', lambda v, cw=cw: v.tensor_tensor(strow2[:, 1024:1024+cw], strow2[:, 0:cw],
                                                        strow2[:, 0:cw], ALU.mult))
            R.run('v', lambda v, cw=cw: v.tensor_sub(strow2[:, 1536:1536+cw], strow2[:, 1536:1536+cw],
                                                     strow2[:, 1024:1024+cw]))
            rs2 = R.run('s', lambda s, cw=cw: s.activation(strow2[:, 512:512+cw],
                                                           strow2[:, 1536:1536+cw],
                                                           AF.Sqrt, bias=1e-5),
                        after=[('v', R.mark('v'))])
            R.run('v', lambda v, cw=cw: v.reciprocal(strow2[:, 512:512+cw],
                                                     strow2[:, 512:512+cw]),
                  after=[('s', rs2)])
            mr2 = R.run('v', lambda v, cw=cw: v.tensor_tensor(strow2[:, 1024:1024+cw],
                                                              strow2[:, 0:cw],
                                                              strow2[:, 512:512+cw], ALU.mult))
            tb1 = R.run('t', lambda t, cw=cw: t.matmul(psC2[:, :cw], cst["ones_row"][:1, :128],
                                                       strow2[:1, 512:512+cw], start=True, stop=True),
                        after=[('v', mr2)])
            vb1 = R.run('v', lambda v, cw=cw: v.tensor_copy(rstdb2[:, :cw], psC2[:, :cw]),
                        after=[('t', tb1)])
            tb2 = R.run('t', lambda t, cw=cw: t.matmul(psC2[:, :cw], cst["ones_row"][:1, :128],
                                                       strow2[:1, 1024:1024+cw], start=True, stop=True),
                        after=[('v', vb1)])
            R.run('v', lambda v, cw=cw: v.tensor_copy(mrb2[:, :cw], psC2[:, :cw]), after=[('t', tb2)])
            for i in range(8):
                R.run('t', lambda t, i=i, cw=cw: t.matmul(
                    psC2[:64, :cw], wres["wo1_h"][:, 64*i:64*(i+1)], xqh[:, cw*i:cw*(i+1)],
                    start=(i == 0), stop=False), after=[('v', R.mark('v'))] if i == 0 else ())
                R.run('t', lambda t, i=i, cw=cw: t.matmul(
                    psC2[:64, :cw], wres["wo1_h"][:, 64*i:64*(i+1)], xql[:, cw*i:cw*(i+1)],
                    start=False, stop=False))
                R.run('t', lambda t, i=i, cw=cw: t.matmul(
                    psC2[:64, :cw], wres["wo1_l"][:, 64*i:64*(i+1)], xqh[:, cw*i:cw*(i+1)],
                    start=False, stop=(i == 7)))
            ez = R.run('v', lambda v, cw=cw: v.tensor_tensor(ptm[:64, :cw], psC2[:64, :cw],
                                                             rstdb2[:64, :cw], ALU.mult),
                       after=[('t', R.mark('t'))])
            ez2 = R.run('v', lambda v, cw=cw, c0=c0: v.scalar_tensor_tensor(
                z1g[:, c0:c0+cw], mrb2[:64, :cw], w1col["wo1"][:64, 0:1], ptm[:64, :cw],
                ALU.mult, ALU.add))
        vgh = R.run('v', lambda v: v.tensor_copy(z1gh[:], z1g[:]))
        vgl = R.run('v', lambda v: v.tensor_sub(z1gl[:], z1g[:], z1gh[:]))
        for (c0, cw) in SL:
            R.barrier(with_q=True)
            nq = R.dma_reg(xq[:, :8*cw], ofull, SL.index((c0, cw)),
                           [[MQ4, 128], [128*MQ4, 8], [1, cw]],
                           after=[('q2', naa), ('q', ngx)])
            for i in range(8):
                tg1 = R.run('t', lambda t, i=i, cw=cw, c0=c0: t.matmul(
                    psC2[:, :cw], wres["wo2_h"][:, 128*i:128*(i+1)], z1gh[:, c0:c0+cw],
                    start=True, stop=False),
                    after=[('v', vgl), ('v', R.mark('v')), ('s', R.mark('s'))])
                R.run('t', lambda t, i=i, cw=cw, c0=c0: t.matmul(
                    psC2[:, :cw], wres["wo2_h"][:, 128*i:128*(i+1)], z1gl[:, c0:c0+cw],
                    start=False, stop=False))
                tg3 = R.run('t', lambda t, i=i, cw=cw, c0=c0: t.matmul(
                    psC2[:, :cw], wres["wo2_l"][:, 128*i:128*(i+1)], z1gh[:, c0:c0+cw],
                    start=False, stop=True))
                sg = R.run('s', lambda s, cw=cw: s.activation(gt8[:, :cw], psC2[:, :cw], AF.Sigmoid),
                           after=[('t', tg3), ('v', R.mark('v'))])
                R.run('v', lambda v, i=i, cw=cw: v.tensor_tensor(
                    xq[:, cw*i:cw*(i+1)], xq[:, cw*i:cw*(i+1)], gt8[:, :cw], ALU.mult),
                      after=[('s', sg), ('q', nq)])
            for i in range(8):
                R.run('s', lambda s, i=i, cw=cw: s.activation(sqb2[:, cw*i:cw*(i+1)],
                                                              xq[:, cw*i:cw*(i+1)], AF.Square),
                      after=[('v', R.mark('v'))] + ([('t', R.mark('t'))] if i == 0 else []))
                R.run('v', lambda v, i=i, cw=cw: v.tensor_copy(xqh[:, cw*i:cw*(i+1)],
                                                               xq[:, cw*i:cw*(i+1)]),
                      after=[('t', R.mark('t'))] if i == 0 else ())
                R.run('v', lambda v, i=i, cw=cw: v.tensor_sub(xql[:, cw*i:cw*(i+1)],
                                                              xq[:, cw*i:cw*(i+1)],
                                                              xqh[:, cw*i:cw*(i+1)]))
            for i in range(8):
                R.run('t', lambda t, i=i, cw=cw: t.matmul(psA2[:, :cw], onesb[:],
                                                          xqh[:, cw*i:cw*(i+1)],
                                                          start=(i == 0), stop=False),
                      after=[('v', R.mark('v')), ('s', R.mark('s'))])
                R.run('t', lambda t, i=i, cw=cw: t.matmul(psA2[:, :cw], onesb[:],
                                                          xql[:, cw*i:cw*(i+1)],
                                                          start=False, stop=(i == 7)))
            nmu = R.run('v', lambda v, cw=cw: v.tensor_scalar_mul(strow2[:, 0:cw], psA2[:, :cw], 1.0/D),
                  after=[('t', R.mark('t'))])
            for i in range(8):
                R.run('t', lambda t, i=i, cw=cw: t.matmul(psA2[:, :cw], onesb[:],
                                                          sqb2[:, cw*i:cw*(i+1)],
                                                          start=(i == 0), stop=(i == 7)),
                      after=[('v', nmu)] if i == 0 else ())
            R.run('v', lambda v, cw=cw: v.tensor_scalar_mul(strow2[:, 1536:1536+cw], psA2[:, :cw], 1.0/D),
                  after=[('t', R.mark('t'))])
            R.run('v', lambda v, cw=cw: v.tensor_tensor(strow2[:, 1024:1024+cw], strow2[:, 0:cw],
                                                        strow2[:, 0:cw], ALU.mult))
            R.run('v', lambda v, cw=cw: v.tensor_sub(strow2[:, 1536:1536+cw],
                                                     strow2[:, 1536:1536+cw],
                                                     strow2[:, 1024:1024+cw]))
            rs3 = R.run('s', lambda s, cw=cw: s.activation(strow2[:, 512:512+cw],
                                                           strow2[:, 1536:1536+cw],
                                                           AF.Sqrt, bias=1e-5),
                        after=[('v', R.mark('v'))])
            rs3 = R.run('v', lambda v, cw=cw: v.reciprocal(strow2[:, 512:512+cw],
                                                           strow2[:, 512:512+cw]),
                        after=[('s', rs3)])
            tb1 = R.run('t', lambda t, cw=cw: t.matmul(psC2[:, :cw], cst["ones_row"][:1, :128],
                                                       strow2[:1, 512:512+cw], start=True, stop=True),
                        after=[('v', rs3)])
            vb1 = R.run('v', lambda v, cw=cw: v.tensor_copy(rstdb2[:, :cw], psC2[:, :cw]),
                        after=[('t', tb1)])
            tb2 = R.run('t', lambda t, cw=cw: t.matmul(psC2[:, :cw], cst["ones_row"][:1, :128],
                                                       strow2[:1, 0:cw], start=True, stop=True),
                        after=[('v', vb1)])
            vb2 = R.run('v', lambda v, cw=cw: v.tensor_copy(mrb2[:, :cw], psC2[:, :cw]),
                        after=[('t', tb2)])
            for i in range(8):
                R.run('v', lambda v, i=i, cw=cw: v.tensor_sub(xln[:, cw*i:cw*(i+1)],
                                                              xq[:, cw*i:cw*(i+1)], mrb2[:, :cw]))
                R.run('v', lambda v, i=i, cw=cw: v.tensor_tensor(xln[:, cw*i:cw*(i+1)],
                                                                 xln[:, cw*i:cw*(i+1)],
                                                                 rstdb2[:, :cw], ALU.mult))
                R.run('v', lambda v, i=i, cw=cw: v.tensor_scalar_mul(xln[:, cw*i:cw*(i+1)],
                                                                     xln[:, cw*i:cw*(i+1)],
                                                                     wres["ln_w"][:, i:i+1]))
                R.run('v', lambda v, i=i, cw=cw: v.tensor_copy(xlnh[:, cw*i:cw*(i+1)],
                                                               xln[:, cw*i:cw*(i+1)]))
                R.run('v', lambda v, i=i, cw=cw: v.tensor_sub(xlnl[:, cw*i:cw*(i+1)],
                                                              xln[:, cw*i:cw*(i+1)],
                                                              xlnh[:, cw*i:cw*(i+1)]))
            for oc in range(8):
                for i in range(8):
                    R.run('t', lambda t, oc=oc, i=i, cw=cw: t.matmul(
                        psD2[:, :cw], wres["wout_h"][:, 1024*i+128*oc:1024*i+128*oc+128],
                        xlnh[:, cw*i:cw*(i+1)], start=(i == 0), stop=False),
                        after=[('v', R.mark('v'))] if i == 0 else ())
                    R.run('t', lambda t, oc=oc, i=i, cw=cw: t.matmul(
                        psD2[:, :cw], wres["wout_h"][:, 1024*i+128*oc:1024*i+128*oc+128],
                        xlnl[:, cw*i:cw*(i+1)], start=False, stop=False))
                    R.run('t', lambda t, oc=oc, i=i, cw=cw: t.matmul(
                        psD2[:, :cw], wres["wout_l"][:, 1024*i+128*oc:1024*i+128*oc+128],
                        xlnh[:, cw*i:cw*(i+1)], start=False, stop=(i == 7)))
                ev = R.run('v', lambda v, cw=cw: v.tensor_copy(ptm[:, :cw], psD2[:, :cw]),
                           after=[('t', R.mark('t')), ('q', R.n['q'])])
                R.dma(dslice(upd_scr, 128*oc, 128, c0, cw, MQ), ptm[:, :cw], after=[('v', ev)])
    R.barrier(with_q=True)
    W12.close()
    TSL = []
    c0 = 0
    while c0 < TILE:
        TSL.append((c0, min(512, TILE - c0)))
        c0 += 512
    R.run('v', lambda v: v.memset(carry_n[:], 0.0))
    R.run('v', lambda v: v.memset(carry_d[:], 0.0))
    with ExitStack() as P3:
        p3sb = lambda nme, sh, dt=F32: P3.enter_context(nc.sbuf_tensor(nme, sh, dt))
        xt = p3sb("xt", [128, 8*TILE])
        xnh = p3sb("xnh", [128, 8*TILE], BF16)
        xnl = p3sb("xnl", [128, 8*TILE], BF16)
        sqb3 = p3sb("sqb3", [128, 8*512], BF16)
        ut = p3sb("ut", [128, 8*256])
        hbh = p3sb("hbh", [128, 12*512], BF16)
        hbl = p3sb("hbl", [128, 12*512], BF16)
        dacc = p3sb("dacc", [128, 8*512])
        wstr = p3sb("wstr", [128, 1024], BF16)
        wstr2 = p3sb("wstr2", [128, 1024], BF16)
        wstrB = p3sb("wstrB", [128, 1024], BF16)
        wstr2B = p3sb("wstr2B", [128, 1024], BF16)
        wdstr = p3sb("wdstr", [128, 3072], BF16)
        wdstrB = p3sb("wdstrB", [128, 3072], BF16)
        ptm3 = p3sb("ptm3", [128, 512])
        ptm3b = p3sb("ptm3b", [128, 512])
        strow3 = p3sb("strow3", [1, 2048])
        rstdb3 = p3sb("rstdb3", [128, 512])
        mub3 = p3sb("mub3", [128, 512])
        wst1 = p3sb("wst1", [128, 512])
        kst1 = p3sb("kst1", [128, 512])
        vst1 = p3sb("vst1", [128, 512])
        psA3 = P3.enter_context(nc.psum_tensor("p3psA", [1, 512], F32))
        psC3 = P3.enter_context(nc.psum_tensor("p3psC", [128, 512], F32))
        psD3 = P3.enter_context(nc.psum_tensor("p3psD", [128, 512], F32))

        def stats3(srcbuf, c0, cw, with_sq_split=True):
            for i in range(8):
                R.run('s', lambda s, i=i, c0=c0, cw=cw: s.activation(
                    sqb3[:, 512*i:512*i+cw], srcbuf[:, TILE*i+c0:TILE*i+c0+cw], AF.Square),
                      after=[('v', R.mark('v'))])
            for i in range(8):
                R.run('t', lambda t, i=i, c0=c0, cw=cw: t.matmul(
                    psA3[:, :cw], onesb[:], xnh[:, TILE*i+c0:TILE*i+c0+cw],
                    start=(i == 0), stop=False),
                      after=[('v', R.mark('v')), ('s', R.mark('s'))])
                R.run('t', lambda t, i=i, c0=c0, cw=cw: t.matmul(
                    psA3[:, :cw], onesb[:], xnl[:, TILE*i+c0:TILE*i+c0+cw],
                    start=False, stop=(i == 7)))
            nmu = R.run('v', lambda v, cw=cw: v.tensor_scalar_mul(strow3[:, 0:cw], psA3[:, :cw], 1.0/D),
                  after=[('t', R.mark('t'))])
            for i in range(8):
                R.run('t', lambda t, i=i, cw=cw: t.matmul(
                    psA3[:, :cw], onesb[:], sqb3[:, 512*i:512*i+cw],
                    start=(i == 0), stop=(i == 7)), after=[('v', nmu)] if i == 0 else ())
            R.run('v', lambda v, cw=cw: v.tensor_scalar_mul(strow3[:, 1536:1536+cw], psA3[:, :cw], 1.0/D),
                  after=[('t', R.mark('t'))])
            R.run('v', lambda v, cw=cw: v.tensor_tensor(strow3[:, 1024:1024+cw], strow3[:, 0:cw],
                                                        strow3[:, 0:cw], ALU.mult))
            R.run('v', lambda v, cw=cw: v.tensor_sub(strow3[:, 1536:1536+cw],
                                                     strow3[:, 1536:1536+cw],
                                                     strow3[:, 1024:1024+cw]))
            rs = R.run('s', lambda s, cw=cw: s.activation(strow3[:, 512:512+cw],
                                                          strow3[:, 1536:1536+cw],
                                                          AF.Sqrt, bias=1e-5),
                       after=[('v', R.mark('v'))])
            rs = R.run('v', lambda v, cw=cw: v.reciprocal(strow3[:, 512:512+cw],
                                                          strow3[:, 512:512+cw]),
                       after=[('s', rs)])
            t1 = R.run('t', lambda t, cw=cw: t.matmul(psC3[:, :cw], cst["ones_row"][:1, :128],
                                                      strow3[:1, 512:512+cw], start=True, stop=True),
                       after=[('v', rs)])
            v1 = R.run('v', lambda v, cw=cw: v.tensor_copy(rstdb3[:, :cw], psC3[:, :cw]),
                       after=[('t', t1)])
            t2 = R.run('t', lambda t, cw=cw: t.matmul(psC3[:, :cw], cst["ones_row"][:1, :128],
                                                      strow3[:1, 0:cw], start=True, stop=True),
                       after=[('v', v1)])
            R.run('v', lambda v, cw=cw: v.tensor_copy(mub3[:, :cw], psC3[:, :cw]), after=[('t', t2)])

        def split_xn(c0, cw, from_norm=False):
            for i in range(8):
                if from_norm:
                    R.run('v', lambda v, i=i, c0=c0, cw=cw: v.tensor_sub(
                        ptm3[:, :cw], xt[:, TILE*i+c0:TILE*i+c0+cw], mub3[:, :cw]))
                    R.run('v', lambda v, i=i, c0=c0, cw=cw: v.tensor_tensor(
                        ptm3[:, :cw], ptm3[:, :cw], rstdb3[:, :cw], ALU.mult))
                    src = ptm3
                    R.run('v', lambda v, i=i, c0=c0, cw=cw, src=src: v.tensor_copy(
                        xnh[:, TILE*i+c0:TILE*i+c0+cw], src[:, :cw]))
                    R.run('v', lambda v, i=i, c0=c0, cw=cw, src=src: v.tensor_sub(
                        xnl[:, TILE*i+c0:TILE*i+c0+cw], src[:, :cw],
                        xnh[:, TILE*i+c0:TILE*i+c0+cw]))
                else:
                    R.run('v', lambda v, i=i, c0=c0, cw=cw: v.tensor_copy(
                        xnh[:, TILE*i+c0:TILE*i+c0+cw], xt[:, TILE*i+c0:TILE*i+c0+cw]))
                    R.run('v', lambda v, i=i, c0=c0, cw=cw: v.tensor_sub(
                        xnl[:, TILE*i+c0:TILE*i+c0+cw], xt[:, TILE*i+c0:TILE*i+c0+cw],
                        xnh[:, TILE*i+c0:TILE*i+c0+cw]))

        for ti in range(NT):
            tc0 = TILE*ti
            R.barrier(with_q=True)
            nq = R.dma(xt[:], dslab(xq_fm, 0, 8, tc0, TILE, QP))
            mfirst = (tc0 + 2)//3
            mlast = (tc0 + TILE - 1)//3
            cnt = mlast - mfirst + 1
            nu = R.dma(ut[:, :8*cnt], dslab(upd_scr, 0, 8, mfirst, cnt, MQ))
            off = 3*mfirst - tc0
            for i in range(8):
                R.run('v', lambda v, i=i, off=off, cnt=cnt: v.tensor_add(
                    bass.AP(xt, TILE*i + off, [[8*TILE, 128], [3, cnt]]),
                    bass.AP(xt, TILE*i + off, [[8*TILE, 128], [3, cnt]]),
                    ut[:, cnt*i:cnt*(i+1)]),
                      after=[('q', max(nq, nu))] if i == 0 else ())
            if x1_dbg is not None:
                R.dma(dslab(x1_dbg, 0, 8, tc0, TILE, QP), xt[:], after=[('v', R.mark('v'))])
            for (c0, cw) in TSL:
                stats_done = False
                split_xn(c0, cw, from_norm=False)
                stats3(xt, c0, cw)
                split_xn(c0, cw, from_norm=True)
                if xn_dbg is not None:
                    R.dma(dslab(xn_dbg, 0, 8, tc0+c0, cw, QP),
                          bass.AP(xnh, c0, [[8*TILE, 128], [TILE, 8], [1, cw]]),
                          after=[('v', R.mark('v'))])
                for hf in range(2):
                    tm2 = tm1 = 0
                    for hj12 in range(12):
                        hj = 12*hf + hj12
                        wA = wstr if hj12 % 2 == 0 else wstrB
                        wB = wstr2 if hj12 % 2 == 0 else wstr2B
                        nw = R.dma(wA[:], dslab(wbf["ffn_g"], 0, 8, 128*hj, 128, 3072),
                                   after=[('t', tm2 if hj12 >= 2 else R.mark('t'))])
                        nw2 = R.dma(wB[:], dslab(wbf["ffn_u"], 0, 8, 128*hj, 128, 3072))
                        for i in range(8):
                            R.run('t', lambda t, i=i, c0=c0, cw=cw, wA=wA: t.matmul(
                                psC3[:, :cw], wA[:, 128*i:128*(i+1)],
                                xnh[:, TILE*i+c0:TILE*i+c0+cw], start=(i == 0), stop=False),
                                after=[('q', nw2), ('v', R.mark('v'))] if i == 0 else ())
                            R.run('t', lambda t, i=i, c0=c0, cw=cw, wA=wA: t.matmul(
                                psC3[:, :cw], wA[:, 128*i:128*(i+1)],
                                xnl[:, TILE*i+c0:TILE*i+c0+cw], start=False, stop=(i == 7)))
                        for i in range(8):
                            R.run('t', lambda t, i=i, c0=c0, cw=cw, wB=wB: t.matmul(
                                psD3[:, :cw], wB[:, 128*i:128*(i+1)],
                                xnh[:, TILE*i+c0:TILE*i+c0+cw], start=(i == 0), stop=False))
                            R.run('t', lambda t, i=i, c0=c0, cw=cw, wB=wB: t.matmul(
                                psD3[:, :cw], wB[:, 128*i:128*(i+1)],
                                xnl[:, TILE*i+c0:TILE*i+c0+cw], start=False, stop=(i == 7)))
                        tm2 = tm1
                        tm1 = R.mark('t')
                        sg = R.run('s', lambda s, cw=cw: s.activation(ptm3[:, :cw], psC3[:, :cw],
                                                                      AF.Silu, scale=1.0/127.0),
                                   after=[('t', R.mark('t')), ('v', R.mark('v'))])
                        hv = R.run('v', lambda v, cw=cw: v.scalar_tensor_tensor(
                            ptm3b[:, :cw], psD3[:, :cw], 1.0/127.0, ptm3[:, :cw], ALU.mult, ALU.mult),
                                   after=[('s', sg)])
                        R.run('v', lambda v, hj12=hj12, cw=cw: v.tensor_copy(
                            hbh[:, 512*hj12:512*hj12+cw], ptm3b[:, :cw]))
                        R.run('v', lambda v, hj12=hj12, cw=cw: v.tensor_sub(
                            hbl[:, 512*hj12:512*hj12+cw], ptm3b[:, :cw],
                            hbh[:, 512*hj12:512*hj12+cw]))
                    dm2 = dm1 = 0
                    for oc in range(8):
                        wD = wdstr if oc % 2 == 0 else wdstrB
                        nw = R.dma(wD[:, :12*128],
                                   dslab(wbf["ffn_d"], 1536*hf, 12, 128*oc, 128, 1024),
                                   after=[('t', dm2 if oc >= 2 else R.mark('t'))])
                        for hj12 in range(12):
                            R.run('t', lambda t, hj12=hj12, oc=oc, cw=cw, wD=wD: t.matmul(
                                psC3[:, :cw], wD[:, 128*hj12:128*(hj12+1)],
                                hbh[:, 512*hj12:512*hj12+cw], start=(hj12 == 0), stop=False),
                                after=[('q', nw), ('v', R.mark('v'))] if hj12 == 0 else ())
                            R.run('t', lambda t, hj12=hj12, oc=oc, cw=cw, wD=wD: t.matmul(
                                psC3[:, :cw], wD[:, 128*hj12:128*(hj12+1)],
                                hbl[:, 512*hj12:512*hj12+cw], start=False, stop=(hj12 == 11)))
                        dm2 = dm1
                        dm1 = R.mark('t')
                        if hf == 0:
                            R.run('v', lambda v, oc=oc, cw=cw: v.tensor_copy(
                                dacc[:, 512*oc:512*oc+cw], psC3[:, :cw]), after=[('t', R.mark('t'))])
                        else:
                            R.run('v', lambda v, oc=oc, cw=cw: v.tensor_add(
                                ptm3[:, :cw], dacc[:, 512*oc:512*oc+cw], psC3[:, :cw]),
                                  after=[('t', R.mark('t'))])
                            R.run('v', lambda v, oc=oc, c0=c0, cw=cw: v.scalar_tensor_tensor(
                                xt[:, TILE*oc+c0:TILE*oc+c0+cw], ptm3[:, :cw], 1.0/127.0,
                                xt[:, TILE*oc+c0:TILE*oc+c0+cw], ALU.mult, ALU.add))
            nst = R.dma(dslab(x2_scr, 0, 8, tc0, TILE, QP), xt[:], after=[('v', R.mark('v'))])
            for (c0, cw) in TSL:
                split_xn(c0, cw, from_norm=False)
                stats3(xt, c0, cw)
                si = R.run('s', lambda s, cw=cw: s.activation(strow3[:, 1024:1024+cw],
                                                              strow3[:, 1536:1536+cw],
                                                              AF.Sqrt, bias=1e-5),
                           after=[('v', R.mark('v'))])
                mr3 = R.run('v', lambda v, cw=cw: v.tensor_tensor(strow3[:, 1536:1536+cw],
                                                                  strow3[:, 0:cw],
                                                                  strow3[:, 512:512+cw], ALU.mult),
                            after=[('s', si)])
                tb = R.run('t', lambda t, cw=cw: t.matmul(psC3[:, :cw], cst["ones_row"][:1, :128],
                                                          strow3[:1, 1536:1536+cw],
                                                          start=True, stop=True), after=[('v', mr3)])
                vmr = R.run('v', lambda v, cw=cw: v.tensor_copy(mub3[:, :cw], psC3[:, :cw]),
                            after=[('t', tb)])
                tb2 = R.run('t', lambda t, cw=cw: t.matmul(psC3[:, :cw], cst["ones_row"][:1, :128],
                                                           strow3[:1, 1024:1024+cw],
                                                           start=True, stop=True),
                            after=[('v', vmr)])
                vir = R.run('v', lambda v, cw=cw: v.tensor_copy(ptm3b[:, :cw], psC3[:, :cw]),
                            after=[('t', tb2)])
                am2 = am1 = 0
                ak = 0
                for oc in range(8):
                    for wn, dstb in [("wr_t", None), ("ww_t", wst1), ("wkb_t", kst1),
                                     ("wvb_t", vst1)]:
                        wA = wstr if ak % 2 == 0 else wstrB
                        wB = wstr2 if ak % 2 == 0 else wstr2B
                        nw = R.dma(wA[:], dslab(wbf[wn+"_h"], 0, 8, 128*oc, 128, D),
                                   after=[('t', am2 if ak >= 2 else R.mark('t'))])
                        nw2 = R.dma(wB[:], dslab(wbf[wn+"_l"], 0, 8, 128*oc, 128, D))
                        for i in range(8):
                            R.run('t', lambda t, i=i, c0=c0, cw=cw, wA=wA: t.matmul(
                                psC3[:, :cw], wA[:, 128*i:128*(i+1)],
                                xnh[:, TILE*i+c0:TILE*i+c0+cw], start=(i == 0), stop=False),
                                after=[('q', nw2), ('v', R.mark('v'))] if i == 0 else ())
                            R.run('t', lambda t, i=i, c0=c0, cw=cw, wA=wA: t.matmul(
                                psC3[:, :cw], wA[:, 128*i:128*(i+1)],
                                xnl[:, TILE*i+c0:TILE*i+c0+cw], start=False, stop=False))
                            R.run('t', lambda t, i=i, c0=c0, cw=cw, wB=wB: t.matmul(
                                psC3[:, :cw], wB[:, 128*i:128*(i+1)],
                                xnh[:, TILE*i+c0:TILE*i+c0+cw], start=False, stop=(i == 7)))
                        e1 = R.run('v', lambda v, cw=cw: v.tensor_tensor(
                            ptm3[:, :cw], psC3[:, :cw], rstdb3[:, :cw], ALU.mult),
                                   after=[('t', R.mark('t')), ('s', R.mark('s'))])
                        e2 = R.run('v', lambda v, cw=cw, wn=wn, oc=oc: v.scalar_tensor_tensor(
                            ptm3[:, :cw], mub3[:, :cw], w1col[wn][:, oc:oc+1], ptm3[:, :cw],
                            ALU.mult, ALU.add))
                        e3 = R.run('v', lambda v, cw=cw, wn=wn, oc=oc: v.tensor_scalar_add(
                            ptm3[:, :cw], ptm3[:, :cw], w1col[wn+"_b"][:, oc:oc+1]))
                        if wn == "wr_t":
                            sgr = R.run('s', lambda s, cw=cw: s.activation(
                                ptm3[:, :cw], ptm3[:, :cw], AF.Sigmoid), after=[('v', e3)])
                            R.dma(dslice(r_scr, 128*oc, 128, tc0 + c0, cw, QP), ptm3[:, :cw],
                                  after=[('s', sgr)])
                        else:
                            R.run('v', lambda v, cw=cw, dstb=dstb: v.tensor_copy(
                                dstb[:, :cw], ptm3[:, :cw]), after=[('v', e3), ('q', R.n['q'])])
                        am2 = am1
                        am1 = R.mark('t')
                        ak += 1
                    s1 = R.run('s', lambda s, cw=cw: s.activation(ptm3[:, :cw], wst1[:, :cw],
                                                                  AF.Exp),
                               after=[('v', R.mark('v')), ('q', R.n['q'])])
                    v1 = R.run('v', lambda v, cw=cw: v.tensor_sub(ptm3[:, :cw], kst1[:, :cw],
                                                                  ptm3[:, :cw]), after=[('s', s1)])
                    s2 = R.run('s', lambda s, cw=cw: s.activation(ptm3[:, :cw], ptm3[:, :cw],
                                                                  AF.Exp), after=[('v', v1)])
                    v2 = R.run('v', lambda v, cw=cw: v.tensor_tensor(ptm3b[:, :cw], ptm3[:, :cw],
                                                                     vst1[:, :cw], ALU.mult),
                               after=[('s', s2)])
                    vn = R.run('v', lambda v, oc=oc, cw=cw: v.tensor_tensor_scan(
                        wst1[:, :cw], dcol[:, oc:oc+1].broadcast_to((128, cw)),
                        ptm3b[:, :cw], carry_n[:, oc:oc+1], ALU.mult, ALU.add))
                    vd = R.run('v', lambda v, oc=oc, cw=cw: v.tensor_tensor_scan(
                        kst1[:, :cw], dcol[:, oc:oc+1].broadcast_to((128, cw)),
                        ptm3[:, :cw], carry_d[:, oc:oc+1], ALU.mult, ALU.add))
                    R.run('v', lambda v, oc=oc, cw=cw: v.tensor_copy(
                        carry_n[:, oc:oc+1], wst1[:, cw-1:cw]))
                    R.run('v', lambda v, oc=oc, cw=cw: v.tensor_copy(
                        carry_d[:, oc:oc+1], kst1[:, cw-1:cw]))
                    R.dma(dslice(nd_scr, 128*oc, 128, tc0 + c0, cw, QP), wst1[:, :cw],
                          after=[('v', R.mark('v'))])
                    R.dma(dslice(nd_scr, 1024 + 128*oc, 128, tc0 + c0, cw, QP), kst1[:, :cw])
    R.barrier(with_q=True)
    ncb1 = R.dma(bass.AP(cbuf, 0, [[1, 128], [128, 8]]), carry_n[:], after=[('v', R.mark('v'))], slow=True)
    ncb2 = R.dma(bass.AP(cbuf, 1024, [[1, 128], [128, 8]]), carry_d[:], slow=True)
    R.run('q2', lambda h: h.collective_compute(
        "AllGather", ALU.bypass, replica_groups=[[0, 1, 2, 3], [4, 5, 6, 7]],
        ins=[cbuf.ap().opt()], outs=[cfull.ap().opt()]), after=[('q', R.n['q'])])
    ncc2 = R.n['q2']
    nhl = R.dma(hall[:], bass.AP(cfull, 0, [[1, 128], [2048, 4], [1024, 2], [128, 8]]),
                after=[('q2', ncc2)], slow=True)
    R.run('v', lambda v: v.tensor_scalar_add(dqm1[:], d4096[:], -1.0), after=[('q', nhl)])
    R.run('v', lambda v: v.memset(hcol[:], 0.0))
    for r in range(3):
        R.run('v', lambda v, r=r: v.tensor_scalar_mul(dmr[:], dqm1[:], cmk[:, r:r+1]))
        R.run('v', lambda v: v.tensor_scalar_add(dmr[:], dmr[:], 1.0))
        for t in range(2):
            R.run('v', lambda v, r=r, t=t: v.tensor_tensor(hcol[:, 8*t:8*(t+1)],
                                                           hcol[:, 8*t:8*(t+1)], dmr[:], ALU.mult))
            R.run('v', lambda v, r=r, t=t: v.scalar_tensor_tensor(
                hcol[:, 8*t:8*(t+1)], hall[:, 16*r+8*t:16*r+8*(t+1)], cmk[:, r:r+1],
                hcol[:, 8*t:8*(t+1)], ALU.mult, ALU.add))
    R.run('v', lambda v: v.tensor_copy(hbase[:], hcol[:]))
    R.barrier(with_q=True)

    with ExitStack() as P6:
        p6sb = lambda nme, sh, dt=F32: P6.enter_context(nc.sbuf_tensor(nme, sh, dt))
        x3t = p6sb("x3t", [128, 8*TILE])
        xnh6 = p6sb("xnh6", [128, 8*TILE], BF16)
        xnl6 = p6sb("xnl6", [128, 8*TILE], BF16)
        sqb6 = p6sb("sqb6", [128, 8*512], BF16)
        wstr6 = p6sb("wstr6", [128, 4096], BF16)
        wstr6b = p6sb("wstr6b", [128, 4096], BF16)
        wstr6B = p6sb("wstr6B", [128, 3072], BF16)
        wstr6bB = p6sb("wstr6bB", [128, 1024], BF16)
        ptm6 = p6sb("ptm6", [128, 512])
        ptm6b = p6sb("ptm6b", [128, 512])
        strow6 = p6sb("strow6", [1, 2048])
        rstdb6 = p6sb("rstdb6", [128, 512])
        mub6 = p6sb("mub6", [128, 512])
        psA6 = P6.enter_context(nc.psum_tensor("p6psA", [1, 512], F32))
        psC6 = P6.enter_context(nc.psum_tensor("p6psC", [128, 512], F32))
        psD6 = P6.enter_context(nc.psum_tensor("p6psD", [128, 512], F32))

        def stats6(c0, cw, src):
            for i in range(8):
                R.run('s', lambda s, i=i, c0=c0, cw=cw, src=src: s.activation(
                    sqb6[:, 512*i:512*i+cw], src[:, TILE*i+c0:TILE*i+c0+cw], AF.Square),
                      after=[('v', R.mark('v'))])
            for i in range(8):
                R.run('t', lambda t, i=i, c0=c0, cw=cw: t.matmul(
                    psA6[:, :cw], onesb[:], xnh6[:, TILE*i+c0:TILE*i+c0+cw],
                    start=(i == 0), stop=False),
                      after=[('v', R.mark('v')), ('s', R.mark('s'))])
                R.run('t', lambda t, i=i, c0=c0, cw=cw: t.matmul(
                    psA6[:, :cw], onesb[:], xnl6[:, TILE*i+c0:TILE*i+c0+cw],
                    start=False, stop=(i == 7)))
            nmu = R.run('v', lambda v, cw=cw: v.tensor_scalar_mul(strow6[:, 0:cw], psA6[:, :cw], 1.0/D),
                  after=[('t', R.mark('t'))])
            for i in range(8):
                R.run('t', lambda t, i=i, cw=cw: t.matmul(
                    psA6[:, :cw], onesb[:], sqb6[:, 512*i:512*i+cw],
                    start=(i == 0), stop=(i == 7)), after=[('v', nmu)] if i == 0 else ())
            R.run('v', lambda v, cw=cw: v.tensor_scalar_mul(strow6[:, 1536:1536+cw], psA6[:, :cw], 1.0/D),
                  after=[('t', R.mark('t'))])
            R.run('v', lambda v, cw=cw: v.tensor_tensor(strow6[:, 1024:1024+cw], strow6[:, 0:cw],
                                                        strow6[:, 0:cw], ALU.mult))
            R.run('v', lambda v, cw=cw: v.tensor_sub(strow6[:, 1536:1536+cw],
                                                     strow6[:, 1536:1536+cw],
                                                     strow6[:, 1024:1024+cw]))
            rs = R.run('s', lambda s, cw=cw: s.activation(strow6[:, 512:512+cw],
                                                          strow6[:, 1536:1536+cw],
                                                          AF.Sqrt, bias=1e-5),
                       after=[('v', R.mark('v'))])
            rs = R.run('v', lambda v, cw=cw: v.reciprocal(strow6[:, 512:512+cw],
                                                          strow6[:, 512:512+cw]),
                       after=[('s', rs)])
            si = R.run('s', lambda s, cw=cw: s.activation(strow6[:, 1024:1024+cw],
                                                          strow6[:, 1536:1536+cw],
                                                          AF.Sqrt, bias=1e-5))
            mr6 = R.run('v', lambda v, cw=cw: v.tensor_tensor(strow6[:, 1536:1536+cw],
                                                              strow6[:, 0:cw],
                                                              strow6[:, 512:512+cw], ALU.mult),
                        after=[('s', si)])
            t1 = R.run('t', lambda t, cw=cw: t.matmul(psC6[:, :cw], cst["ones_row"][:1, :128],
                                                      strow6[:1, 512:512+cw], start=True, stop=True),
                       after=[('v', mr6)])
            v1 = R.run('v', lambda v, cw=cw: v.tensor_copy(rstdb6[:, :cw], psC6[:, :cw]),
                       after=[('t', t1)])
            t2 = R.run('t', lambda t, cw=cw: t.matmul(psC6[:, :cw], cst["ones_row"][:1, :128],
                                                      strow6[:1, 1536:1536+cw], start=True, stop=True),
                       after=[('v', v1)])
            v2 = R.run('v', lambda v, cw=cw: v.tensor_copy(mub6[:, :cw], psC6[:, :cw]),
                       after=[('t', t2)])
            t3 = R.run('t', lambda t, cw=cw: t.matmul(psC6[:, :cw], cst["ones_row"][:1, :128],
                                                      strow6[:1, 1024:1024+cw], start=True, stop=True),
                       after=[('v', v2)])
            R.run('v', lambda v, cw=cw: v.tensor_copy(ptm6b[:, :cw], psC6[:, :cw]),
                  after=[('t', t3)])

        def split6(c0, cw, src):
            for i in range(8):
                R.run('v', lambda v, i=i, c0=c0, cw=cw, src=src: v.tensor_copy(
                    xnh6[:, TILE*i+c0:TILE*i+c0+cw], src[:, TILE*i+c0:TILE*i+c0+cw]),
                      after=[('t', R.mark('t'))] if i == 0 else ())
                R.run('v', lambda v, i=i, c0=c0, cw=cw, src=src: v.tensor_sub(
                    xnl6[:, TILE*i+c0:TILE*i+c0+cw], src[:, TILE*i+c0:TILE*i+c0+cw],
                    xnh6[:, TILE*i+c0:TILE*i+c0+cw]))

        def proj_folded6(wn, oc, c0, cw, with_beta=True, db=None):
            if db is None:
                db = {'k': 0, 'm2': 0, 'm1': 0}
            wA = wstr6 if db['k'] % 2 == 0 else wstr6B
            wB = wstr6b if db['k'] % 2 == 0 else wstr6bB
            nw = R.dma(wA[:, :1024], dslab(wbf[wn+"_h"], 0, 8, 128*oc, 128, D),
                       after=[('t', db['m2'] if db['k'] >= 2 else R.mark('t'))])
            nw2 = R.dma(wB[:, :1024], dslab(wbf[wn+"_l"], 0, 8, 128*oc, 128, D))
            for i in range(8):
                R.run('t', lambda t, i=i, c0=c0, cw=cw, wA=wA: t.matmul(
                    psC6[:, :cw], wA[:, 128*i:128*(i+1)],
                    xnh6[:, TILE*i+c0:TILE*i+c0+cw], start=(i == 0), stop=False),
                    after=[('q', nw2), ('v', R.mark('v'))] if i == 0 else ())
                R.run('t', lambda t, i=i, c0=c0, cw=cw, wA=wA: t.matmul(
                    psC6[:, :cw], wA[:, 128*i:128*(i+1)],
                    xnl6[:, TILE*i+c0:TILE*i+c0+cw], start=False, stop=False))
                R.run('t', lambda t, i=i, c0=c0, cw=cw, wB=wB: t.matmul(
                    psC6[:, :cw], wB[:, 128*i:128*(i+1)],
                    xnh6[:, TILE*i+c0:TILE*i+c0+cw], start=False, stop=(i == 7)))
            db['m2'] = db['m1']
            db['m1'] = R.mark('t')
            db['k'] += 1
            e1 = R.run('v', lambda v, cw=cw: v.tensor_tensor(ptm6[:, :cw], psC6[:, :cw],
                                                             rstdb6[:, :cw], ALU.mult),
                       after=[('t', R.mark('t')), ('s', R.mark('s'))])
            R.run('v', lambda v, cw=cw, wn=wn, oc=oc: v.scalar_tensor_tensor(
                ptm6[:, :cw], mub6[:, :cw], w1col[wn][:, oc:oc+1], ptm6[:, :cw],
                ALU.mult, ALU.add))
            if with_beta:
                R.run('v', lambda v, cw=cw, wn=wn, oc=oc: v.tensor_scalar_add(
                    ptm6[:, :cw], ptm6[:, :cw], w1col[wn+"_b"][:, oc:oc+1]))
            return R.mark('v')

        for ti in range(NT):
            tc0 = TILE*ti
            R.barrier(with_q=True)
            with ExitStack() as S1:
                s1sb = lambda nme, sh, dt=F32, ti=ti: S1.enter_context(nc.sbuf_tensor(f"{nme}_{ti}", sh, dt))
                ndn = s1sb("ndn", [128, 8*TILE])
                ndd = s1sb("ndd", [128, 8*TILE])
                rt = s1sb("rt", [128, 8*TILE])
                rwh = s1sb("rwh", [128, 8*TILE], BF16)
                rwl = s1sb("rwl", [128, 8*TILE], BF16)
                n1 = R.dma(ndn[:], dslab(nd_scr, 0, 8, tc0, TILE, QP))
                n2 = R.dma(ndd[:], dslab(nd_scr, 1024, 8, tc0, TILE, QP))
                n3 = R.dma(rt[:], dslab(r_scr, 0, 8, tc0, TILE, QP))
                n4 = R.dma(x3t[:], dslab(x2_scr, 0, 8, tc0, TILE, QP))
                for i in range(8):
                    vn = R.run('v', lambda v, i=i, ndn=ndn: v.scalar_tensor_tensor(
                        ndn[:, TILE*i:TILE*(i+1)], tpow[:, TILE*i:TILE*(i+1)],
                        hbase[:, i:i+1], ndn[:, TILE*i:TILE*(i+1)], ALU.mult, ALU.add),
                               after=[('q', n2)] if i == 0 else ())
                    vd = R.run('v', lambda v, i=i, ndd=ndd: v.scalar_tensor_tensor(
                        ndd[:, TILE*i:TILE*(i+1)], tpow[:, TILE*i:TILE*(i+1)],
                        hbase[:, 8+i:9+i], ndd[:, TILE*i:TILE*(i+1)], ALU.mult, ALU.add))
                    sr = R.run('v', lambda v, i=i, ndd=ndd: v.tensor_scalar_add(
                        ndd[:, TILE*i:TILE*(i+1)], ndd[:, TILE*i:TILE*(i+1)], 1e-8),
                               after=[('v', vd)])
                    sr = R.run('v', lambda v, i=i, ndd=ndd: v.reciprocal(
                        ndd[:, TILE*i:TILE*(i+1)], ndd[:, TILE*i:TILE*(i+1)]))
                    v1 = R.run('v', lambda v, i=i, ndn=ndn, ndd=ndd: v.tensor_tensor(
                        ndn[:, TILE*i:TILE*(i+1)], ndn[:, TILE*i:TILE*(i+1)],
                        ndd[:, TILE*i:TILE*(i+1)], ALU.mult), after=[('q', n3)])
                    v2 = R.run('v', lambda v, i=i, ndn=ndn, rt=rt: v.tensor_tensor(
                        ndn[:, TILE*i:TILE*(i+1)], ndn[:, TILE*i:TILE*(i+1)],
                        rt[:, TILE*i:TILE*(i+1)], ALU.mult))
                    R.run('v', lambda v, i=i, rwh=rwh, ndn=ndn: v.tensor_copy(
                        rwh[:, TILE*i:TILE*(i+1)], ndn[:, TILE*i:TILE*(i+1)]))
                    R.run('v', lambda v, i=i, rwl=rwl, ndn=ndn, rwh=rwh: v.tensor_sub(
                        rwl[:, TILE*i:TILE*(i+1)], ndn[:, TILE*i:TILE*(i+1)],
                        rwh[:, TILE*i:TILE*(i+1)]))
                R.run('v', lambda v: v.tensor_tensor(hbase[:, :8], hbase[:, :8], d512t[:], ALU.mult))
                R.run('v', lambda v: v.tensor_tensor(hbase[:, 8:], hbase[:, 8:], d512t[:], ALU.mult))
                for (c0, cw) in TSL:
                    for oc in range(8):
                        nw = R.dma(wstr6[:, :1024], dslab(wbf["wo_t_h"], 0, 8, 128*oc, 128, D),
                                   after=[('t', R.mark('t'))])
                        nw2 = R.dma(wstr6b[:, :1024], dslab(wbf["wo_t_l"], 0, 8, 128*oc, 128, D))
                        for i in range(8):
                            R.run('t', lambda t, i=i, c0=c0, cw=cw, rwh=rwh: t.matmul(
                                psC6[:, :cw], wstr6[:, 128*i:128*(i+1)],
                                rwh[:, TILE*i+c0:TILE*i+c0+cw], start=(i == 0), stop=False),
                                after=[('q', nw2), ('v', R.mark('v'))] if i == 0 else ())
                            R.run('t', lambda t, i=i, c0=c0, cw=cw, rwl=rwl: t.matmul(
                                psC6[:, :cw], wstr6[:, 128*i:128*(i+1)],
                                rwl[:, TILE*i+c0:TILE*i+c0+cw], start=False, stop=False))
                            R.run('t', lambda t, i=i, c0=c0, cw=cw, rwh=rwh: t.matmul(
                                psC6[:, :cw], wstr6b[:, 128*i:128*(i+1)],
                                rwh[:, TILE*i+c0:TILE*i+c0+cw], start=False, stop=(i == 7)))
                        R.run('v', lambda v, oc=oc, c0=c0, cw=cw: v.tensor_add(
                            x3t[:, TILE*oc+c0:TILE*oc+c0+cw], x3t[:, TILE*oc+c0:TILE*oc+c0+cw],
                            psC6[:, :cw]), after=[('t', R.mark('t')), ('q', n4)])
            S2 = ExitStack()
            kqh = S2.enter_context(nc.sbuf_tensor(f"kqh_{ti}", [128, 32*512], BF16))
            kql = S2.enter_context(nc.sbuf_tensor(f"kql_{ti}", [128, 32*512], BF16))
            rrst = S2.enter_context(nc.sbuf_tensor(f"rrst_{ti}", [128, 8*TILE], F32))
            for (c0, cw) in TSL:
                split6(c0, cw, x3t)
                stats6(c0, cw, x3t)
                dbr = {'k': 0, 'm2': 0, 'm1': 0}
                for oc in range(8):
                    pv = proj_folded6("wrec_t", oc, c0, cw, db=dbr)
                    R.run('s', lambda s, oc=oc, c0=c0, cw=cw, rrst=rrst: s.activation(
                        rrst[:, TILE*oc+c0:TILE*oc+c0+cw], ptm6[:, :cw], AF.Sigmoid),
                          after=[('v', pv)])
                km2 = km1 = 0
                for hj in range(32):
                    wA = wstr6 if hj % 2 == 0 else wstr6B
                    wB = wstr6b if hj % 2 == 0 else wstr6bB
                    nw = R.dma(wA[:, :1024], dslab(wbf["wkey_t_h"], 0, 8, 128*hj, 128, 4096),
                               after=[('t', km2 if hj >= 2 else R.mark('t'))])
                    nw2 = R.dma(wB[:, :1024], dslab(wbf["wkey_t_l"], 0, 8, 128*hj, 128, 4096))
                    for i in range(8):
                        R.run('t', lambda t, i=i, c0=c0, cw=cw, wA=wA: t.matmul(
                            psC6[:, :cw], wA[:, 128*i:128*(i+1)],
                            xnh6[:, TILE*i+c0:TILE*i+c0+cw], start=(i == 0), stop=False),
                            after=[('q', nw2), ('v', R.mark('v'))] if i == 0 else ())
                        R.run('t', lambda t, i=i, c0=c0, cw=cw, wA=wA: t.matmul(
                            psC6[:, :cw], wA[:, 128*i:128*(i+1)],
                            xnl6[:, TILE*i+c0:TILE*i+c0+cw], start=False, stop=False))
                        R.run('t', lambda t, i=i, c0=c0, cw=cw, wB=wB: t.matmul(
                            psC6[:, :cw], wB[:, 128*i:128*(i+1)],
                            xnh6[:, TILE*i+c0:TILE*i+c0+cw], start=False, stop=(i == 7)))
                    e1 = R.run('v', lambda v, cw=cw: v.tensor_tensor(ptm6[:, :cw], psC6[:, :cw],
                                                                     rstdb6[:, :cw], ALU.mult),
                               after=[('t', R.mark('t')), ('s', R.mark('s'))])
                    R.run('v', lambda v, cw=cw, hj=hj: v.scalar_tensor_tensor(
                        ptm6[:, :cw], mub6[:, :cw], w1col["wkey_t"][:, hj:hj+1], ptm6[:, :cw],
                        ALU.mult, ALU.add))
                    e3 = R.run('v', lambda v, cw=cw, hj=hj: v.tensor_scalar_add(
                        ptm6[:, :cw], ptm6[:, :cw], w1col["wkey_t_b"][:, hj:hj+1]))
                    sr = R.run('s', lambda s, cw=cw: s.activation(ptm6[:, :cw], ptm6[:, :cw],
                                                                  AF.Relu), after=[('v', e3)])
                    sq = R.run('s', lambda s, cw=cw: s.activation(ptm6[:, :cw], ptm6[:, :cw],
                                                                  AF.Square))
                    R.run('v', lambda v, hj=hj, cw=cw, kqh=kqh: v.tensor_copy(
                        kqh[:, 512*hj:512*hj+cw], ptm6[:, :cw]), after=[('s', sq)])
                    R.run('v', lambda v, hj=hj, cw=cw, kqh=kqh, kql=kql: v.tensor_sub(
                        kql[:, 512*hj:512*hj+cw], ptm6[:, :cw], kqh[:, 512*hj:512*hj+cw]))
                    km2 = km1
                    km1 = R.mark('t')
                for oc in range(8):
                    wA = wstr6
                    wB = wstr6b
                    nw = R.dma(wA[:], dslab(wbf["wval_t_h"], 0, 32, 128*oc, 128, D),
                               after=[('t', R.mark('t'))])
                    nw2 = R.dma(wB[:], dslab(wbf["wval_t_l"], 0, 32, 128*oc, 128, D))
                    for hj in range(32):
                        R.run('t', lambda t, hj=hj, cw=cw, kqh=kqh, wA=wA: t.matmul(
                            psC6[:, :cw], wA[:, 128*hj:128*(hj+1)], kqh[:, 512*hj:512*hj+cw],
                            start=(hj == 0), stop=False),
                            after=[('q', nw2), ('v', R.mark('v')),
                                   ('s', R.mark('s'))] if hj == 0 else ())
                        R.run('t', lambda t, hj=hj, cw=cw, kql=kql, wA=wA: t.matmul(
                            psC6[:, :cw], wA[:, 128*hj:128*(hj+1)], kql[:, 512*hj:512*hj+cw],
                            start=False, stop=False))
                        R.run('t', lambda t, hj=hj, cw=cw, kqh=kqh, wB=wB: t.matmul(
                            psC6[:, :cw], wB[:, 128*hj:128*(hj+1)], kqh[:, 512*hj:512*hj+cw],
                            start=False, stop=(hj == 31)))
                    v1 = R.run('v', lambda v, oc=oc, c0=c0, cw=cw, rrst=rrst: v.tensor_tensor(
                        ptm6[:, :cw], rrst[:, TILE*oc+c0:TILE*oc+c0+cw], psC6[:, :cw], ALU.mult),
                               after=[('t', R.mark('t')), ('s', R.mark('s'))])
                    R.run('v', lambda v, oc=oc, c0=c0, cw=cw: v.tensor_add(
                        x3t[:, TILE*oc+c0:TILE*oc+c0+cw], x3t[:, TILE*oc+c0:TILE*oc+c0+cw],
                        ptm6[:, :cw]))
            for (c0, cw) in TSL:
                split6(c0, cw, x3t)
                stats6(c0, cw, x3t)
                for i in range(8):
                    R.run('v', lambda v, i=i, c0=c0, cw=cw: v.tensor_tensor(
                        ptm6[:, :cw], x3t[:, TILE*i+c0:TILE*i+c0+cw], rstdb6[:, :cw], ALU.mult),
                          after=[('t', R.mark('t')), ('s', R.mark('s'))] if i == 0 else ())
                    R.run('v', lambda v, i=i, c0=c0, cw=cw: v.tensor_sub(
                        ptm6[:, :cw], ptm6[:, :cw], mub6[:, :cw]))
                    R.run('v', lambda v, i=i, c0=c0, cw=cw: v.tensor_copy(
                        xnh6[:, TILE*i+c0:TILE*i+c0+cw], ptm6[:, :cw]))
                    R.run('v', lambda v, i=i, c0=c0, cw=cw: v.tensor_sub(
                        xnl6[:, TILE*i+c0:TILE*i+c0+cw], ptm6[:, :cw],
                        xnh6[:, TILE*i+c0:TILE*i+c0+cw]))
                fm2 = fm1 = 0
                for hj in range(24):
                    wA = wstr6 if hj % 2 == 0 else wstr6B
                    wB = wstr6b if hj % 2 == 0 else wstr6bB
                    nw = R.dma(wA[:, :1024], dslab(wbf["ffn_g"], 0, 8, 128*hj, 128, 3072),
                               after=[('t', fm2 if hj >= 2 else R.mark('t'))])
                    nw2 = R.dma(wB[:, :1024], dslab(wbf["ffn_u"], 0, 8, 128*hj, 128, 3072))
                    for i in range(8):
                        R.run('t', lambda t, i=i, c0=c0, cw=cw, wA=wA: t.matmul(
                            psC6[:, :cw], wA[:, 128*i:128*(i+1)],
                            xnh6[:, TILE*i+c0:TILE*i+c0+cw], start=(i == 0), stop=False),
                            after=[('q', nw2), ('v', R.mark('v')),
                                   ('s', R.mark('s'))] if i == 0 else ())
                        R.run('t', lambda t, i=i, c0=c0, cw=cw, wA=wA: t.matmul(
                            psC6[:, :cw], wA[:, 128*i:128*(i+1)],
                            xnl6[:, TILE*i+c0:TILE*i+c0+cw], start=False, stop=(i == 7)))
                    for i in range(8):
                        R.run('t', lambda t, i=i, c0=c0, cw=cw, wB=wB: t.matmul(
                            psD6[:, :cw], wB[:, 128*i:128*(i+1)],
                            xnh6[:, TILE*i+c0:TILE*i+c0+cw], start=(i == 0), stop=False))
                        R.run('t', lambda t, i=i, c0=c0, cw=cw, wB=wB: t.matmul(
                            psD6[:, :cw], wB[:, 128*i:128*(i+1)],
                            xnl6[:, TILE*i+c0:TILE*i+c0+cw], start=False, stop=(i == 7)))
                    sg = R.run('s', lambda s, cw=cw: s.activation(ptm6[:, :cw], psC6[:, :cw],
                                                                  AF.Silu, scale=1.0/127.0),
                               after=[('t', R.mark('t')), ('v', R.mark('v'))])
                    hv = R.run('v', lambda v, cw=cw: v.scalar_tensor_tensor(
                        ptm6b[:, :cw], psD6[:, :cw], 1.0/127.0, ptm6[:, :cw], ALU.mult, ALU.mult),
                               after=[('s', sg)])
                    R.run('v', lambda v, hj=hj, cw=cw, kqh=kqh: v.tensor_copy(
                        kqh[:, 512*hj:512*hj+cw], ptm6b[:, :cw]))
                    R.run('v', lambda v, hj=hj, cw=cw, kqh=kqh, kql=kql: v.tensor_sub(
                        kql[:, 512*hj:512*hj+cw], ptm6b[:, :cw], kqh[:, 512*hj:512*hj+cw]))
                    fm2 = fm1
                    fm1 = R.mark('t')
                gm2 = gm1 = 0
                for oc in range(8):
                    wA = wstr6 if oc % 2 == 0 else wstr6B
                    nw = R.dma(wA[:, :3072], dslab(wbf["ffn_d"], 0, 24, 128*oc, 128, 1024),
                               after=[('t', gm2 if oc >= 2 else R.mark('t'))])
                    for hj in range(24):
                        R.run('t', lambda t, hj=hj, cw=cw, kqh=kqh, wA=wA: t.matmul(
                            psC6[:, :cw], wA[:, 128*hj:128*(hj+1)], kqh[:, 512*hj:512*hj+cw],
                            start=(hj == 0), stop=False),
                            after=[('q', nw), ('v', R.mark('v')),
                                   ('s', R.mark('s'))] if hj == 0 else ())
                        R.run('t', lambda t, hj=hj, cw=cw, kql=kql, wA=wA: t.matmul(
                            psC6[:, :cw], wA[:, 128*hj:128*(hj+1)], kql[:, 512*hj:512*hj+cw],
                            start=False, stop=(hj == 23)))
                    ev = R.run('v', lambda v, oc=oc, c0=c0, cw=cw: v.scalar_tensor_tensor(
                        ptm6[:, :cw], psC6[:, :cw], 1.0/127.0,
                        x3t[:, TILE*oc+c0:TILE*oc+c0+cw], ALU.mult, ALU.add),
                          after=[('t', R.mark('t')), ('q', R.n['q'])])
                    R.dma(dslice(out_fm, 128*oc, 128, tc0 + c0, cw, QP), ptm6[:, :cw],
                          after=[('v', ev)])
                    gm2 = gm1
                    gm1 = R.mark('t')
            S2.close()
    with ExitStack() as FIN:
        semT = FIN.enter_context(nc.semaphore("semT"))
        semV = FIN.enter_context(nc.semaphore("semV"))
        semS = FIN.enter_context(nc.semaphore("semS"))
        semQ = FIN.enter_context(nc.semaphore("semQ"))
        semQ2 = FIN.enter_context(nc.semaphore("semQ2"))
        block = FIN.enter_context(nc.Block())
        regref = {'t': gidx_t}
        R.replay(nc, block, {'t': semT, 'v': semV, 's': semS, 'q': semQ, 'q2': semQ2},
                 regref=regref)
    top.close()
    return nc, R, ap_in



def host_maps(inputs, N=16384):
    D = 1024
    MQ = (((N - 1)//3 + 1 + 3)//4)
    MQ = MQ + (-MQ) % 2
    QP = 3*MQ
    xx = np.asarray(inputs['L0_attn_a_x'])
    cstnp = scan_consts_np()
    xT = [np.zeros((D, N + 16), np.float32) for _ in range(2)]
    for b in range(2):
        xT[b][:, :N] = np.ascontiguousarray(xx[b].T)
    T = lambda a: np.ascontiguousarray(np.asarray(a).T)
    CC = np.ascontiguousarray
    common = dict(
        wf1=None,
        ffn_g=T(inputs['L0_ffn_w_gate']), ffn_u=T(inputs['L0_ffn_w_up']),
        ffn_d=T(inputs['L0_ffn_w_down']),
        wr_t=T(inputs['L0_attn_b_Wr']), ww_t=T(inputs['L0_attn_b_Ww']),
        wkb_t=T(inputs['L0_attn_b_Wk']), wvb_t=T(inputs['L0_attn_b_Wv']),
        wo_t=T(inputs['L0_attn_b_Wo']), wkey_t=T(inputs['L0_attn_b_Wkey']),
        wval_t=T(inputs['L0_attn_b_Wval']), wrec_t=T(inputs['L0_attn_b_Wrec']),
        ln_w=CC(np.asarray(inputs['L0_attn_a_ln_w']).reshape(D, 1)),
        ln1g=CC(np.asarray(inputs['L0_attn_b_ln1_g']).reshape(D, 1)),
        ln1b=CC(np.asarray(inputs['L0_attn_b_ln1_b']).reshape(D, 1)),
        ln2g=CC(np.asarray(inputs['L0_attn_b_ln2_g']).reshape(D, 1)),
        ln2b=CC(np.asarray(inputs['L0_attn_b_ln2_b']).reshape(D, 1)),
        tdec=CC(np.asarray(inputs['L0_attn_b_time_decay']).reshape(D, 1)),
        wo1=CC(np.asarray(inputs['L0_attn_a_Wo1'])),
        wo2=CC(np.asarray(inputs['L0_attn_a_Wo2'])),
        wout=CC(np.asarray(inputs['L0_attn_a_Wout'])),
        wf2=None,
    )
    del common['wf1'], common['wf2']
    for nme, arr in cstnp.items():
        common["cst_"+nme] = arr
    MQh = (((N - 1)//3 + 1 + 3)//4); MQh = MQh + (-MQh) % 2
    ac = np.zeros((128, 8), np.float32)
    for ci_, cv_ in enumerate([1e-5, 1.0, -1.0, float(3*MQh), float(3*MQh//6), 1.0/127.0, 1e-8, 2.0]):
        ac[:, ci_] = cv_
    common["actc"] = ac
    maps = []
    for c in range(8):
        b, g = c//4, c % 4
        qs = QP*g
        qe = min(qs + QP, N)
        m = dict(common)
        m['x_fm'] = xT[b]
        xqf = np.zeros((D, QP), np.float32)
        xqf[:, :qe-qs] = xT[b][:, qs:qe]
        m['xq_fm'] = xqf
        x3 = np.zeros((D, MQ), np.float32)
        sl = xT[b][:, qs:qe:3]
        x3[:, :sl.shape[1]] = sl
        m['xq3'] = x3
        m['wk_s'] = CC(np.asarray(inputs['L0_attn_a_Wk'])[:, 256*g:256*(g+1)])
        m['wv_s'] = CC(np.asarray(inputs['L0_attn_a_Wv'])[:, 256*g:256*(g+1)])
        m['wq_s'] = CC(np.asarray(inputs['L0_attn_a_Wq'])[:, 256*g:256*(g+1)])
        m['wfg'] = CC(np.concatenate([np.asarray(inputs['L0_attn_a_Wf1']),
                                      np.asarray(inputs['L0_attn_a_Wg'])[:, 4*g:4*(g+1)]], axis=1))
        m['wf2_s'] = CC(np.asarray(inputs['L0_attn_a_Wf2'])[:, 256*g:256*(g+1)])
        cm = np.zeros((128, 3), np.float32)
        for r in range(3):
            cm[:, r] = 1.0 if r < g else 0.0
        m['cmask'] = cm
        gvec = np.zeros((1, 8), np.int32)
        SLh = []
        c0h = 0
        while c0h < MQ:
            SLh.append(c0h)
            c0h += 512
        for kk, c0h in enumerate(SLh):
            gvec[0, kk] = c0h + MQ*g
        m['gidx'] = gvec
        maps.append(m)
    return maps, MQ, QP


def kernel(**inputs):
    import os
    os.environ["TRN_ALLOW_ICI_ALL_TO_ALL"] = "1"
    from concourse.bass_utils import run_bass_kernel_spmd
    N = int(np.asarray(inputs['L0_attn_a_x']).shape[1])
    maps, MQ, QP = host_maps(inputs, N)
    nc, R, _ = build_core_program(N)
    res = run_bass_kernel_spmd(nc, maps, core_ids=list(range(8)))
    D = 1024
    out = np.zeros((2, N, D), np.float32)
    for c in range(8):
        b, g = c//4, c % 4
        qs = QP*g
        qe = min(qs + QP, N)
        out[b, qs:qe, :] = res.results[c]['out_fm'][:, :qe-qs].T
    return out
